# version 02
# baseline (speedup 1.0000x reference)
import math
import numpy as np
import ml_dtypes
import concourse.bass as bass
import concourse.mybir as mybir
from concourse.bass_utils import run_bass_kernel_spmd
from contextlib import ExitStack

F32 = mybir.dt.float32
BF16 = mybir.dt.bfloat16
OP = mybir.AluOpType
AF = mybir.ActivationFunctionType
AX = mybir.AxisListType.X

D = 2048
KT = 16
NH = 16
HD = 128
FFN = 5632
FFM = 8192
EPS = 1e-5
NEG = -1.0e30
BF = ml_dtypes.bfloat16


def alibi_slopes(n=16):
    start = 2 ** (-(2 ** -(math.log2(n) - 3)))
    return np.array([start * start ** i for i in range(n)], dtype=np.float64)


ENGS = ("pe", "dve", "act", "sync", "pool")
ESEM = {"pe": "PE", "dve": "DV", "act": "AC", "sync": "SQ", "pool": "GQ"}


class Prog:
    def __init__(self, nc):
        self.nc = nc
        self.streams = {e: [] for e in ENGS}
        self.es = ExitStack()
        self.sem = {}
        self.cnt = {}
        for name in ("SQ", "GQ", "PE", "DV", "AC", "CC"):
            self.sem[name] = self.es.enter_context(nc.semaphore(name))
            self.cnt[name] = 0
        self.nops = 0
        self.waited = {e: {} for e in ENGS}

    def raw(self, eng, fn):
        self.streams[eng].append(fn)
        self.nops += 1

    def op(self, eng, fn, inc=True, inc_n=None):
        if not inc:
            self.raw(eng, fn)
            return (ESEM[eng], self.cnt[ESEM[eng]])
        s = ESEM[eng]
        n = inc_n if inc_n is not None else (16 if eng in ("sync", "pool") else 1)
        self.cnt[s] += n
        h = self.sem[s]
        self.raw(eng, lambda e, fn=fn, h=h, n=n: fn(e).then_inc(h, n))
        return (s, self.cnt[s])

    def bop(self, eng, fn, reads=(), writes=(), inc=True):
        deps = []
        for b in reads:
            deps += b.need_for_read()
        for b in writes:
            deps += b.need_for_write()
        self.waits(eng, deps)
        tag = self.op(eng, fn, inc=inc)
        for b in reads:
            b.read(tag)
        for b in writes:
            b.wrote(tag)
        return tag

    def wait(self, eng, tag):
        if tag is None:
            return
        s, t = tag
        if t <= 0:
            return
        assert t <= self.cnt[s], f"future wait {s}:{t} > {self.cnt[s]}"
        if self.waited[eng].get(s, 0) >= t:
            return
        self.waited[eng][s] = t
        h = self.sem[s]
        self.raw(eng, lambda e, h=h, t=t: e.wait_ge(h, t))

    def waits(self, eng, tags):
        best = {}
        for tg in tags:
            if tg is None:
                continue
            s, t = tg
            if t <= 0:
                continue
            best[s] = max(best.get(s, 0), t)
        for s, t in best.items():
            self.wait(eng, (s, t))

    def tag(self, eng):
        return (ESEM[eng], self.cnt[ESEM[eng]])

    def dsem(self, key):
        if key not in self.sem:
            self.sem[key] = self.es.enter_context(self.nc.semaphore(key))
            self.cnt[key] = 0
        return key

    def dma(self, eng, dst, src, key, deps=()):
        self.waits(eng, deps)
        k = self.dsem(key)
        self.cnt[k] += 16
        h = self.sem[k]
        self.raw(eng, lambda e, d=dst, s=src, h=h: e.dma_start(d, s).then_inc(h, 16))
        return (k, self.cnt[k])

    def sb(self, name, shape, dtype=F32):
        return self.es.enter_context(self.nc.sbuf_tensor(name, shape, dtype))

    def barrier(self):
        tags = [(s, self.cnt[s]) for s in self.cnt]
        for e in ENGS:
            self.waits(e, tags)

    def finish(self):
        nc = self.nc
        with nc.Block() as block:
            @block.tensor
            def _(t):
                for fn in self.streams["pe"]:
                    fn(t)

            @block.vector
            def _(v):
                for fn in self.streams["dve"]:
                    fn(v)

            @block.scalar
            def _(s):
                for fn in self.streams["act"]:
                    fn(s)

            @block.sync
            def _(sy):
                for fn in self.streams["sync"]:
                    fn(sy)

            @block.gpsimd
            def _(g):
                for fn in self.streams["pool"]:
                    fn(g)


class Buf:

    def __init__(self, ap):
        self.ap = ap
        self.writer = None
        self.readers = []

    def need_for_write(self):
        return ([self.writer] if self.writer else []) + self.readers

    def need_for_read(self):
        return [self.writer] if self.writer else []

    def wrote(self, tag):
        self.writer = tag
        self.readers = []

    def read(self, tag):
        for i, (s, t) in enumerate(self.readers):
            if s == tag[0]:
                self.readers[i] = (s, max(t, tag[1]))
                return
        self.readers.append(tag)


def build(T=2048, n_cores=8):
    groups = [[2 * i, 2 * i + 1] for i in range(n_cores // 2)]
    TC = T // 512
    QC = T // 128
    SK = 2 * T // 512
    PK = T // 512
    NB = T // 128
    ST = 2 * T // 128

    nc = bass.Bass("TRN2", target_bir_lowering=False)
    p = Prog(nc)

    def di(name, shape, dtype=F32):
        return nc.dram_tensor(name, list(shape), dtype, kind="ExternalInput").ap()

    x_own = di("x_own", (KT, 128, T))
    x_prev = di("x_prev", (KT, 128, T))
    WA = {n: di(n, (KT, 128, D)) for n in ("wqa", "wka", "wva", "wpa")}
    WB = {n: di(n, (KT, 128, D)) for n in ("wqb", "wkb", "wvb", "wob", "wig", "wog")}
    wgg = di("wgg", (KT, 128, NH))
    wgate = di("wgate", (FFM // 128, 128, D))
    wval = di("wval", (FFM // 128, 128, D))
    wout = di("wout", (KT, 128, FFM))
    fw1 = di("fw1", (FFN // 128, 128, D))
    fw2 = di("fw2", (FFN // 128, 128, D))
    fw3 = di("fw3", (KT, 128, FFN))
    smalls = {}
    for n, w in [("qrw", 1), ("krw", 1), ("slw", 1), ("vnw", 1), ("vnb", 1),
                 ("mnw", 1), ("mnb", 1), ("cw", KT * 4), ("cb", KT),
                 ("bigb", KT), ("bogb", KT), ("bgb", NH), ("gnw", KT),
                 ("gnb", KT), ("l1w", KT), ("l1b", KT), ("l2w", KT),
                 ("l2b", KT)]:
        smalls[n] = di(n, (128, w))
    CL = [4 * (PK + g + 1) * 512 for g in range(QC // 4)]
    CO = [sum(CL[:g]) for g in range(QC // 4)]
    cstrd = di("cstrd", (128, sum(CL)))
    onesbf = di("onesbf", (128, 128), BF16)
    onesf = di("onesf", (1, 128))
    onesc = di("onesc", (128, 1))
    onehotf = di("onehotf", (16, NH * 128))
    identf = di("identf", (128, 128))
    trineg = di("trineg", (128, 128))
    cvb = di("cvb", (128, 8))
    ident = di("ident", (128, 128), BF16)
    maskprev = di("maskprev", (128, 64))

    y = nc.dram_tensor("y", [KT, 128, T], F32, kind="ExternalOutput").ap()

    def dsc(name, shape, dtype=F32):
        ap = nc.dram_tensor(name, list(shape), dtype).ap()
        return ap, Buf(ap)

    r1, r1b = dsc("r1", (KT, 128, T))
    r2, r2b = dsc("r2", (KT, 128, T))
    r3, r3b = dsc("r3", (KT, 128, T))
    xbd, xbdb = dsc("xbd", (KT, 128, T))
    hmid, hmidb = dsc("hmid", (KT, 128, T))
    accd, accdb = dsc("accd", (KT, 128, T))
    xbfd, xbfdb = dsc("xbfd", (KT, 128, 2 * T), BF16)
    hbd, hbdb = dsc("hbd", (KT, 128, T + 3), BF16)
    xcd, xcdb = dsc("xcd", (KT, 128, T), BF16)
    qad, qadb = dsc("qad", (NH, 128, T), BF16)
    kad, kadb = dsc("kad", (NH, 128, 2 * T), BF16)
    vtd, vtdb = dsc("vtd", (NH, 128, ST * 128), BF16)
    yad, yadb = dsc("yad", (KT, 128, T), BF16)
    qbd, qbdb = dsc("qbd", (NH, 128, T), BF16)
    umd, umdb = dsc("umd", (NH, 128, T), BF16)
    memd, memdb = dsc("memd", (NH, 128, T))
    wq1d, wq1db = dsc("wq1d", (FFN // 128, 128, D), BF16)
    wq2d, wq2db = dsc("wq2d", (FFN // 128, 128, D), BF16)
    wq3d, wq3db = dsc("wq3d", (KT, 128, FFN), BF16)
    csd, csdb = dsc("csd", (1, 128 * (2 * (FFN // 128) + KT)), BF16)
    cc1i, cc1ib = dsc("cc1i", (128, 64))
    cc1o, cc1ob = dsc("cc1o", (2 * 128, 64))
    cc2i, cc2ib = dsc("cc2i", (128, 16))
    cc2o, cc2ob = dsc("cc2o", (2 * 128, 16))

    csb = {}
    creq = [
            ("c_ones", onesbf, [128, 128], BF16),
            ("c_onesf", onesf, [1, 128], F32),
            ("c_onesc", onesc, [128, 1], F32),
            ("c_ohf", onehotf, [16, NH * 128], F32),
            ("c_idf", identf, [128, 128], F32),
            ("c_trin", trineg, [128, 128], F32),
            ("c_cvb", cvb, [128, 8], F32),
            ("c_id", ident, [128, 128], BF16),
            ("c_mask", maskprev, [128, 64], F32)]
    for n, src in smalls.items():
        creq.append(("c_" + n, src, list(src.shape), F32))
    for name, src, shape, dt in creq:
        t = p.sb(name, shape, dt)
        ctag = p.dma("sync", t[:, :] if len(shape) == 2 else t[:], src, "cst")
        csb[name] = t
    const_tag = ctag
    c_ones = csb["c_ones"]
    c_id = csb["c_id"]
    c_qrws = p.sb("c_qrws", [128, 1])
    p.wait("dve", const_tag)
    qrws_tag = p.op("dve", lambda e: e.tensor_scalar_mul(
        c_qrws[:, :], csb["c_qrw"][:, :], 2 ** -3.5), inc=True)
    c_qrws_b = Buf(c_qrws)
    c_qrws_b.wrote(qrws_tag)
    c_l2wbf = p.sb("c_l2wbf", [128, KT], BF16)
    c_l2bbf = p.sb("c_l2bbf", [128, KT], BF16)
    p.wait("dve", const_tag)
    t_ = p.op("dve", lambda e: e.tensor_copy(c_l2wbf[:, :], csb["c_l2w"][:, :]))
    t_ = p.op("dve", lambda e: e.tensor_copy(c_l2bbf[:, :], csb["c_l2b"][:, :]))
    c_l2wbf_b = Buf(c_l2wbf)
    c_l2wbf_b.wrote(t_)
    c_l2bbf_b = Buf(c_l2bbf)
    c_l2bbf_b.wrote(t_)

    fb = [Buf(p.es.enter_context(nc.psum_tensor(f"pb{i}", [128, 512], F32)).ap())
          for i in range(6)]
    xb = [Buf(p.es.enter_context(nc.psum_tensor(f"px{i}", [128, 1024], BF16)).ap())
          for i in range(2)]

    NWS = 12
    xt, xtb = [None] * KT, [None] * KT
    hbt, hbtb = [None] * KT, [None] * KT
    wsl, wslb = [None] * NWS, [None] * NWS
    stg, stgb = [None] * 4, [None] * 4
    stbf, stbfb = [None] * 4, [None] * 4
    rws = [None]
    rwsb = Buf(None)
    lnr, lnm = [None], [None]
    lnrb, lnmb = Buf(None), Buf(None)
    _gen_ctr = [0]

    def alloc_generic(es, nws=NWS, n_xt=KT, with_hbt=True, rows_len=None):
        g = _gen_ctr[0]
        _gen_ctr[0] += 1
        sbx = lambda n, s, d=F32: es.enter_context(
            nc.sbuf_tensor(f"{n}_g{g}", s, d))
        for k in range(KT):
            if k < n_xt:
                xt[k] = sbx(f"xt{k}", [128, 512])
                xtb[k] = Buf(xt[k])
            else:
                xt[k], xtb[k] = None, None
            if with_hbt:
                hbt[k] = sbx(f"hbt{k}", [128, 512], BF16)
                hbtb[k] = Buf(hbt[k])
            else:
                hbt[k], hbtb[k] = None, None
        for i in range(nws):
            wsl[i] = sbx(f"wsl{i}", [128, D], BF16)
            wslb[i] = Buf(wsl[i])
        for i in range(nws, NWS):
            wsl[i] = None
            wslb[i] = None
        for i in range(4):
            stg[i] = sbx(f"stg{i}", [128, 512])
            stgb[i] = Buf(stg[i])
            stbf[i] = sbx(f"stbf{i}", [128, 512], BF16)
            stbfb[i] = Buf(stbf[i])
        rws[0] = sbx("rws", [1, 8 * 512])
        rwsb.__init__(None)
        rl = rows_len if rows_len is not None else 2 * T
        lnr[0] = sbx("lnr", [1, rl])
        lnm[0] = sbx("lnm", [1, rl], BF16)
        lnrb.__init__(None)
        lnmb.__init__(None)

    colrow = p.sb("colrow", [1, 128 * 12], BF16)
    colrowb = Buf(colrow)
    NC2 = 2 * (FFN // 128) + KT

    slopes = alibi_slopes(NH)

    def evac(eng, bank, fn, dstbuf=None):
        p.waits(eng, bank.need_for_read())
        tag = p.op(eng, fn, inc=True)
        bank.read(tag)
        if dstbuf is not None:
            dstbuf.wrote(tag)
        return tag

    def rw(slot):
        return rws[0][0:1, slot * 512:(slot + 1) * 512]

    def ln_fold_stream(roff, srcap, tc, dst, dstb, doff, wcol=None):
        b = fb[0]
        sqdone = []
        for k in range(KT):
            sl = k % 2
            xl = p.dma("sync", xt[sl][:, :], srcap[k, :, tc * 512:(tc + 1) * 512],
                       f"xts{sl}", deps=xtb[sl].need_for_write())
            xtb[sl].wrote(xl)
            p.bop("dve", lambda e, d=dst[k], s=xt[sl], co=doff: e.tensor_copy(
                d[:, co:co + 512], s[:, :]), reads=[xtb[sl]], writes=[dstb[k]])
            p.bop("pe", lambda e, s=dst[k], st=(k == 0), sp=(k == KT - 1),
                  co=doff, o=b.ap: e.matmul(o[0:1, 0:512], c_ones[:, 0:1],
                                            s[:, co:co + 512],
                                            start=st, stop=sp),
                  reads=[dstb[k]], writes=[b])
            s2_ = stbf[k % 2]
            s2b_ = stbfb[k % 2]
            p.bop("act", lambda e, d=s2_, i=dst[k], co=doff: e.activation(
                d[:, :], i[:, co:co + 512], AF.Square),
                reads=[dstb[k]], writes=[s2b_])
            p.bop("pe", lambda e, d=s2_, st=(k == 0), sp=(k == KT - 1),
                  o=fb[5].ap: e.matmul(o[0:1, 0:512], c_ones[:, 0:1], d[:, :],
                                       start=st, stop=sp),
                  reads=[s2b_], writes=[fb[5]])
            if wcol is not None:
                p.bop("dve", lambda e, d=dst[k], s=xt[sl], co=doff, k=k:
                      e.tensor_scalar(d[:, co:co + 512], s[:, :],
                                      wcol[:, k:k + 1], None, OP.mult),
                      reads=[xtb[sl]], writes=[dstb[k]])
        p.bop("dve", lambda e, o=rw(0), i=b.ap: e.tensor_copy(o, i[0:1, 0:512]),
              reads=[b], writes=[rwsb])
        p.bop("dve", lambda e, o=rw(1), i=fb[5].ap: e.tensor_copy(
            o, i[0:1, 0:512]), reads=[fb[5]], writes=[rwsb])
        _ln_rows(roff)

    def _ln_rows(roff):
        p.bop("dve", lambda e, o=rw(2), i=rw(0): e.tensor_scalar_mul(o, i, 1.0 / D),
              reads=[rwsb], writes=[rwsb])
        p.bop("dve", lambda e, o=rw(3), i=rw(2): e.tensor_mul(o, i, i),
              reads=[rwsb], writes=[rwsb])
        p.bop("dve", lambda e, o=rw(4), i=rw(1): e.tensor_scalar(
            o, i, 1.0 / D, None, OP.mult), reads=[rwsb], writes=[rwsb])
        p.bop("dve", lambda e, o=rw(4), i=rw(3): e.tensor_sub(o, o, i),
              reads=[rwsb], writes=[rwsb])
        p.bop("dve", lambda e, o=lnm[0][0:1, roff:roff + 512], i=rw(2):
              e.tensor_scalar_mul(o, i, -1.0), reads=[rwsb], writes=[lnmb])
        p.bop("act", lambda e, o=rw(4): e.activation(
            o, o, AF.Ln, bias=csb["c_cvb"][0:1, 0:1]),
            reads=[rwsb], writes=[rwsb])
        p.bop("act", lambda e, o=lnr[0][0:1, roff:roff + 512], i=rw(4):
              e.activation(o, i, AF.Exp, scale=-0.5), reads=[rwsb], writes=[lnrb])

    def ln_fold_chunk(roff, goff, store_bf=True, dst=None, dstb=None,
                      doff=0):
        if dst is None:
            ct, ctb, co = hbt, hbtb, 0
        else:
            ct, ctb, co = dst, dstb, doff
        for k in range(KT):
            p.bop("dve", lambda e, d=ct[k], s=xt[k], co=co: e.tensor_copy(
                d[:, co:co + 512], s[:, :]), reads=[xtb[k]], writes=[ctb[k]])
        if store_bf and dst is None:
            deps = list(xbfdb.need_for_write())
            for k in range(KT):
                deps += ctb[k].need_for_read()
            for k in range(KT):
                tg = p.dma("sync", xbfd[k, :, goff:goff + 512], ct[k][:, :],
                           "hbt_st", deps=deps if k == 0 else ())
            for k in range(KT):
                ctb[k].read(tg)
            xbfdb.wrote(tg)
        b = fb[0]
        for k in range(KT):
            p.bop("pe", lambda e, s=ct[k], st=(k == 0), sp=(k == KT - 1),
                  co=co, o=b.ap: e.matmul(o[0:1, 0:512], c_ones[:, 0:1],
                                          s[:, co:co + 512],
                                          start=st, stop=sp),
                  reads=[ctb[k]], writes=[b])
        p.bop("dve", lambda e, o=rw(0), i=b.ap: e.tensor_copy(o, i[0:1, 0:512]),
              reads=[b], writes=[rwsb])
        for k in range(KT):
            s = stbf[k % 2]
            sb_ = stbfb[k % 2]
            p.bop("act", lambda e, d=s, i=ct[k], co=co: e.activation(
                d[:, :], i[:, co:co + 512], AF.Square),
                reads=[ctb[k]], writes=[sb_])
            p.bop("pe", lambda e, d=s, st=(k == 0), sp=(k == KT - 1), o=b.ap:
                  e.matmul(o[0:1, 0:512], c_ones[:, 0:1], d[:, :],
                           start=st, stop=sp), reads=[sb_], writes=[b])
        p.bop("dve", lambda e, o=rw(1), i=b.ap: e.tensor_copy(o, i[0:1, 0:512]),
              reads=[b], writes=[rwsb])
        _ln_rows(roff)

    def colsum(wi, csl):
        b = fb[5]
        for k in range(KT):
            p.bop("pe", lambda e, k=k, w=wsl[wi]: e.matmul(
                b.ap[0:1, 0:128], c_ones[:, 0:1], w[:, k * 128:(k + 1) * 128],
                start=(k == 0), stop=(k == KT - 1)), reads=[wslb[wi]], writes=[b])
        return p.bop("dve", lambda e, o=colrow[0:1, csl * 128:(csl + 1) * 128]:
                     e.tensor_copy(o, b.ap[0:1, 0:128]),
                     reads=[b], writes=[colrowb])

    def phase_A():
        for sc in range(SK):
            is_own = sc >= PK
            srcx = x_own if is_own else x_prev
            t0 = (sc - PK) * 512 if is_own else sc * 512
            deps = []
            for k in range(KT):
                deps += xtb[k].need_for_write()
            for k in range(KT):
                xl = p.dma("sync", xt[k][:, :], srcx[k, :, t0:t0 + 512], "xt",
                           deps=deps if k == 0 else ())
            for k in range(KT):
                xtb[k].wrote(xl)
            ln_fold_chunk(sc * 512, sc * 512)
        for hg in range(NH // 4):
            heads = range(hg * 4, hg * 4 + 4)
            for hi, h in enumerate(heads):
                for wi2, (nm, wsrc) in enumerate([("k", WA["wka"]),
                                                  ("v", WA["wva"]),
                                                  ("q", WA["wqa"])]):
                    slot = hi * 3 + wi2
                    tg = p.dma("pool", wsl[slot][:, :], wsrc[h], f"wsl{slot}",
                               deps=wslb[slot].need_for_write())
                    wslb[slot].wrote(tg)
                    colsum(slot, slot)
            for sc in range(SK):
                is_own = sc >= PK
                roff = sc * 512
                deps = list(xbfdb.need_for_read())
                for k in range(KT):
                    deps += hbtb[k].need_for_write()
                for k in range(KT):
                    xbl = p.dma("sync", hbt[k][:, :],
                                xbfd[k, :, roff:roff + 512], "hbt",
                                deps=deps if k == 0 else ())
                for k in range(KT):
                    hbtb[k].wrote(xbl)
                xbfdb.read(xbl)
                for hi, h in enumerate(heads):
                    projs = [("k", hi * 3), ("v", hi * 3 + 1)]
                    if is_own:
                        projs.append(("q", hi * 3 + 2))
                    for nm, slot in projs:
                        bank = fb[3 if (h + sc) % 2 == 0 else 4]
                        for k in range(KT):
                            p.bop("pe", lambda e, k=k, o=bank.ap[:, 0:512],
                                  w=wsl[slot], r=hbt[k]: e.matmul(
                                      o, w[:, k * 128:(k + 1) * 128], r[:, :],
                                      start=(k == 0), stop=False),
                                  reads=[wslb[slot], hbtb[k]], writes=[bank])
                        p.bop("pe", lambda e, o=bank.ap[:, 0:512],
                              l=colrow[0:1, slot * 128:(slot + 1) * 128],
                              r=lnm[0][0:1, roff:roff + 512]: e.matmul(
                                  o, l, r, start=False, stop=True),
                              reads=[colrowb, lnmb], writes=[bank])
                        if nm == "v":
                            a_v_epilogue(h, sc, bank, roff)
                        else:
                            a_qk_epilogue(nm, h, sc, bank, roff)

    def a_qk_epilogue(nm, h, sc, bank, roff):
        sb_, sbb = stbf[2], stbfb[2]
        p.bop("act", lambda e, d=sb_, i=bank.ap: e.activation(
            d[:, :], i[:, 0:512], AF.Square), reads=[bank], writes=[sbb])
        b2 = fb[5]
        p.bop("pe", lambda e, d=sb_, o=b2.ap: e.matmul(
            o[0:1, 0:512], c_ones[:, 0:1], d[:, :], start=True, stop=True),
            reads=[sbb], writes=[b2])
        p.bop("dve", lambda e, o=rw(5), i=b2.ap: e.tensor_copy(o, i[0:1, 0:512]),
              reads=[b2], writes=[rwsb])
        p.bop("dve", lambda e, o=rw(6), r_=lnr[0][0:1, roff:roff + 512]:
              e.tensor_mul(o, r_, r_), reads=[lnrb, rwsb], writes=[rwsb])
        p.bop("dve", lambda e, o=rw(6), i=rw(5): e.tensor_mul(o, o, i),
              reads=[rwsb], writes=[rwsb])
        p.bop("act", lambda e, o=rw(6): e.activation(
            o, o, AF.Ln, bias=csb["c_cvb"][0:1, 0:1], scale=1.0 / 128),
            reads=[rwsb], writes=[rwsb])
        p.bop("act", lambda e, o=rw(6): e.activation(o, o, AF.Exp, scale=-0.5),
              reads=[rwsb], writes=[rwsb])
        p.bop("dve", lambda e, o=rw(6), r_=lnr[0][0:1, roff:roff + 512]:
              e.tensor_mul(o, o, r_), reads=[lnrb, rwsb], writes=[rwsb])
        p.bop("pe", lambda e, i=rw(6), o=b2.ap: e.matmul(
            o[:, 0:512], csb["c_onesf"][0:1, :], i, start=True, stop=True),
            reads=[rwsb], writes=[b2])
        sg, sgb = stg[2], stgb[2]
        p.bop("dve", lambda e, d=sg, i=b2.ap: e.tensor_copy(d[:, :], i[:, 0:512]),
              reads=[b2], writes=[sgb])
        wcol = c_qrws if nm == "q" else csb["c_krw"]
        wbuf_ = c_qrws_b if nm == "q" else None
        dst, dstb = stbf[3], stbfb[3]
        ft = p.bop("dve", lambda e, w=wcol, d=dst, s=sg, i=bank.ap:
                   e.scalar_tensor_tensor(
            d[:, :], i[:, 0:512], w[:, 0:1], s[:, :], OP.mult, OP.mult),
            reads=[bank, sgb] + ([wbuf_] if wbuf_ else []), writes=[dstb])
        if nm == "q":
            tg = p.dma("sync", qad[h, :, roff - PK * 512:roff - PK * 512 + 512],
                       dst[:, :], "stbf3", deps=[ft] + qadb.need_for_write())
            qadb.wrote(tg)
        else:
            tg = p.dma("sync", kad[h, :, roff:roff + 512], dst[:, :], "stbf3",
                       deps=[ft] + kadb.need_for_write())
            kadb.wrote(tg)
        dstb.read(tg)

    def a_v_epilogue(h, sc, bank, roff):
        b2 = fb[5]
        p.bop("pe", lambda e, r_=lnr[0][0:1, roff:roff + 512], o=b2.ap:
              e.matmul(o[:, 0:512], csb["c_onesf"][0:1, :], r_,
                       start=True, stop=True), reads=[lnrb], writes=[b2])
        sg, sgb = stg[2], stgb[2]
        p.bop("dve", lambda e, d=sg, i=b2.ap: e.tensor_copy(d[:, :], i[:, 0:512]),
              reads=[b2], writes=[sgb])
        dst, dstb = stbf[3], stbfb[3]
        p.bop("dve", lambda e, d=dst, s=sg, i=bank.ap: e.tensor_tensor(
            d[:, :], i[:, 0:512], s[:, :], OP.mult),
            reads=[bank, sgb], writes=[dstb])
        xbk = xb[sc % 2]
        for j in range(4):
            p.bop("pe", lambda e, j=j, d=dst, o=xbk.ap: e.transpose(
                o[:, j * 128:(j + 1) * 128], d[:, j * 128:(j + 1) * 128],
                c_id[:, :]), reads=[dstb], writes=[xbk])
        vs, vsb = stbf[2], stbfb[2]
        t = p.bop("dve", lambda e, d=vs, i=xbk.ap: e.tensor_copy(
            d[:, :], i[:, 0:512]), reads=[xbk], writes=[vsb])
        tg = p.dma("sync", vtd[h, :, sc * 512:(sc + 1) * 512], vs[:, :], "stbf2",
                   deps=[t] + vtdb.need_for_write())
        vsb.read(tg)
        vtdb.wrote(tg)

    with ExitStack() as g_es:
        alloc_generic(g_es)
        phase_A()
    p.barrier()

    with ExitStack() as a1_es:
        sba1 = lambda n, s, d=F32: a1_es.enter_context(nc.sbuf_tensor(n, s, d))
        ka_s = [sba1(f"ka{i}", [128, 2 * T], BF16) for i in range(2)]
        ka_b = [Buf(t) for t in ka_s]
        vt_s = [sba1(f"vt{i}", [128, ST * 128], BF16) for i in range(2)]
        vt_b = [Buf(t) for t in vt_s]
        qa_s = [sba1(f"qa{i}", [128, 512], BF16) for i in range(2)]
        qa_b = [Buf(t) for t in qa_s]
        NKX = PK + (QC // 4 - 1) + 1
        cstr = sba1("cstr", [128, 4 * NKX * 512])
        cstrb = Buf(cstr)
        zstr = sba1("zstr", [128, NKX * 512])
        zstrb = Buf(zstr)
        pstr = sba1("pstr", [128, 4 * NKX * 512], BF16)
        pstrb = Buf(pstr)
        ptb = sba1("ptb", [128, ST * 512], BF16)
        ptbb = Buf(ptb)
        mrow = sba1("mrow", [128, 32])
        mrowb = Buf(mrow)
        arow = sba1("arow", [1, 3 * 512])
        arowb = Buf(arow)
        ydst = sba1("ydst", [128, 512], BF16)
        ydstb = Buf(ydst)
        astg = [sba1(f"astg{i}", [128, 512]) for i in range(2)]
        astgb = [Buf(t) for t in astg]
        asbf = sba1("asbf", [128, 512], BF16)
        asbfb = Buf(asbf)

        def nkc_of(qc):
            return PK + qc // 4 + 1

        def phase_A1():
            for qcg in range(QC // 4):
                qcs = list(range(qcg * 4, qcg * 4 + 4))
                nkcg = PK + qcg + 1
                ct_ = p.dma("sync", cstr[:, 0:4 * nkcg * 512],
                            cstrd[:, CO[qcg]:CO[qcg] + CL[qcg]], "cstr",
                            deps=cstrb.need_for_write())
                cstrb.wrote(ct_)
                for h in range(NH):
                    par = h % 2
                    ktag = p.dma("sync", ka_s[par][:, :], kad[h], f"ka{par}",
                                 deps=ka_b[par].need_for_write() +
                                 kadb.need_for_read())
                    ka_b[par].wrote(ktag)
                    kadb.read(ktag)
                    vtag = p.dma("sync", vt_s[par][:, :], vtd[h], f"vt{par}",
                                 deps=vt_b[par].need_for_write() +
                                 vtdb.need_for_read())
                    vt_b[par].wrote(vtag)
                    vtdb.read(vtag)
                    qtag = p.dma("sync", qa_s[par][:, 0:512],
                                 qad[h, :, qcg * 512:(qcg + 1) * 512],
                                 f"qa{par}",
                                 deps=qa_b[par].need_for_write() +
                                 qadb.need_for_read())
                    qa_b[par].wrote(qtag)
                    qadb.read(qtag)
                    kab, vtb_, qab = ka_b[par], vt_b[par], qa_b[par]
                    kaa, vta, qaa = ka_s[par], vt_s[par], qa_s[par]
                    slope = float(slopes[h])
                    for qi, qc in enumerate(qcs):
                        nk = nkc_of(qc)
                        for kc in range(nk):
                            bk = fb[2 + kc % 2]
                            p.bop("pe", lambda e, qc=qc, kc=kc, o=bk.ap,
                                  qaa=qaa, kaa=kaa, qi=qi:
                                  e.matmul(o[:, 0:512],
                                           qaa[:, qi * 128:(qi + 1) * 128],
                                           kaa[:, kc * 512:(kc + 1) * 512],
                                           start=True, stop=True),
                                  reads=[qab, kab], writes=[bk])
                            p.bop("dve", lambda e, qi=qi, kc=kc, o=bk.ap,
                                  sl=slope, nk=nk: e.scalar_tensor_tensor(
                                      zstr[:, kc * 512:(kc + 1) * 512],
                                      cstr[:, (qi * nk + kc) * 512:
                                           (qi * nk + kc + 1) * 512],
                                      -sl, o[:, 0:512], OP.mult, OP.add),
                                  reads=[bk, cstrb], writes=[zstrb])
                        p.bop("dve", lambda e, nk=nk: e.reduce_max(
                            mrow[:, 0:1], zstr[:, 0:nk * 512], AX),
                            reads=[zstrb], writes=[mrowb])
                        p.bop("dve", lambda e: e.tensor_scalar_mul(
                            mrow[:, 1:2], mrow[:, 0:1], -1.0),
                            reads=[mrowb], writes=[mrowb])
                        poff = qi * nkcg * 512
                        p.bop("act", lambda e, nk=nk, po=poff: e.activation(
                            pstr[:, po:po + nk * 512], zstr[:, 0:nk * 512],
                            AF.Exp, bias=mrow[:, 1:2]),
                            reads=[zstrb, mrowb], writes=[pstrb])
                    nst_max = nkc_of(qcs[-1]) * 4
                    for st in range(nst_max):
                        xbk = xb[st % 2]
                        q0 = 0
                        while nkc_of(qcs[0] + q0) * 4 <= st:
                            q0 += 1
                        for qi in range(q0, 4):
                            poff = qi * nkcg * 512
                            p.bop("pe", lambda e, st=st, qi=qi, po=poff,
                                  o=xbk.ap: e.transpose(
                                      o[:, qi * 128:(qi + 1) * 128],
                                      pstr[:, po + st * 128:po + (st + 1) * 128],
                                      c_id[:, :]), reads=[pstrb], writes=[xbk])
                        p.bop("dve", lambda e, st=st, q0=q0, o=xbk.ap:
                              e.tensor_copy(
                                  ptb[:, st * 512 + q0 * 128:(st + 1) * 512],
                                  o[:, q0 * 128:512]), reads=[xbk], writes=[ptbb])
                    yk = fb[4]
                    sk_ = fb[5]
                    for st in range(nst_max):
                        q0 = 0
                        while nkc_of(qcs[0] + q0) * 4 <= st:
                            q0 += 1
                        p.bop("pe", lambda e, st=st, q0=q0, vta=vta:
                              e.matmul(yk.ap[:, q0 * 128:512],
                                       vta[:, st * 128:(st + 1) * 128],
                                       ptb[:, st * 512 + q0 * 128:(st + 1) * 512],
                                       start=(st == 0), stop=(st == nst_max - 1)),
                              reads=[vtb_, ptbb], writes=[yk])
                        p.bop("pe", lambda e, st=st, q0=q0:
                              e.matmul(sk_.ap[0:1, q0 * 128:512],
                                       c_ones[:, 0:1],
                                       ptb[:, st * 512 + q0 * 128:(st + 1) * 512],
                                       start=(st == 0), stop=(st == nst_max - 1)),
                              reads=[ptbb], writes=[sk_])
                    p.bop("dve", lambda e: e.tensor_copy(arow[0:1, 0:512],
                                                         sk_.ap[0:1, 0:512]),
                          reads=[sk_], writes=[arowb])
                    sq_, sqb_ = asbf, asbfb
                    p.bop("act", lambda e, sq_=sq_: e.activation(
                        sq_[:, :], yk.ap[:, 0:512], AF.Square),
                        reads=[yk], writes=[sqb_])
                    p.bop("pe", lambda e, sq_=sq_: e.matmul(
                        sk_.ap[0:1, 0:512], c_ones[:, 0:1], sq_[:, :],
                        start=True, stop=True), reads=[sqb_], writes=[sk_])
                    p.bop("dve", lambda e: e.tensor_copy(arow[0:1, 512:1024],
                                                         sk_.ap[0:1, 0:512]),
                          reads=[sk_], writes=[arowb])
                    p.bop("dve", lambda e: e.tensor_mul(
                        arow[0:1, 1024:1536], arow[0:1, 0:512], arow[0:1, 0:512]),
                        reads=[arowb], writes=[arowb])
                    p.bop("dve", lambda e: e.tensor_scalar(
                        arow[0:1, 1024:1536], arow[0:1, 1024:1536], EPS, None,
                        OP.mult), reads=[arowb], writes=[arowb])
                    p.bop("dve", lambda e: e.scalar_tensor_tensor(
                        arow[0:1, 1024:1536], arow[0:1, 512:1024], 1.0 / 128,
                        arow[0:1, 1024:1536], OP.mult, OP.add),
                        reads=[arowb], writes=[arowb])
                    p.bop("act", lambda e: e.activation(
                        arow[0:1, 1024:1536], arow[0:1, 1024:1536], AF.Ln),
                        reads=[arowb], writes=[arowb])
                    p.bop("act", lambda e: e.activation(
                        arow[0:1, 1024:1536], arow[0:1, 1024:1536], AF.Exp,
                        scale=-0.5), reads=[arowb], writes=[arowb])
                    fbk = fb[3]
                    p.bop("pe", lambda e: e.matmul(
                        fbk.ap[:, 0:512], csb["c_onesf"][0:1, :],
                        arow[0:1, 1024:1536], start=True, stop=True),
                        reads=[arowb], writes=[fbk])
                    sg2, sg2b = astg[0], astgb[0]
                    p.bop("dve", lambda e, sg2=sg2: e.tensor_scalar(
                        sg2[:, :], yk.ap[:, 0:512], csb["c_slw"][:, 0:1], None,
                        OP.mult), reads=[yk], writes=[sg2b])
                    ft = p.bop("dve", lambda e, sg2=sg2: e.tensor_tensor(
                        ydst[:, :], sg2[:, :], fbk.ap[:, 0:512], OP.mult),
                        reads=[sg2b, fbk], writes=[ydstb])
                    tg = p.dma("sync", yad[h, :, qcg * 512:(qcg + 1) * 512],
                               ydst[:, :], "ydst",
                               deps=[ft] + yadb.need_for_write())
                    ydstb.read(tg)
                    yadb.wrote(tg)

        phase_A1()
    p.barrier()

    if True:
        with ExitStack() as a2_es:
            alloc_generic(a2_es, nws=2)
            ya_s = [a2_es.enter_context(nc.sbuf_tensor(f"ya{k}", [128, T], BF16))
                    for k in range(KT)]
            ya_b = [Buf(t) for t in ya_s]
            deps = list(yadb.need_for_read())
            for k in range(KT):
                deps += ya_b[k].need_for_write()
            for k in range(KT):
                tg = p.dma("sync", ya_s[k][:, :], yad[k], "ya_l",
                           deps=deps if k == 0 else ())
            for k in range(KT):
                ya_b[k].wrote(tg)
            yadb.read(tg)
            for m in range(KT):
                slot = m % 2
                wt = p.dma("pool", wsl[slot][:, :], WA["wpa"][m], f"wsl{slot}",
                           deps=wslb[slot].need_for_write())
                wslb[slot].wrote(wt)
                for tc in range(TC):
                    bk = fb[tc % 2]
                    for k in range(KT):
                        p.bop("pe", lambda e, k=k, tc=tc, o=bk.ap, w=wsl[slot],
                              ya=ya_s[k]:
                              e.matmul(o[:, 0:512],
                                       w[:, k * 128:(k + 1) * 128],
                                       ya[:, tc * 512:(tc + 1) * 512],
                                       start=(k == 0), stop=(k == KT - 1)),
                              reads=[wslb[slot], ya_b[k]], writes=[bk])
                    xs, xsb = stg[tc % 2], stgb[tc % 2]
                    lt = p.dma("sync", xs[:, :], x_own[m, :, tc * 512:(tc + 1) * 512],
                               f"a2x{tc % 2}", deps=xsb.need_for_write())
                    xsb.wrote(lt)
                    os_, osb = stg[2 + tc % 2], stgb[2 + tc % 2]
                    et = p.bop("dve", lambda e, o=bk.ap, xs=xs, os_=os_:
                               e.tensor_tensor(
                        os_[:, :], xs[:, :], o[:, 0:512], OP.add),
                        reads=[bk, xsb], writes=[osb])
                    tg = p.dma("sync", r1[m, :, tc * 512:(tc + 1) * 512],
                               os_[:, :], f"a2o{tc % 2}",
                               deps=[et] + r1b.need_for_write())
                    osb.read(tg)
                    r1b.wrote(tg)
        p.barrier()

    with ExitStack() as q_es:
        sbq = lambda n, s, d=F32: q_es.enter_context(nc.sbuf_tensor(n, s, d))
        qw = sbq("qw", [128, FFN])
        qwb = Buf(qw)
        qa2 = sbq("qa2", [128, FFN], BF16)
        qa2b = Buf(qa2)
        qn_ = sbq("qn", [128, FFN], BF16)
        qnb = Buf(qn_)
        gac = sbq("gac", [128, 48])
        gacb = Buf(gac)
        trow = sbq("trow", [1, 8])
        trowb = Buf(trow)
        tcol = sbq("tcol", [128, 2])
        tcolb = Buf(tcol)
        csrow = sbq("csrow", [1, 128], BF16)
        csrowb = Buf(csrow)

        def quantize(wsrc, wdst, wdstb, mm, kk, cs0):
            wid = kk * 128
            nelem = mm * 128 * wid
            for m in range(mm):
                tg = p.dma("sync", qw[:, 0:wid], wsrc[m], "qw",
                           deps=qwb.need_for_write())
                qwb.wrote(tg)
                p.bop("act", lambda e, m=m, w=wid: e.activation(
                    qw[:, 0:w], qw[:, 0:w], AF.Abs,
                    accum_out=gac[:, m:m + 1]), reads=[qwb], writes=[qwb, gacb])
            p.bop("dve", lambda e, mm=mm: e.reduce_sum(
                gac[:, 47:48], gac[:, 0:mm], AX), reads=[gacb], writes=[gacb])
            b = fb[0]
            p.bop("pe", lambda e, o=b.ap: e.matmul(
                o[0:1, 0:1], csb["c_onesc"][:, 0:1], gac[:, 47:48],
                start=True, stop=True), reads=[gacb], writes=[b])
            p.bop("dve", lambda e, o=b.ap, n=nelem: e.tensor_scalar(
                trow[0:1, 0:1], o[0:1, 0:1], 0.5 / n, None, OP.mult),
                reads=[b], writes=[trowb])
            p.bop("dve", lambda e: e.tensor_scalar(
                trow[0:1, 0:1], trow[0:1, 0:1], 0.5e-8, None, OP.add),
                reads=[trowb], writes=[trowb])
            b2 = fb[1]
            p.bop("pe", lambda e, o=b2.ap: e.matmul(
                o[:, 0:1], csb["c_onesf"][0:1, :], trow[0:1, 0:1],
                start=True, stop=True), reads=[trowb], writes=[b2])
            p.bop("dve", lambda e, i=b2.ap: e.tensor_copy(tcol[:, 0:1],
                                                          i[:, 0:1]),
                  reads=[b2], writes=[tcolb])
            p.bop("dve", lambda e: e.tensor_scalar_mul(
                tcol[:, 1:2], tcol[:, 0:1], -1.0), reads=[tcolb], writes=[tcolb])
            for m in range(mm):
                tg = p.dma("sync", qw[:, 0:wid], wsrc[m], "qw",
                           deps=qwb.need_for_write())
                qwb.wrote(tg)
                p.bop("dve", lambda e, w=wid: e.tensor_scalar(
                    qa2[:, 0:w], qw[:, 0:w], tcol[:, 0:1], None, OP.is_ge),
                    reads=[qwb, tcolb], writes=[qa2b])
                ft = p.bop("dve", lambda e, w=wid: e.scalar_tensor_tensor(
                    qn_[:, 0:w], qw[:, 0:w], tcol[:, 1:2], qa2[:, 0:w],
                    OP.is_le, OP.subtract), reads=[qwb, tcolb, qa2b],
                    writes=[qnb])
                tg = p.dma("sync", wdst[m], qn_[:, 0:wid], "qn",
                           deps=[ft] + wdstb.need_for_write())
                qnb.read(tg)
                wdstb.wrote(tg)
                bcs = fb[2]
                for k in range(kk):
                    p.bop("pe", lambda e, k=k, o=bcs.ap, st=(k == 0),
                          sp=(k == kk - 1): e.matmul(
                              o[0:1, 0:128], c_ones[:, 0:1],
                              qn_[:, k * 128:(k + 1) * 128], start=st, stop=sp),
                          reads=[qnb], writes=[bcs])
                ct_ = p.bop("dve", lambda e, o=csrow[0:1, 0:128],
                            i=bcs.ap: e.tensor_copy(o, i[0:1, 0:128]),
                            reads=[bcs], writes=[csrowb])
                tg = p.dma("sync", csd[0:1, (cs0 + m) * 128:(cs0 + m + 1) * 128],
                           csrow[0:1, 0:128], "csr",
                           deps=[ct_] + csdb.need_for_write())
                csrowb.read(tg)
                csdb.wrote(tg)

        quantize(fw1, wq1d, wq1db, FFN // 128, KT, 0)
        quantize(fw2, wq2d, wq2db, FFN // 128, KT, FFN // 128)
        quantize(fw3, wq3d, wq3db, KT, FFN // 128, 2 * (FFN // 128))
    p.barrier()

    def ffn_phase(tag, xin, out, outb, FF, w_gate, w_val, w_out, gate_silu,
                  quantized, cs_g0, cs_v0, extras, wcol=None, bcolbf=None,
                  wcolbf=None):
        MM = FF // 128
        NQ = 4 if FF == FFN else 8
        MQ = MM // NQ
        with ExitStack() as f_es:
            sbf = lambda n, s, d=F32: f_es.enter_context(
                nc.sbuf_tensor(f"{n}_{tag}", s, d))
            alloc_generic(f_es, nws=4, n_xt=2, with_hbt=False,
                          rows_len=T)
            xbff = [sbf(f"xbf{k}", [128, T], BF16) for k in range(KT)]
            xbffb = [Buf(t) for t in xbff]
            rbt = [sbf(f"rbt{t}", [128, 512]) for t in range(TC)]
            rbtb = [Buf(t) for t in rbt]
            hstr = [sbf(f"hs{i}", [128, T], BF16) for i in range(MQ)]
            hstrb = [Buf(t) for t in hstr]
            sgt = [sbf(f"sgt{i}", [128, 512]) for i in range(4)]
            sgtb = [Buf(t) for t in sgt]
            bwc = sbf("bwc", [128, 4])
            bwcb = Buf(bwc)
            crq = [sbf(f"crq{i}", [1, MQ * 128], BF16) for i in range(2)]
            crqb = [Buf(t) for t in crq]
            wob = sbf("wob", [128, MQ * 128], BF16)
            wobb = Buf(wob)
            for tc in range(TC):
                ln_fold_stream(tc * 512, xin, tc, xbff, xbffb, tc * 512,
                               wcol=wcol)
                bk = fb[1]
                p.bop("pe", lambda e, tc=tc, o=bk.ap,
                      r_=lnr[0][0:1, tc * 512:(tc + 1) * 512]: e.matmul(
                          o[:, 0:512], csb["c_onesf"][0:1, :], r_,
                          start=True, stop=True), reads=[lnrb], writes=[bk])
                p.bop("dve", lambda e, d=rbt[tc], i=bk.ap: e.tensor_copy(
                    d[:, :], i[:, 0:512]), reads=[bk], writes=[rbtb[tc]])
            for q in range(4):
                for mi in range(MQ):
                    m = q * MQ + mi
                    gsl, vsl = mi % 2, 2 + mi % 2
                    dmaq = "sync" if quantized else "pool"
                    sfx = "s" if quantized else ""
                    gt = p.dma(dmaq, wsl[gsl][:, :], w_gate[m], f"wsl{gsl}{sfx}",
                               deps=wslb[gsl].need_for_write())
                    wslb[gsl].wrote(gt)
                    vt = p.dma(dmaq, wsl[vsl][:, :], w_val[m], f"wsl{vsl}{sfx}",
                               deps=wslb[vsl].need_for_write())
                    wslb[vsl].wrote(vt)
                    if quantized:
                        if mi == 0:
                            for ci, c0 in ((0, cs_g0), (1, cs_v0)):
                                ct_ = p.dma("sync", crq[ci][0:1, :],
                                            csd[0:1, (c0 + q * MQ) * 128:
                                                (c0 + q * MQ + MQ) * 128],
                                            f"crq{ci}",
                                            deps=crqb[ci].need_for_write() +
                                            csdb.need_for_read())
                                crqb[ci].wrote(ct_)
                                csdb.read(ct_)
                        csg = (crq[0][0:1, mi * 128:(mi + 1) * 128], crqb[0])
                        csv = (crq[1][0:1, mi * 128:(mi + 1) * 128], crqb[1])
                    else:
                        for wi, sl, c0 in ((0, gsl, 8), (1, vsl, 9)):
                            bcs = fb[4]
                            lh = wcolbf if wcolbf is not None else c_ones
                            lsl = (lambda k: lh[:, k:k + 1]) if wcolbf is not None \
                                else (lambda k: c_ones[:, 0:1])
                            for k in range(KT):
                                p.bop("pe", lambda e, k=k, w=wsl[sl], o=bcs.ap,
                                      l=lsl(k), st=(k == 0), sp=(k == KT - 1):
                                      e.matmul(o[0:1, 0:128], l,
                                               w[:, k * 128:(k + 1) * 128],
                                               start=st, stop=sp),
                                      reads=[wslb[sl]], writes=[bcs])
                            p.bop("dve", lambda e, o=colrow[0:1, c0 * 128:
                                                           (c0 + 1) * 128],
                                  i=bcs.ap: e.tensor_copy(o, i[0:1, 0:128]),
                                  reads=[bcs], writes=[colrowb])
                            if bcolbf is not None:
                                bcs2 = fb[5]
                                for k in range(KT):
                                    p.bop("pe", lambda e, k=k, w=wsl[sl], o=bcs2.ap,
                                          st=(k == 0), sp=(k == KT - 1):
                                          e.matmul(o[:, 0:1],
                                                   w[:, k * 128:(k + 1) * 128],
                                                   bcolbf[:, k:k + 1],
                                                   start=st, stop=sp),
                                          reads=[wslb[sl]], writes=[bcs2])
                                p.bop("dve", lambda e, o=bwc[:, wi:wi + 1],
                                      i=bcs2.ap: e.tensor_copy(o, i[:, 0:1]),
                                      reads=[bcs2], writes=[bwcb])
                        csg = (colrow[0:1, 8 * 128:9 * 128], colrowb)
                        csv = (colrow[0:1, 9 * 128:10 * 128], colrowb)
                    for tc in range(TC):
                        bg = fb[tc % 2]
                        bv = fb[2 + tc % 2]
                        for (bk, sl, cs) in ((bg, gsl, csg), (bv, vsl, csv)):
                            for k in range(KT):
                                p.bop("pe", lambda e, k=k, w=wsl[sl], o=bk.ap,
                                      tc=tc, xb_=xbff[k]: e.matmul(
                                          o[:, 0:512],
                                          w[:, k * 128:(k + 1) * 128],
                                          xb_[:, tc * 512:(tc + 1) * 512],
                                          start=(k == 0), stop=False),
                                      reads=[wslb[sl], xbffb[k]], writes=[bk])
                            p.bop("pe", lambda e, o=bk.ap, l=cs[0],
                                  r_=lnm[0][0:1, tc * 512:(tc + 1) * 512]:
                                  e.matmul(o[:, 0:512], l, r_, start=False,
                                           stop=True),
                                  reads=[cs[1], lnmb], writes=[bk])
                        e1, e1b = sgt[tc % 2], sgtb[tc % 2]
                        e2, e2b = sgt[2 + tc % 2], sgtb[2 + tc % 2]
                        p.bop("dve", lambda e, d=e1, i=bg.ap, r_=rbt[tc]:
                              e.tensor_tensor(d[:, :], i[:, 0:512], r_[:, :],
                                              OP.mult),
                              reads=[bg, rbtb[tc]], writes=[e1b])
                        af_ = AF.Silu if gate_silu else AF.Sigmoid
                        sc_ = -1.0 if quantized else 1.0
                        if bcolbf is not None:
                            p.bop("act", lambda e, d=e1, af=af_, sc=sc_:
                                  e.activation(d[:, :], d[:, :], af,
                                               bias=bwc[:, 0:1], scale=sc),
                                  reads=[e1b, bwcb], writes=[e1b])
                        else:
                            p.bop("act", lambda e, d=e1, af=af_, sc=sc_:
                                  e.activation(d[:, :], d[:, :], af, scale=sc),
                                  reads=[e1b], writes=[e1b])
                        p.bop("dve", lambda e, d=e2, i=bv.ap, r_=rbt[tc]:
                              e.tensor_tensor(d[:, :], i[:, 0:512], r_[:, :],
                                              OP.mult),
                              reads=[bv, rbtb[tc]], writes=[e2b])
                        if bcolbf is not None:
                            p.bop("dve", lambda e, d=hstr[mi], tc=tc, e1=e1,
                                  e2=e2: e.scalar_tensor_tensor(
                                      d[:, tc * 512:(tc + 1) * 512], e2[:, :],
                                      bwc[:, 1:2], e1[:, :], OP.add, OP.mult),
                                  reads=[e1b, e2b, bwcb], writes=[hstrb[mi]])
                        else:
                            p.bop("dve", lambda e, d=hstr[mi], tc=tc, e1=e1,
                                  e2=e2: e.tensor_tensor(
                                      d[:, tc * 512:(tc + 1) * 512], e1[:, :],
                                      e2[:, :], OP.mult),
                                  reads=[e1b, e2b], writes=[hstrb[mi]])
                for mo in range(KT):
                    wt = p.dma("sync" if quantized else "pool", wob[:, :],
                               w_out[mo, :, q * MQ * 128:(q + 1) * MQ * 128],
                               "wobs" if quantized else "wob",
                               deps=wobb.need_for_write())
                    wobb.wrote(wt)
                    for tc in range(TC):
                        bk = fb[4 + tc % 2]
                        for ki in range(MQ):
                            p.bop("pe", lambda e, ki=ki, o=bk.ap, tc=tc:
                                  e.matmul(o[:, 0:512],
                                           wob[:, ki * 128:(ki + 1) * 128],
                                           hstr[ki][:, tc * 512:(tc + 1) * 512],
                                           start=(ki == 0), stop=(ki == MQ - 1)),
                                  reads=[wobb, hstrb[ki]], writes=[bk])
                        s1, s1b = stg[tc % 2], stgb[tc % 2]
                        s2, s2b = stg[2 + tc % 2], stgb[2 + tc % 2]
                        if q == 0:
                            et = p.bop("dve", lambda e, d=s2, i=bk.ap:
                                       e.tensor_copy(d[:, :], i[:, 0:512]),
                                       reads=[bk], writes=[s2b])
                            tg = p.dma("sync",
                                       accd[mo, :, tc * 512:(tc + 1) * 512],
                                       s2[:, :], f"fo{tc % 2}",
                                       deps=[et] + accdb.need_for_write())
                            s2b.read(tg)
                            accdb.wrote(tg)
                        else:
                            lt = p.dma("sync", s1[:, :],
                                       accd[mo, :, tc * 512:(tc + 1) * 512],
                                       f"fa{tc % 2}",
                                       deps=s1b.need_for_write() +
                                       accdb.need_for_read())
                            s1b.wrote(lt)
                            accdb.read(lt)
                            et = p.bop("dve", lambda e, d=s2, a=s1, i=bk.ap:
                                       e.tensor_tensor(d[:, :], a[:, :],
                                                       i[:, 0:512], OP.add),
                                       reads=[bk, s1b], writes=[s2b])
                            if q < 3:
                                tg = p.dma("sync",
                                           accd[mo, :, tc * 512:(tc + 1) * 512],
                                           s2[:, :], f"fo{tc % 2}",
                                           deps=[et] + accdb.need_for_write())
                                s2b.read(tg)
                                accdb.wrote(tg)
                            else:
                                for xs in [xin] + extras:
                                    lt = p.dma("sync", s1[:, :],
                                               xs[mo, :, tc * 512:(tc + 1) * 512],
                                               f"fa{tc % 2}",
                                               deps=s1b.need_for_write())
                                    s1b.wrote(lt)
                                    et = p.bop("dve", lambda e, d=s2, a=s1:
                                               e.tensor_add(d[:, :], d[:, :],
                                                            a[:, :]),
                                               reads=[s1b, s2b], writes=[s2b])
                                tg = p.dma("sync",
                                           out[mo, :, tc * 512:(tc + 1) * 512],
                                           s2[:, :], f"fo{tc % 2}",
                                           deps=[et] + outb.need_for_write())
                                s2b.read(tg)
                                outb.wrote(tg)

    ffn_phase("f1", r1, r2, r2b, FFN, wq1d, wq2d, wq3d, False, True,
              0, FFN // 128, [])
    p.barrier()

    halosb = p.sb("halosb", [128, 64])
    halosbb = Buf(halosb)
    mt_ = p.bop("dve", lambda e: e.memset(halosb[:, :], 0.0), writes=[halosbb])
    with ExitStack() as b0_es:
        alloc_generic(b0_es, nws=0)
        sb0 = lambda n, s, d=F32: b0_es.enter_context(nc.sbuf_tensor(n, s, d))
        xbt = [sb0(f"xbt{k}", [128, 512]) for k in range(KT)]
        xbtb = [Buf(t) for t in xbt]
        h1t = [sb0(f"h1t{k}", [128, 512], BF16) for k in range(KT)]
        h1tb = [Buf(t) for t in h1t]
        for tc in range(TC):
            deps = []
            for k in range(KT):
                deps += xtb[k].need_for_write()
            for k in range(KT):
                xl = p.dma("sync", xt[k][:, :], r2[k, :, tc * 512:(tc + 1) * 512],
                           "xt", deps=deps if k == 0 else ())
            for k in range(KT):
                xtb[k].wrote(xl)
            ln_fold_chunk(tc * 512, 0, store_bf=False)
            ro = tc * 512
            p.bop("dve", lambda e, o=rw(5), r_=lnr[0][0:1, ro:ro + 512]:
                  e.tensor_mul(o, r_, r_), reads=[lnrb, rwsb], writes=[rwsb])
            p.bop("dve", lambda e, o=rw(5): e.tensor_scalar(
                o, o, -EPS, 1.0 + EPS, OP.mult, OP.add),
                reads=[rwsb], writes=[rwsb])
            p.bop("act", lambda e, o=rw(5): e.activation(o, o, AF.Ln),
                  reads=[rwsb], writes=[rwsb])
            p.bop("act", lambda e, o=rw(5): e.activation(o, o, AF.Exp,
                                                         scale=-0.5),
                  reads=[rwsb], writes=[rwsb])
            p.bop("dve", lambda e, o=rw(6), r_=lnr[0][0:1, ro:ro + 512],
                  nm=lnm[0][0:1, ro:ro + 512]: e.tensor_tensor(
                      o, r_, nm, OP.mult), reads=[lnrb, lnmb, rwsb],
                  writes=[rwsb])
            for bi, off in ((1, None), (2, 6), (3, 5)):
                bk = fb[bi]
                if off is None:
                    p.bop("pe", lambda e, o=bk.ap,
                          r_=lnr[0][0:1, ro:ro + 512]: e.matmul(
                              o[:, 0:512], csb["c_onesf"][0:1, :], r_,
                              start=True, stop=True),
                          reads=[lnrb], writes=[bk])
                else:
                    p.bop("pe", lambda e, o=bk.ap, i=rw(off): e.matmul(
                        o[:, 0:512], csb["c_onesf"][0:1, :], i,
                        start=True, stop=True), reads=[rwsb], writes=[bk])
            sb_s2 = stg[0]
            p.bop("dve", lambda e, d=sb_s2, i=fb[3].ap: e.tensor_copy(
                d[:, :], i[:, 0:512]), reads=[fb[3]], writes=[stgb[0]])
            for k in range(KT):
                p.bop("dve", lambda e, d=xbt[k], s=xt[k], i=fb[1].ap:
                      e.tensor_tensor(d[:, :], s[:, :], i[:, 0:512], OP.mult),
                      reads=[xtb[k], fb[1]], writes=[xbtb[k]])
                et = p.bop("dve", lambda e, d=xbt[k], i=fb[2].ap:
                           e.tensor_tensor(d[:, :], d[:, :], i[:, 0:512],
                                           OP.add), reads=[fb[2], xbtb[k]],
                           writes=[xbtb[k]])
                tg = p.dma("sync", xbd[k, :, tc * 512:(tc + 1) * 512],
                           xbt[k][:, :], "xbt", deps=([et] +
                           xbdb.need_for_write() if k == 0 else [et]))
                xbtb[k].read(tg)
                p.bop("dve", lambda e, d=stg[1], s=xbt[k], s2=sb_s2:
                      e.tensor_tensor(d[:, :], s[:, :], s2[:, :], OP.mult),
                      reads=[xbtb[k], stgb[0]], writes=[stgb[1]])
                ht_ = p.bop("dve", lambda e, d=h1t[k], s=stg[1], k=k:
                            e.tensor_scalar(d[:, :], s[:, :],
                                            csb["c_l1w"][:, k:k + 1],
                                            csb["c_l1b"][:, k:k + 1],
                                            OP.mult, OP.add),
                            reads=[stgb[1]], writes=[h1tb[k]])
                tg = p.dma("sync", hbd[k, :, 3 + tc * 512:3 + (tc + 1) * 512],
                           h1t[k][:, :], "h1t",
                           deps=[ht_] + (hbdb.need_for_write() if k == 0 else []))
                h1tb[k].read(tg)
                if tc == TC - 1:
                    p.bop("dve", lambda e, k=k, s=h1t[k]: e.tensor_copy(
                        halosb[:, k * 3:k * 3 + 3], s[:, 509:512]),
                        reads=[h1tb[k]], writes=[halosbb])
            xbdb.wrote(tg)
            hbdb.wrote(tg)
    ht_ = p.dma("sync", cc1i[:, :], halosb[:, :], "cc1",
                deps=halosbb.need_for_read() + cc1ib.need_for_write())
    halosbb.read(ht_)
    cc1ib.wrote(ht_)
    p.barrier()

    def collective(cin, cout):
        p.cnt["CC"] += 1
        h = p.sem["CC"]
        p.raw("pool", lambda e, h=h, ci=cin, co=cout: e.collective_compute(
            "AllGather", OP.bypass, replica_groups=groups,
            ins=[ci.opt()], outs=[co.opt()]).then_inc(h, 1))
        return ("CC", p.cnt["CC"])

    cct1 = collective(cc1i, cc1o)
    tg = p.dma("sync", halosb[:, 0:64], cc1o[0:128, :], "cc1l", deps=[cct1])
    halosbb.wrote(tg)
    hm = p.bop("dve", lambda e: e.tensor_tensor(
        halosb[:, 0:64], halosb[:, 0:64], csb["c_mask"][:, 0:64], OP.mult),
        reads=[halosbb], writes=[halosbb])
    with ExitStack() as hc_es:
        halobf = hc_es.enter_context(nc.sbuf_tensor("halobf", [128, 64], BF16))
        halobfb = Buf(halobf)
        hb_ = p.bop("dve", lambda e: e.tensor_copy(halobf[:, :], halosb[:, :]),
                    reads=[halosbb], writes=[halobfb])
        for k in range(KT):
            tg = p.dma("sync", hbd[k, :, 0:3], halobf[:, k * 3:k * 3 + 3],
                       "halost", deps=[hb_] + (hbdb.need_for_write()
                                              if k == 0 else []))
            halobfb.read(tg)
        hbdb.wrote(tg)
    p.barrier()

    NB_ = NB
    afd, afdb = dsc("afd", (16, T))
    with ExitStack() as b1_es:
        sb1 = lambda n, s, d=F32: b1_es.enter_context(nc.sbuf_tensor(n, s, d))
        alloc_generic(b1_es, nws=8, n_xt=0, rows_len=64)
        h1in = [sb1(f"h1in{k}", [128, 515], BF16) for k in range(KT)]
        h1inb = [Buf(t) for t in h1in]
        xct = [sb1(f"xct{k}", [128, 512], BF16) for k in range(KT)]
        xctb = [Buf(t) for t in xct]
        lgam = sb1("lgam", [16, T])
        lgamb = Buf(lgam)
        cblk = sb1("cblk", [16, T])
        cblkb = Buf(cblk)
        cbl2 = sb1("cbl2", [16, T])
        cbl2b = Buf(cbl2)
        afull = sb1("afull", [16, T])
        afullb = Buf(afull)
        bsr = sb1("bsr", [16, 64])
        bsrb = Buf(bsr)
        ctn = sb1("ctn", [128, NB_ * 16])
        ctnb = Buf(ctn)
        carry = sb1("carry", [128, 16])
        carryb = Buf(carry)
        gcolt = sb1("gcolt", [128, 1])
        gcoltb = Buf(gcolt)
        wggbf = sb1("wggbf", [128, KT * NH], BF16)
        wggbfb = Buf(wggbf)
        ufm = sb1("ufm", [128, 128], BF16)
        ufmb = Buf(ufm)
        uts = sb1("uts", [128, 128], BF16)
        utsb = Buf(uts)
        wt_ = sb1("wt", [128, 128], BF16)
        wtb = Buf(wt_)
        mst = sb1("mst", [128, 128])
        mstb = Buf(mst)
        kbt = sb1("kbt", [128, 512], BF16)
        kbtb = Buf(kbt)
        vft = sb1("vft", [128, 512])
        vftb = Buf(vft)
        igt = sb1("igt", [128, 512])
        igtb = Buf(igt)

        wg_ = p.dma("pool", wggbf[:, :], wgg, "wggbf")
        wggbfb.wrote(wg_)

        def b_l2(bk, out_t, out_b, _bf):
            sq_, sqb_ = stbf[0], stbfb[0]
            p.bop("act", lambda e, d=sq_, i=bk.ap: e.activation(
                d[:, :], i[:, 0:512], AF.Square), reads=[bk], writes=[sqb_])
            bs = fb[4]
            p.bop("pe", lambda e, d=sq_, o=bs.ap: e.matmul(
                o[0:1, 0:512], c_ones[:, 0:1], d[:, :], start=True, stop=True),
                reads=[sqb_], writes=[bs])
            p.bop("dve", lambda e, o=rw(0), i=bs.ap: e.tensor_copy(
                o, i[0:1, 0:512]), reads=[bs], writes=[rwsb])
            p.bop("act", lambda e, o=rw(0): e.activation(
                o, o, AF.Ln, bias=csb["c_cvb"][0:1, 1:2]),
                reads=[rwsb], writes=[rwsb])
            p.bop("act", lambda e, o=rw(0): e.activation(
                o, o, AF.Exp, scale=-0.5), reads=[rwsb], writes=[rwsb])
            bb = fb[5]
            p.bop("pe", lambda e, o=bb.ap, i=rw(0): e.matmul(
                o[:, 0:512], csb["c_onesf"][0:1, :], i, start=True, stop=True),
                reads=[rwsb], writes=[bb])
            s1_, s1b_ = stg[3], stgb[3]
            p.bop("dve", lambda e, d=s1_, i=bb.ap: e.tensor_copy(
                d[:, :], i[:, 0:512]), reads=[bb], writes=[s1b_])
            p.bop("dve", lambda e, d=out_t, i=bk.ap, r_=s1_: e.tensor_tensor(
                d[:, :], i[:, 0:512], r_[:, :], OP.mult),
                reads=[bk, s1b_], writes=[out_b])

        def b_vln(bk, out_t, out_b):
            sq_, sqb_ = stbf[0], stbfb[0]
            p.bop("act", lambda e, d=sq_, i=bk.ap: e.activation(
                d[:, :], i[:, 0:512], AF.Square), reads=[bk], writes=[sqb_])
            bs = fb[4]
            p.bop("pe", lambda e, d=sq_, o=bs.ap: e.matmul(
                o[0:1, 0:512], c_ones[:, 0:1], d[:, :], start=True, stop=True),
                reads=[sqb_], writes=[bs])
            p.bop("dve", lambda e, o=rw(1), i=bs.ap: e.tensor_copy(
                o, i[0:1, 0:512]), reads=[bs], writes=[rwsb])
            s0_, s0b_ = stg[3], stgb[3]
            p.bop("dve", lambda e, d=s0_, i=bk.ap: e.tensor_copy(
                d[:, :], i[:, 0:512]), reads=[bk], writes=[s0b_])
            p.bop("pe", lambda e, d=s0_, o=bs.ap: e.matmul(
                o[0:1, 0:512], csb["c_onesc"][:, 0:1], d[:, :],
                start=True, stop=True), reads=[s0b_], writes=[bs])
            p.bop("dve", lambda e, o=rw(0), i=bs.ap: e.tensor_copy(
                o, i[0:1, 0:512]), reads=[bs], writes=[rwsb])
            p.bop("dve", lambda e, o=rw(2), i=rw(0): e.tensor_scalar_mul(
                o, i, 1.0 / 128), reads=[rwsb], writes=[rwsb])
            p.bop("dve", lambda e, o=rw(3), i=rw(2): e.tensor_mul(o, i, i),
                  reads=[rwsb], writes=[rwsb])
            p.bop("dve", lambda e, o=rw(4), i=rw(1): e.tensor_scalar(
                o, i, 1.0 / 128, None, OP.mult), reads=[rwsb], writes=[rwsb])
            p.bop("dve", lambda e, o=rw(4), i=rw(3): e.tensor_sub(o, o, i),
                  reads=[rwsb], writes=[rwsb])
            p.bop("act", lambda e, o=rw(4): e.activation(
                o, o, AF.Ln, bias=csb["c_cvb"][0:1, 0:1]),
                reads=[rwsb], writes=[rwsb])
            p.bop("act", lambda e, o=rw(4): e.activation(
                o, o, AF.Exp, scale=-0.5), reads=[rwsb], writes=[rwsb])
            p.bop("dve", lambda e, o=rw(5), a=rw(2), b_=rw(4): e.tensor_mul(
                o, a, b_), reads=[rwsb], writes=[rwsb])
            p.bop("dve", lambda e, o=rw(5), i=rw(5): e.tensor_scalar_mul(
                o, i, -1.0), reads=[rwsb], writes=[rwsb])
            b1_, b2_ = fb[4], fb[5]
            p.bop("pe", lambda e, o=b1_.ap, i=rw(4): e.matmul(
                o[:, 0:512], csb["c_onesf"][0:1, :], i, start=True, stop=True),
                reads=[rwsb], writes=[b1_])
            p.bop("pe", lambda e, o=b2_.ap, i=rw(5): e.matmul(
                o[:, 0:512], csb["c_onesf"][0:1, :], i, start=True, stop=True),
                reads=[rwsb], writes=[b2_])
            s1_, s1b_ = stg[3], stgb[3]
            p.bop("dve", lambda e, d=s1_, i=b1_.ap: e.tensor_copy(
                d[:, :], i[:, 0:512]), reads=[b1_], writes=[s1b_])
            p.bop("dve", lambda e, d=out_t, i=bk.ap, r_=s1_: e.tensor_tensor(
                d[:, :], i[:, 0:512], r_[:, :], OP.mult),
                reads=[bk, s1b_], writes=[out_b])
            p.bop("dve", lambda e, d=out_t, i=b2_.ap: e.tensor_tensor(
                d[:, :], d[:, :], i[:, 0:512], OP.add),
                reads=[b2_, out_b], writes=[out_b])
            p.bop("dve", lambda e, d=out_t: e.tensor_scalar(
                d[:, :], d[:, :], csb["c_vnw"][:, 0:1], csb["c_vnb"][:, 0:1],
                OP.mult, OP.add), reads=[out_b], writes=[out_b])

        for tc in range(TC):
            deps = []
            for k in range(KT):
                deps += h1inb[k].need_for_write()
            for k in range(KT):
                hl = p.dma("sync", h1in[k][:, :],
                           hbd[k, :, tc * 512:tc * 512 + 515], "h1in",
                           deps=(deps + hbdb.need_for_read()) if k == 0 else ())
            for k in range(KT):
                h1inb[k].wrote(hl)
            hbdb.read(hl)
            for k in range(KT):
                acc, accb = stg[k % 2], stgb[k % 2]
                p.bop("dve", lambda e, d=acc, s=h1in[k], k=k: e.tensor_scalar(
                    d[:, :], s[:, 0:512], csb["c_cw"][:, k * 4:k * 4 + 1], None,
                    OP.mult), reads=[h1inb[k]], writes=[accb])
                for j in range(1, 4):
                    p.bop("dve", lambda e, d=acc, s=h1in[k], k=k, j=j:
                          e.scalar_tensor_tensor(
                              d[:, :], s[:, j:512 + j],
                              csb["c_cw"][:, k * 4 + j:k * 4 + j + 1],
                              d[:, :], OP.mult, OP.add),
                          reads=[h1inb[k], accb], writes=[accb])
                ct_ = p.bop("act", lambda e, d=xct[k], s=acc, k=k: e.activation(
                    d[:, :], s[:, :], AF.Silu, bias=csb["c_cb"][:, k:k + 1]),
                    reads=[accb], writes=[xctb[k]])
                tg = p.dma("sync", xcd[k, :, tc * 512:(tc + 1) * 512],
                           xct[k][:, :], "xcst",
                           deps=[ct_] + (xcdb.need_for_write() if k == 0 else []))
                xctb[k].read(tg)
            xcdb.wrote(tg)
            for tk in range(4):
                bk = fb[4]
                for k in range(KT):
                    p.bop("pe", lambda e, k=k, tk=tk, o=bk.ap, s=xct[k]:
                          e.matmul(o[:, 0:16], s[:, tk * 128:(tk + 1) * 128],
                                   wggbf[:, k * NH:(k + 1) * NH],
                                   start=(k == 0), stop=(k == KT - 1)),
                          reads=[xctb[k], wggbfb], writes=[bk])
                z_, zb_ = stg[2], stgb[2]
                p.bop("dve", lambda e, d=z_, i=bk.ap: e.tensor_tensor(
                    d[:, 0:16], i[:, 0:16], csb["c_bgb"][:, :], OP.add),
                    reads=[bk], writes=[zb_])
                a_, ab_ = stg[3], stgb[3]
                p.bop("dve", lambda e, d=a_, s=z_: e.tensor_scalar(
                    d[:, 0:16], s[:, 0:16], -1.0, None, OP.mult),
                    reads=[zb_], writes=[ab_])
                p.bop("dve", lambda e, d=a_, s=z_: e.tensor_max(
                    d[:, 0:16], d[:, 0:16], s[:, 0:16]),
                    reads=[zb_, ab_], writes=[ab_])
                p.bop("act", lambda e, d=a_: e.activation(
                    d[:, 0:16], d[:, 0:16], AF.Exp, scale=-1.0),
                    reads=[ab_], writes=[ab_])
                p.bop("act", lambda e, d=a_: e.activation(
                    d[:, 0:16], d[:, 0:16], AF.Ln, bias=1.0),
                    reads=[ab_], writes=[ab_])
                p.bop("dve", lambda e, d=z_, s=z_: e.tensor_scalar(
                    d[:, 16:32], s[:, 0:16], 0.0, None, OP.min),
                    reads=[zb_], writes=[zb_])
                lg_ = p.bop("dve", lambda e, d=z_, a=a_: e.tensor_tensor(
                    d[:, 0:16], d[:, 16:32], a[:, 0:16], OP.subtract),
                    reads=[zb_, ab_], writes=[zb_])
                xbk = fb[5]
                p.bop("pe", lambda e, s=z_, o=xbk.ap: e.transpose(
                    o[0:16, 0:128], s[:, 0:16], csb["c_idf"][:, :]),
                    reads=[zb_], writes=[xbk])
                p.bop("dve", lambda e, i=xbk.ap,
                      o=lgam[0:16, tc * 512 + tk * 128:tc * 512 + (tk + 1) * 128]:
                      e.tensor_copy(o, i[0:16, 0:128]),
                      reads=[xbk], writes=[lgamb])
        def hs_step(sv, sb_, dv, db_, d_):
            p.bop("dve", lambda e, s=sv, dd=dv, d_=d_: e.tensor_add(
                dd[:, :, d_:128], s[:, :, d_:128], s[:, :, 0:128 - d_]),
                reads=[sb_], writes=[db_])
            p.bop("dve", lambda e, s=sv, dd=dv, d_=d_: e.tensor_copy(
                dd[:, :, 0:d_], s[:, :, 0:d_]), reads=[sb_], writes=[db_])

        v_lg = lgam[0:16, 0:T].rearrange("p (n j) -> p n j", j=128)
        v_c1 = cblk[0:16, 0:T].rearrange("p (n j) -> p n j", j=128)
        v_c2 = cbl2[0:16, 0:T].rearrange("p (n j) -> p n j", j=128)
        seq = [(v_lg, lgamb), (v_c1, cblkb), (v_c2, cbl2b)]
        cur = 0
        for st_i, d_ in enumerate([1, 2, 4, 8, 16, 32, 64]):
            svv, svb = seq[cur]
            dst_i = 1 if cur != 1 else 2
            if st_i == 6:
                dst_i = 1
            dvv, dvb = seq[dst_i]
            hs_step(svv, svb, dvv, dvb, d_)
            cur = dst_i
        p.bop("dve", lambda e, o=bsr[0:16, 0:NB_],
              s=cblk[0:16, 0:T].rearrange("p (n j) -> p n j", j=128)[:, :, 127:128]:
              e.tensor_copy(o, s.rearrange("p n j -> p (n j)")),
              reads=[cblkb], writes=[bsrb])
        cur_o, oth_o = 0, 16
        d_ = 1
        while d_ < NB_:
            p.bop("dve", lambda e, d_=d_, a=cur_o, b_=oth_o: e.tensor_add(
                bsr[0:16, b_ + d_:b_ + NB_], bsr[0:16, a + d_:a + NB_],
                bsr[0:16, a:a + NB_ - d_]), reads=[bsrb], writes=[bsrb])
            p.bop("dve", lambda e, d_=d_, a=cur_o, b_=oth_o: e.tensor_copy(
                bsr[0:16, b_:b_ + d_], bsr[0:16, a:a + d_]),
                reads=[bsrb], writes=[bsrb])
            cur_o, oth_o = oth_o, cur_o
            d_ *= 2
        p.bop("dve", lambda e, a=cur_o, o=bsr[0:16, 32:32 + NB_],
              s=cblk[0:16, 0:T].rearrange("p (n j) -> p n j", j=128)[:, :, 127:128]:
              e.tensor_sub(o, bsr[0:16, a:a + NB_],
                           s.rearrange("p n j -> p (n j)")),
              reads=[bsrb, cblkb], writes=[bsrb])
        p.bop("dve", lambda e,
              o=cbl2[0:16, 0:T].rearrange("p (n j) -> p n j", j=128),
              s=cblk[0:16, 0:T].rearrange("p (n j) -> p n j", j=128),
              of=bsr[0:16, 32:32 + NB_].rearrange(
                  "p (n j) -> p n j", j=1).broadcast_to((16, NB_, 128)):
              e.tensor_tensor(o, s, of, OP.add),
              reads=[bsrb, cblkb], writes=[cbl2b])
        af_ = p.bop("act", lambda e: e.activation(
            afull[0:16, 0:T], cbl2[0:16, 0:T], AF.Exp),
            reads=[cbl2b], writes=[afullb])
        tg = p.dma("sync", afd[0:16, :], afull[0:16, 0:T], "afd",
                   deps=[af_] + afdb.need_for_write())
        afullb.read(tg)
        afdb.wrote(tg)
        for nb in range(NB_):
            xbk = fb[5]
            p.bop("pe", lambda e, nb=nb, o=xbk.ap: e.transpose(
                o[:, 0:16], cblk[0:16, nb * 128:(nb + 1) * 128],
                csb["c_idf"][0:16, 0:16]), reads=[cblkb], writes=[xbk])
            p.bop("dve", lambda e, nb=nb, i=xbk.ap: e.tensor_scalar_mul(
                ctn[:, nb * 16:(nb + 1) * 16], i[:, 0:16], -1.0),
                reads=[xbk], writes=[ctnb])
        for hg in range(NH // 2):
            heads = [hg * 2, hg * 2 + 1]
            wt8 = {}
            for hi, h in enumerate(heads):
                for wi2, (nm, wsrc) in enumerate(
                        [("k", WB["wkb"]), ("v", WB["wvb"]),
                         ("ig", WB["wig"]), ("q", WB["wqb"])]):
                    slot = hi * 4 + wi2
                    tg = p.dma("pool", wsl[slot][:, :], wsrc[h], f"wsl{slot}",
                               deps=wslb[slot].need_for_write())
                    wslb[slot].wrote(tg)
                    wt8[(h, nm)] = tg
            for tc in range(TC):
                deps = list(hbdb.need_for_read())
                for k in range(KT):
                    deps += hbtb[k].need_for_write()
                for k in range(KT):
                    hl = p.dma("sync", hbt[k][:, :],
                               hbd[k, :, 3 + tc * 512:3 + (tc + 1) * 512],
                               "hbt", deps=deps if k == 0 else ())
                for k in range(KT):
                    hbtb[k].wrote(hl)
                hbdb.read(hl)
                deps = list(xcdb.need_for_read())
                for k in range(KT):
                    deps += xctb[k].need_for_write()
                for k in range(KT):
                    xl = p.dma("sync", xct[k][:, :],
                               xcd[k, :, tc * 512:(tc + 1) * 512],
                               "xct", deps=deps if k == 0 else ())
                for k in range(KT):
                    xctb[k].wrote(xl)
                xcdb.read(xl)
                for hi, h in enumerate(heads):
                    bk_k, bk_v, bk_ig, bk_q = fb[0], fb[1], fb[2], fb[3]
                    for (bk, nm, rhs_t, rhs_b) in (
                            (bk_k, "k", hbt, hbtb), (bk_v, "v", hbt, hbtb),
                            (bk_ig, "ig", xct, xctb), (bk_q, "q", hbt, hbtb)):
                        slot = hi * 4 + {"k": 0, "v": 1, "ig": 2, "q": 3}[nm]
                        for k in range(KT):
                            p.bop("pe", lambda e, k=k, w=wsl[slot], o=bk.ap,
                                  r_=rhs_t[k]: e.matmul(
                                      o[:, 0:512], w[:, k * 128:(k + 1) * 128],
                                      r_[:, :], start=(k == 0),
                                      stop=(k == KT - 1)),
                                  reads=[wslb[slot], rhs_b[k]], writes=[bk])
                    b_l2(bk_k, kbt, kbtb, True)
                    b_vln(bk_v, vft, vftb)
                    p.bop("act", lambda e, d=igt, i=bk_ig.ap, h=h:
                          e.activation(d[:, :], i[:, 0:512], AF.Sigmoid,
                                       bias=csb["c_bigb"][:, h:h + 1]),
                          reads=[bk_ig], writes=[igtb])
                    b_l2(bk_q, stbf[3], stbfb[3], True)
                    qt_ = stbfb[3].writer
                    tg = p.dma("sync", qbd[h, :, tc * 512:(tc + 1) * 512],
                               stbf[3][:, :], "qbst",
                               deps=[qt_] + qbdb.need_for_write())
                    stbfb[3].read(tg)
                    qbdb.wrote(tg)
                    p.bop("dve", lambda e, d=stg[2], a=igt, b_=kbt:
                          e.tensor_tensor(d[:, :], a[:, :], b_[:, :], OP.mult),
                          reads=[igtb, kbtb], writes=[stgb[2]])
                    ut_ = p.bop("dve", lambda e, d=stbf[2], a=stg[2], b_=vft:
                                e.tensor_tensor(d[:, :], a[:, :], b_[:, :],
                                                OP.mult),
                                reads=[stgb[2], vftb], writes=[stbfb[2]])
                    tg = p.dma("sync", umd[h, :, tc * 512:(tc + 1) * 512],
                               stbf[2][:, :], "umst",
                               deps=[ut_] + umdb.need_for_write())
                    stbfb[2].read(tg)
                    umdb.wrote(tg)
        p.bop("dve", lambda e: e.memset(carry[:, :], 0.0), writes=[carryb])
        for nb in range(NB_):
            for h in range(NH):
                ul = p.dma("sync", ufm[:, :], umd[h, :, nb * 128:(nb + 1) * 128],
                           "ufm", deps=ufmb.need_for_write() +
                           umdb.need_for_read())
                ufmb.wrote(ul)
                umdb.read(ul)
                bkg = fb[2]
                p.bop("pe", lambda e, nb=nb, h=h, o=bkg.ap: e.matmul(
                    o[:, 0:1], csb["c_onesf"][0:1, :],
                    ctn[0:1, nb * 16 + h:nb * 16 + h + 1],
                    start=True, stop=True), reads=[ctnb], writes=[bkg])
                p.bop("act", lambda e, i=bkg.ap: e.activation(
                    gcolt[:, 0:1], i[:, 0:1], AF.Exp, scale=-1.0),
                    reads=[bkg], writes=[gcoltb])
                p.bop("dve", lambda e, h=h: e.scalar_tensor_tensor(
                    ufm[:, 0:1], carry[:, h:h + 1], gcolt[:, 0:1],
                    ufm[:, 0:1], OP.mult, OP.add),
                    reads=[carryb, gcoltb, ufmb], writes=[ufmb])
                xbk = xb[nb % 2]
                p.bop("pe", lambda e, o=xbk.ap: e.transpose(
                    o[:, 0:128], ufm[:, :], c_id[:, :]),
                    reads=[ufmb], writes=[xbk])
                p.bop("dve", lambda e, i=xbk.ap: e.tensor_copy(
                    uts[:, :], i[:, 0:128]), reads=[xbk], writes=[utsb])
                bkc = fb[4 + nb % 2]
                p.bop("pe", lambda e, nb=nb, h=h, o=bkc.ap: e.matmul(
                    o[:, 0:128], csb["c_ohf"][0:16, h * 128:(h + 1) * 128],
                    cblk[0:16, nb * 128:(nb + 1) * 128],
                    start=True, stop=True), reads=[cblkb], writes=[bkc])
                zt_, ztb_ = stg[2], stgb[2]
                p.bop("dve", lambda e, i=bkc.ap, d=zt_: e.tensor_tensor(
                    d[:, 0:128], i[:, 0:128], csb["c_trin"][:, :], OP.add),
                    reads=[bkc], writes=[ztb_])
                p.bop("act", lambda e, nb=nb, h=h, s=zt_: e.activation(
                    wt_[:, :], s[:, 0:128], AF.Exp,
                    bias=ctn[:, nb * 16 + h:nb * 16 + h + 1]),
                    reads=[ztb_, ctnb], writes=[wtb])
                bkm = fb[nb % 2]
                p.bop("pe", lambda e, o=bkm.ap: e.matmul(
                    o[:, 0:128], uts[:, :], wt_[:, :], start=True, stop=True),
                    reads=[utsb, wtb], writes=[bkm])
                mt_ = p.bop("dve", lambda e, i=bkm.ap: e.tensor_copy(
                    mst[:, :], i[:, 0:128]), reads=[bkm], writes=[mstb])
                p.bop("dve", lambda e, h=h: e.tensor_copy(
                    carry[:, h:h + 1], mst[:, 127:128]),
                    reads=[mstb], writes=[carryb])
                tg = p.dma("sync", memd[h, :, nb * 128:(nb + 1) * 128],
                           mst[:, :], "memst",
                           deps=[mt_] + memdb.need_for_write())
                mstb.read(tg)
                memdb.wrote(tg)
        tg = p.dma("sync", cc2i[:, :], carry[:, :], "cc2s",
                   deps=carryb.need_for_read() + cc2ib.need_for_write())
        carryb.read(tg)
        cc2ib.wrote(tg)
    p.barrier()

    cct2 = collective(cc2i, cc2o)
    p.barrier()

    with ExitStack() as b2_es:
        sb2 = lambda n, s, d=F32: b2_es.enter_context(nc.sbuf_tensor(n, s, d))
        alloc_generic(b2_es, nws=2, n_xt=0, with_hbt=False,
                      rows_len=64)
        statei = sb2("statei", [128, 16])
        stateib = Buf(statei)
        afl2 = sb2("afl2", [16, T])
        afl2b = Buf(afl2)
        ogs = [sb2(f"ogs{k}", [128, 512], BF16) for k in range(KT)]
        ogsb = [Buf(t) for t in ogs]
        xct2 = [sb2(f"xct2{k}", [128, 512], BF16) for k in range(KT)]
        xct2b = [Buf(t) for t in xct2]
        mg = sb2("mg", [128, 512])
        mgb = Buf(mg)
        m2 = sb2("m2", [128, 512])
        m2b = Buf(m2)
        qbt2 = sb2("qbt2", [128, 512], BF16)
        qbt2b = Buf(qbt2)
        tg = p.dma("sync", statei[:, :], cc2o[0:128, :], "cc2l", deps=[cct2])
        stateib.wrote(tg)
        p.bop("dve", lambda e: e.tensor_tensor(
            statei[:, :], statei[:, :], csb["c_mask"][:, 0:16], OP.mult),
            reads=[stateib], writes=[stateib])
        tg = p.dma("sync", afl2[0:16, :], afd[0:16, :], "afl2",
                   deps=afdb.need_for_read())
        afl2b.wrote(tg)
        afdb.read(tg)

        def dd_norm(src_t, src_b, wcol_ap, bcol_ap, out_t, out_b, out_bf16):
            sq_, sqb_ = stbf[0], stbfb[0]
            p.bop("act", lambda e, d=sq_, s=src_t: e.activation(
                d[:, :], s[:, :], AF.Square), reads=[src_b], writes=[sqb_])
            bs = fb[4]
            p.bop("pe", lambda e, s=src_t, o=bs.ap: e.matmul(
                o[0:1, 0:512], csb["c_onesc"][:, 0:1], s[:, :],
                start=True, stop=True), reads=[src_b], writes=[bs])
            p.bop("dve", lambda e, o=rw(0), i=bs.ap: e.tensor_copy(
                o, i[0:1, 0:512]), reads=[bs], writes=[rwsb])
            p.bop("pe", lambda e, d=sq_, o=bs.ap: e.matmul(
                o[0:1, 0:512], c_ones[:, 0:1], d[:, :], start=True, stop=True),
                reads=[sqb_], writes=[bs])
            p.bop("dve", lambda e, o=rw(1), i=bs.ap: e.tensor_copy(
                o, i[0:1, 0:512]), reads=[bs], writes=[rwsb])
            p.bop("dve", lambda e, o=rw(2), i=rw(0): e.tensor_scalar_mul(
                o, i, 1.0 / 128), reads=[rwsb], writes=[rwsb])
            p.bop("dve", lambda e, o=rw(3), i=rw(2): e.tensor_mul(o, i, i),
                  reads=[rwsb], writes=[rwsb])
            p.bop("dve", lambda e, o=rw(4), i=rw(1): e.tensor_scalar(
                o, i, 1.0 / 128, None, OP.mult), reads=[rwsb], writes=[rwsb])
            p.bop("dve", lambda e, o=rw(4), i=rw(3): e.tensor_sub(o, o, i),
                  reads=[rwsb], writes=[rwsb])
            p.bop("act", lambda e, o=rw(4): e.activation(
                o, o, AF.Ln, bias=csb["c_cvb"][0:1, 0:1]),
                reads=[rwsb], writes=[rwsb])
            p.bop("act", lambda e, o=rw(4): e.activation(
                o, o, AF.Exp, scale=-0.5), reads=[rwsb], writes=[rwsb])
            p.bop("dve", lambda e, o=rw(5), a=rw(2), b_=rw(4): e.tensor_mul(
                o, a, b_), reads=[rwsb], writes=[rwsb])
            p.bop("dve", lambda e, o=rw(5), i=rw(5): e.tensor_scalar_mul(
                o, i, -1.0), reads=[rwsb], writes=[rwsb])
            b1_ = fb[2]
            b2_ = fb[3]
            p.bop("pe", lambda e, o=b1_.ap, i=rw(4): e.matmul(
                o[:, 0:512], csb["c_onesf"][0:1, :], i, start=True, stop=True),
                reads=[rwsb], writes=[b1_])
            p.bop("pe", lambda e, o=b2_.ap, i=rw(5): e.matmul(
                o[:, 0:512], csb["c_onesf"][0:1, :], i, start=True, stop=True),
                reads=[rwsb], writes=[b2_])
            s1_, s1b_ = stg[2], stgb[2]
            p.bop("dve", lambda e, d=s1_, i=b1_.ap: e.tensor_copy(
                d[:, :], i[:, 0:512]), reads=[b1_], writes=[s1b_])
            s3_, s3b_ = stg[3], stgb[3]
            p.bop("dve", lambda e, d=s3_, s=src_t, r_=s1_: e.tensor_tensor(
                d[:, :], s[:, :], r_[:, :], OP.mult),
                reads=[src_b, s1b_], writes=[s3b_])
            p.bop("dve", lambda e, d=s3_, i=b2_.ap: e.tensor_tensor(
                d[:, :], d[:, :], i[:, 0:512], OP.add),
                reads=[b2_, s3b_], writes=[s3b_])
            p.bop("dve", lambda e, d=out_t, s=s3_, w=wcol_ap, b_=bcol_ap:
                  e.tensor_scalar(d[:, :], s[:, :], w, b_, OP.mult, OP.add),
                  reads=[s3b_], writes=[out_b])

        for tc in range(TC):
            deps = list(xcdb.need_for_read())
            for k in range(KT):
                deps += xct2b[k].need_for_write()
            for k in range(KT):
                xl = p.dma("sync", xct2[k][:, :],
                           xcd[k, :, tc * 512:(tc + 1) * 512], "xct2",
                           deps=deps if k == 0 else ())
            for k in range(KT):
                xct2b[k].wrote(xl)
            xcdb.read(xl)
            for h in range(NH):
                ml = p.dma("sync", mg[:, :], memd[h, :, tc * 512:(tc + 1) * 512],
                           "mgl", deps=mgb.need_for_write() +
                           memdb.need_for_read())
                mgb.wrote(ml)
                memdb.read(ml)
                ba = fb[5]
                p.bop("pe", lambda e, h=h, tc=tc, o=ba.ap: e.matmul(
                    o[:, 0:512], csb["c_ohf"][0:16, h * 128:(h + 1) * 128],
                    afl2[0:16, tc * 512:(tc + 1) * 512],
                    start=True, stop=True), reads=[afl2b], writes=[ba])
                p.bop("dve", lambda e, h=h, i=ba.ap: e.scalar_tensor_tensor(
                    mg[:, :], i[:, 0:512], statei[:, h:h + 1], mg[:, :],
                    OP.mult, OP.add), reads=[ba, stateib, mgb], writes=[mgb])
                dd_norm(mg, mgb, csb["c_mnw"][:, 0:1], csb["c_mnb"][:, 0:1],
                        m2, m2b, False)
                ql = p.dma("sync", qbt2[:, :], qbd[h, :, tc * 512:(tc + 1) * 512],
                           "qbl", deps=qbt2b.need_for_write() +
                           qbdb.need_for_read())
                qbt2b.wrote(ql)
                qbdb.read(ql)
                p.bop("dve", lambda e: e.tensor_tensor(
                    m2[:, :], m2[:, :], qbt2[:, :], OP.mult),
                    reads=[qbt2b, m2b], writes=[m2b])
                dd_norm(m2, m2b, csb["c_gnw"][:, h:h + 1],
                        csb["c_gnb"][:, h:h + 1], ogs[h], ogsb[h], True)
            for m_ in range(KT):
                slot = m_ % 2
                wt = p.dma("pool", wsl[slot][:, :], WB["wog"][m_], f"wsl{slot}",
                           deps=wslb[slot].need_for_write())
                wslb[slot].wrote(wt)
                bk = fb[m_ % 2]
                for k in range(KT):
                    p.bop("pe", lambda e, k=k, w=wsl[slot], o=bk.ap,
                          r_=xct2[k]: e.matmul(
                              o[:, 0:512], w[:, k * 128:(k + 1) * 128],
                              r_[:, :], start=(k == 0), stop=(k == KT - 1)),
                          reads=[wslb[slot], xct2b[k]], writes=[bk])
                g_, gb_ = stg[0], stgb[0]
                p.bop("act", lambda e, d=g_, i=bk.ap, m_=m_: e.activation(
                    d[:, :], i[:, 0:512], AF.Sigmoid,
                    bias=csb["c_bogb"][:, m_:m_ + 1]),
                    reads=[bk], writes=[gb_])
                p.bop("dve", lambda e, d=ogs[m_], g=g_: e.tensor_tensor(
                    d[:, :], d[:, :], g[:, :], OP.mult),
                    reads=[gb_, ogsb[m_]], writes=[ogsb[m_]])
            for mo in range(KT):
                slot = mo % 2
                wt = p.dma("pool", wsl[slot][:, :], WB["wob"][mo], f"wsl{slot}",
                           deps=wslb[slot].need_for_write())
                wslb[slot].wrote(wt)
                bk = fb[2 + mo % 2]
                for k in range(KT):
                    p.bop("pe", lambda e, k=k, w=wsl[slot], o=bk.ap,
                          r_=ogs[k]: e.matmul(
                              o[:, 0:512], w[:, k * 128:(k + 1) * 128],
                              r_[:, :], start=(k == 0), stop=(k == KT - 1)),
                          reads=[wslb[slot], ogsb[k]], writes=[bk])
                s1_, s1b_ = stg[1], stgb[1]
                lt = p.dma("sync", s1_[:, :], xbd[mo, :, tc * 512:(tc + 1) * 512],
                           "xbl", deps=s1b_.need_for_write() +
                           xbdb.need_for_read())
                s1b_.wrote(lt)
                xbdb.read(lt)
                s2_, s2b_ = stg[2], stgb[2]
                et = p.bop("dve", lambda e, d=s2_, a=s1_, i=bk.ap:
                           e.tensor_tensor(d[:, :], a[:, :], i[:, 0:512],
                                           OP.add), reads=[bk, s1b_],
                           writes=[s2b_])
                tg = p.dma("sync", hmid[mo, :, tc * 512:(tc + 1) * 512],
                           s2_[:, :], "hmst", deps=[et] +
                           hmidb.need_for_write())
                s2b_.read(tg)
                hmidb.wrote(tg)
    p.barrier()

    ffn_phase("b4", hmid, r3, r3b, FFM, wgate, wval, wout, True, False,
              0, 0, [r2], wcol=csb["c_l2w"], bcolbf=c_l2bbf, wcolbf=c_l2wbf)
    p.barrier()
    yb_ = Buf(y)
    ffn_phase("f2", r3, y, yb_, FFN, wq1d, wq2d, wq3d, False, True,
              0, FFN // 128, [])
    p.barrier()

    for s in list(p.cnt):
        if s in ("PE", "DV", "AC", "CC"):
            continue
        p.wait("sync", (s, p.cnt[s]))

    p.finish()
    p.es.close()
    return nc, dict(T=T, TC=TC, QC=QC, SK=SK, PK=PK, NB=NB, ST=ST)


def _tile_w(w):
    din, dout = w.shape
    kk, mm = din // 128, dout // 128
    return np.ascontiguousarray(
        w.reshape(kk, 128, mm, 128).transpose(2, 1, 0, 3).reshape(mm, 128, din))


def _col16(v):
    return np.ascontiguousarray(np.asarray(v).reshape(-1, 128).T)


def make_core_inputs(inputs, core, T=2048):
    b, half = core // 2, core % 2
    QC = T // 128
    SK = 2 * T // 512
    PK = T // 512
    x = np.asarray(inputs["L0_attn_a_x"])
    xo = np.ascontiguousarray(x[b, half * T:(half + 1) * T].T)
    xp = np.ascontiguousarray(x[b, (half - 1) * T:half * T].T) if half == 1 else xo
    m = {
        "x_own": xo.reshape(KT, 128, T),
        "x_prev": xp.reshape(KT, 128, T),
        "wqa": _tile_w(inputs["L0_attn_a_Wq"]),
        "wka": _tile_w(inputs["L0_attn_a_Wk"]),
        "wva": _tile_w(inputs["L0_attn_a_Wv"]),
        "wpa": _tile_w(inputs["L0_attn_a_Wproj"]),
        "wqb": _tile_w(inputs["L0_attn_b_Wq"]),
        "wkb": _tile_w(inputs["L0_attn_b_Wk"]),
        "wvb": _tile_w(inputs["L0_attn_b_Wv"]),
        "wob": _tile_w(inputs["L0_attn_b_Wo"]),
        "wig": _tile_w(inputs["L0_attn_b_Wig"]),
        "wog": _tile_w(inputs["L0_attn_b_Wog"]),
        "wgate": _tile_w(inputs["L0_attn_b_Wgate"]),
        "wval": _tile_w(inputs["L0_attn_b_Wval"]),
        "wout": _tile_w(inputs["L0_attn_b_Wout"]),
        "fw1": _tile_w(np.ascontiguousarray(inputs["L0_ffn_w1"].T)),
        "fw2": _tile_w(np.ascontiguousarray(inputs["L0_ffn_w2"].T)),
        "fw3": _tile_w(np.ascontiguousarray(inputs["L0_ffn_w3"].T)),
        "wgg": np.ascontiguousarray(
            inputs["L0_attn_b_Wg"].reshape(KT, 128, NH)),
        "qrw": inputs["L0_attn_a_q_rms_w"].reshape(128, 1),
        "krw": inputs["L0_attn_a_k_rms_w"].reshape(128, 1),
        "slw": inputs["L0_attn_a_subln_w"].reshape(128, 1),
        "vnw": inputs["L0_attn_b_vnw"].reshape(128, 1),
        "vnb": inputs["L0_attn_b_vnb"].reshape(128, 1),
        "mnw": inputs["L0_attn_b_mnw"].reshape(128, 1),
        "mnb": inputs["L0_attn_b_mnb"].reshape(128, 1),
        "cw": np.ascontiguousarray(
            inputs["L0_attn_b_conv_w"][:, 0, :].reshape(KT, 128, 4)
            .transpose(1, 0, 2).reshape(128, KT * 4)),
        "cb": _col16(inputs["L0_attn_b_conv_b"]),
        "bigb": _col16(inputs["L0_attn_b_big"]),
        "bogb": _col16(inputs["L0_attn_b_bog"]),
        "bgb": np.ascontiguousarray(
            np.broadcast_to(inputs["L0_attn_b_bg"][None, :], (128, NH))),
        "gnw": _col16(inputs["L0_attn_b_gnw"]),
        "gnb": _col16(inputs["L0_attn_b_gnb"]),
        "l1w": _col16(inputs["L0_attn_b_ln1w"]),
        "l1b": _col16(inputs["L0_attn_b_ln1b"]),
        "l2w": _col16(inputs["L0_attn_b_ln2w"]),
        "l2b": _col16(inputs["L0_attn_b_ln2b"]),
    }
    m = {k: np.ascontiguousarray(v, dtype=np.float32) for k, v in m.items()}

    sl = alibi_slopes(NH)
    PK_ = T // 512
    QG = (T // 128) // 4
    CL = [4 * (PK_ + g + 1) * 512 for g in range(QG)]
    s_glob = np.concatenate([np.arange(T) + (half - 1) * T,
                             np.arange(T) + half * T]).astype(np.float64)
    if half == 0:
        s_glob[:T] = -1.0e12
    cstr = np.zeros((128, sum(CL)), np.float32)
    pp = np.arange(128)
    off = 0
    for g in range(QG):
        nk = PK_ + g + 1
        for qi in range(4):
            qc = g * 4 + qi
            tg_ = half * T + qc * 128 + pp
            for kc in range(nk):
                sg_ = s_glob[kc * 512:(kc + 1) * 512]
                dlt = tg_[:, None] - sg_[None, :]
                cc = np.where(dlt >= 0, dlt, 1.0e9)
                cstr[:, off + (qi * nk + kc) * 512:
                     off + (qi * nk + kc + 1) * 512] = cc
        off += CL[g]
    m["cstrd"] = cstr
    oh = np.zeros((16, NH * 128), np.float32)
    for h in range(NH):
        oh[h, h * 128:(h + 1) * 128] = 1.0
    m["onesbf"] = np.ones((128, 128), BF)
    m["onesf"] = np.ones((1, 128), np.float32)
    m["onesc"] = np.ones((128, 1), np.float32)
    m["onehotf"] = oh
    m["identf"] = np.eye(128, dtype=np.float32)
    s_ = np.arange(128)
    m["trineg"] = np.where(s_[:, None] <= s_[None, :], 0.0, NEG).astype(np.float32)
    cv = np.zeros(8, np.float32)
    cv[0], cv[1], cv[2] = 1e-5, 1e-12, 1e-8
    m["cvb"] = np.ascontiguousarray(np.broadcast_to(cv[None, :], (128, 8)))
    m["ident"] = np.eye(128).astype(BF)
    m["maskprev"] = np.full((128, 64), float(half), np.float32)
    return m


_CACHE = {}


def kernel(**inputs):
    T = 2048
    n_cores = 8
    if "nc" not in _CACHE:
        _CACHE["nc"], _CACHE["meta"] = build(T, n_cores)
    nc = _CACHE["nc"]
    in_maps = [make_core_inputs(inputs, c, T) for c in range(n_cores)]
    res = run_bass_kernel_spmd(nc, in_maps, core_ids=list(range(n_cores)))
    out = np.empty((4, 4096, 2048), np.float32)
    for c in range(n_cores):
        b, half = c // 2, c % 2
        yc = res.results[c]["y"].reshape(D, T)
        out[b, half * T:(half + 1) * T] = yc.T
    return out


# version 03
# speedup vs baseline: 1.1664x; 1.1664x over previous
import math
import numpy as np
import ml_dtypes
import concourse.bass as bass
import concourse.mybir as mybir
from concourse.bass_utils import run_bass_kernel_spmd
from contextlib import ExitStack

F32 = mybir.dt.float32
BF16 = mybir.dt.bfloat16
OP = mybir.AluOpType
AF = mybir.ActivationFunctionType
AX = mybir.AxisListType.X

D = 2048
KT = 16
NH = 16
HD = 128
FFN = 5632
FFM = 8192
EPS = 1e-5
NEG = -1.0e30
BF = ml_dtypes.bfloat16


def alibi_slopes(n=16):
    start = 2 ** (-(2 ** -(math.log2(n) - 3)))
    return np.array([start * start ** i for i in range(n)], dtype=np.float64)


ENGS = ("pe", "dve", "act", "sync", "pool")
ESEM = {"pe": "PE", "dve": "DV", "act": "AC", "sync": "SQ", "pool": "GQ"}


class Prog:
    def __init__(self, nc):
        self.nc = nc
        self.streams = {e: [] for e in ENGS}
        self.es = ExitStack()
        self.sem = {}
        self.cnt = {}
        for name in ("SQ", "GQ", "PE", "DV", "AC", "CC"):
            self.sem[name] = self.es.enter_context(nc.semaphore(name))
            self.cnt[name] = 0
        self.nops = 0
        self.waited = {e: {} for e in ENGS}

    def raw(self, eng, fn):
        self.streams[eng].append(fn)
        self.nops += 1

    def op(self, eng, fn, inc=True, inc_n=None):
        if not inc:
            self.raw(eng, fn)
            return (ESEM[eng], self.cnt[ESEM[eng]])
        s = ESEM[eng]
        n = inc_n if inc_n is not None else (16 if eng in ("sync", "pool") else 1)
        self.cnt[s] += n
        h = self.sem[s]
        self.raw(eng, lambda e, fn=fn, h=h, n=n: fn(e).then_inc(h, n))
        return (s, self.cnt[s])

    def bop(self, eng, fn, reads=(), writes=(), inc=True):
        deps = []
        for b in reads:
            deps += b.need_for_read()
        for b in writes:
            deps += b.need_for_write()
        self.waits(eng, deps)
        tag = self.op(eng, fn, inc=inc)
        for b in reads:
            b.read(tag)
        for b in writes:
            b.wrote(tag)
        return tag

    def wait(self, eng, tag):
        if tag is None:
            return
        s, t = tag
        if t <= 0:
            return
        assert t <= self.cnt[s], f"future wait {s}:{t} > {self.cnt[s]}"
        if self.waited[eng].get(s, 0) >= t:
            return
        self.waited[eng][s] = t
        h = self.sem[s]
        self.raw(eng, lambda e, h=h, t=t: e.wait_ge(h, t))

    def waits(self, eng, tags):
        best = {}
        for tg in tags:
            if tg is None:
                continue
            s, t = tg
            if t <= 0:
                continue
            best[s] = max(best.get(s, 0), t)
        for s, t in best.items():
            self.wait(eng, (s, t))

    def tag(self, eng):
        return (ESEM[eng], self.cnt[ESEM[eng]])

    def dsem(self, key):
        if key not in self.sem:
            self.sem[key] = self.es.enter_context(self.nc.semaphore(key))
            self.cnt[key] = 0
        return key

    def dma(self, eng, dst, src, key, deps=()):
        self.waits(eng, deps)
        k = self.dsem(key)
        self.cnt[k] += 16
        h = self.sem[k]
        self.raw(eng, lambda e, d=dst, s=src, h=h: e.dma_start(d, s).then_inc(h, 16))
        return (k, self.cnt[k])

    def sb(self, name, shape, dtype=F32):
        return self.es.enter_context(self.nc.sbuf_tensor(name, shape, dtype))

    def barrier(self):
        tags = [(s, self.cnt[s]) for s in self.cnt]
        for e in ENGS:
            self.waits(e, tags)

    def finish(self):
        nc = self.nc
        with nc.Block() as block:
            @block.tensor
            def _(t):
                for fn in self.streams["pe"]:
                    fn(t)

            @block.vector
            def _(v):
                for fn in self.streams["dve"]:
                    fn(v)

            @block.scalar
            def _(s):
                for fn in self.streams["act"]:
                    fn(s)

            @block.sync
            def _(sy):
                for fn in self.streams["sync"]:
                    fn(sy)

            @block.gpsimd
            def _(g):
                for fn in self.streams["pool"]:
                    fn(g)


class Buf:

    def __init__(self, ap):
        self.ap = ap
        self.writer = None
        self.readers = []

    def need_for_write(self):
        return ([self.writer] if self.writer else []) + self.readers

    def need_for_read(self):
        return [self.writer] if self.writer else []

    def wrote(self, tag):
        self.writer = tag
        self.readers = []

    def read(self, tag):
        for i, (s, t) in enumerate(self.readers):
            if s == tag[0]:
                self.readers[i] = (s, max(t, tag[1]))
                return
        self.readers.append(tag)


def build(T=2048, n_cores=8):
    groups = [[2 * i, 2 * i + 1] for i in range(n_cores // 2)]
    TC = T // 512
    QC = T // 128
    SK = 2 * T // 512
    PK = T // 512
    NB = T // 128
    ST = 2 * T // 128

    nc = bass.Bass("TRN2", target_bir_lowering=False)
    p = Prog(nc)

    def di(name, shape, dtype=F32):
        return nc.dram_tensor(name, list(shape), dtype, kind="ExternalInput").ap()

    x_own = di("x_own", (KT, 128, T))
    x_prev = di("x_prev", (KT, 128, T))
    WA = {n: di(n, (KT, 128, D)) for n in ("wqa", "wka", "wva", "wpa")}
    WB = {n: di(n, (KT, 128, D)) for n in ("wqb", "wkb", "wvb", "wob", "wig", "wog")}
    wgg = di("wgg", (KT, 128, NH))
    wgate = di("wgate", (FFM // 128, 128, D))
    wval = di("wval", (FFM // 128, 128, D))
    wout = di("wout", (KT, 128, FFM))
    fw1 = di("fw1", (FFN // 128, 128, D))
    fw2 = di("fw2", (FFN // 128, 128, D))
    fw3 = di("fw3", (KT, 128, FFN))
    smalls = {}
    for n, w in [("qrw", 1), ("krw", 1), ("slw", 1), ("vnw", 1), ("vnb", 1),
                 ("mnw", 1), ("mnb", 1), ("cw", KT * 4), ("cb", KT),
                 ("bigb", KT), ("bogb", KT), ("bgb", NH), ("gnw", KT),
                 ("gnb", KT), ("l1w", KT), ("l1b", KT), ("l2w", KT),
                 ("l2b", KT)]:
        smalls[n] = di(n, (128, w))
    CL = [4 * (PK + g + 1) * 512 for g in range(QC // 4)]
    CO = [sum(CL[:g]) for g in range(QC // 4)]
    cstrd = di("cstrd", (128, sum(CL)))
    onesbf = di("onesbf", (128, 128), BF16)
    onesf = di("onesf", (1, 128))
    onesc = di("onesc", (128, 1))
    onehotf = di("onehotf", (16, NH * 128))
    identf = di("identf", (128, 128))
    trineg = di("trineg", (128, 128))
    cvb = di("cvb", (128, 8))
    ident = di("ident", (128, 128), BF16)
    maskprev = di("maskprev", (128, 64))

    y = nc.dram_tensor("y", [KT, 128, T], F32, kind="ExternalOutput").ap()

    def dsc(name, shape, dtype=F32):
        ap = nc.dram_tensor(name, list(shape), dtype).ap()
        return ap, Buf(ap)

    r1, r1b = dsc("r1", (KT, 128, T))
    r2, r2b = dsc("r2", (KT, 128, T))
    r3, r3b = dsc("r3", (KT, 128, T))
    xbd, xbdb = dsc("xbd", (KT, 128, T))
    hmid, hmidb = dsc("hmid", (KT, 128, T))
    accd, accdb = dsc("accd", (KT, 128, T))
    xbfd, xbfdb = dsc("xbfd", (KT, 128, 2 * T), BF16)
    hbd, hbdb = dsc("hbd", (KT, 128, T + 3), BF16)
    xcd, xcdb = dsc("xcd", (KT, 128, T), BF16)
    qad, qadb = dsc("qad", (NH, 128, T), BF16)
    kad, kadb = dsc("kad", (NH, 128, 2 * T), BF16)
    vtd, vtdb = dsc("vtd", (NH, 128, ST * 128), BF16)
    yad, yadb = dsc("yad", (KT, 128, T), BF16)
    qbd, qbdb = dsc("qbd", (NH, 128, T), BF16)
    umd, umdb = dsc("umd", (NH, 128, T), BF16)
    memd, memdb = dsc("memd", (NH, 128, T))
    wq1d, wq1db = dsc("wq1d", (FFN // 128, 128, D), BF16)
    wq2d, wq2db = dsc("wq2d", (FFN // 128, 128, D), BF16)
    wq3d, wq3db = dsc("wq3d", (KT, 128, FFN), BF16)
    csd, csdb = dsc("csd", (1, 128 * (2 * (FFN // 128) + KT)), BF16)
    cc1i, cc1ib = dsc("cc1i", (128, 64))
    cc1o, cc1ob = dsc("cc1o", (2 * 128, 64))
    cc2i, cc2ib = dsc("cc2i", (128, 16))
    cc2o, cc2ob = dsc("cc2o", (2 * 128, 16))

    csb = {}
    creq = [
            ("c_ones", onesbf, [128, 128], BF16),
            ("c_onesf", onesf, [1, 128], F32),
            ("c_onesc", onesc, [128, 1], F32),
            ("c_ohf", onehotf, [16, NH * 128], F32),
            ("c_idf", identf, [128, 128], F32),
            ("c_trin", trineg, [128, 128], F32),
            ("c_cvb", cvb, [128, 8], F32),
            ("c_id", ident, [128, 128], BF16),
            ("c_mask", maskprev, [128, 64], F32)]
    for n, src in smalls.items():
        creq.append(("c_" + n, src, list(src.shape), F32))
    for name, src, shape, dt in creq:
        t = p.sb(name, shape, dt)
        ctag = p.dma("sync", t[:, :] if len(shape) == 2 else t[:], src, "cst")
        csb[name] = t
    const_tag = ctag
    c_ones = csb["c_ones"]
    c_id = csb["c_id"]
    c_qrws = p.sb("c_qrws", [128, 1])
    p.wait("dve", const_tag)
    qrws_tag = p.op("dve", lambda e: e.tensor_scalar_mul(
        c_qrws[:, :], csb["c_qrw"][:, :], 2 ** -3.5), inc=True)
    c_qrws_b = Buf(c_qrws)
    c_qrws_b.wrote(qrws_tag)
    c_l2wbf = p.sb("c_l2wbf", [128, KT], BF16)
    c_l2bbf = p.sb("c_l2bbf", [128, KT], BF16)
    p.wait("dve", const_tag)
    t_ = p.op("dve", lambda e: e.tensor_copy(c_l2wbf[:, :], csb["c_l2w"][:, :]))
    t_ = p.op("dve", lambda e: e.tensor_copy(c_l2bbf[:, :], csb["c_l2b"][:, :]))
    c_l2wbf_b = Buf(c_l2wbf)
    c_l2wbf_b.wrote(t_)
    c_l2bbf_b = Buf(c_l2bbf)
    c_l2bbf_b.wrote(t_)

    fb = [Buf(p.es.enter_context(nc.psum_tensor(f"pb{i}", [128, 512], F32)).ap())
          for i in range(6)]
    xb = [Buf(p.es.enter_context(nc.psum_tensor(f"px{i}", [128, 1024], BF16)).ap())
          for i in range(2)]

    NWS = 12
    xt, xtb = [None] * KT, [None] * KT
    hbt, hbtb = [None] * KT, [None] * KT
    wsl, wslb = [None] * NWS, [None] * NWS
    stg, stgb = [None] * 4, [None] * 4
    stbf, stbfb = [None] * 4, [None] * 4
    rws = [None]
    rwsb = Buf(None)
    lnr, lnm = [None], [None]
    lnrb, lnmb = Buf(None), Buf(None)
    _gen_ctr = [0]

    def alloc_generic(es, nws=NWS, n_xt=KT, with_hbt=True, rows_len=None):
        g = _gen_ctr[0]
        _gen_ctr[0] += 1
        sbx = lambda n, s, d=F32: es.enter_context(
            nc.sbuf_tensor(f"{n}_g{g}", s, d))
        for k in range(KT):
            if k < n_xt:
                xt[k] = sbx(f"xt{k}", [128, 512])
                xtb[k] = Buf(xt[k])
            else:
                xt[k], xtb[k] = None, None
            if with_hbt:
                hbt[k] = sbx(f"hbt{k}", [128, 512], BF16)
                hbtb[k] = Buf(hbt[k])
            else:
                hbt[k], hbtb[k] = None, None
        for i in range(nws):
            wsl[i] = sbx(f"wsl{i}", [128, D], BF16)
            wslb[i] = Buf(wsl[i])
        for i in range(nws, NWS):
            wsl[i] = None
            wslb[i] = None
        for i in range(4):
            stg[i] = sbx(f"stg{i}", [128, 512])
            stgb[i] = Buf(stg[i])
            stbf[i] = sbx(f"stbf{i}", [128, 512], BF16)
            stbfb[i] = Buf(stbf[i])
        rws[0] = sbx("rws", [1, 8 * 512])
        rwsb.__init__(None)
        rl = rows_len if rows_len is not None else 2 * T
        lnr[0] = sbx("lnr", [1, rl])
        lnm[0] = sbx("lnm", [1, rl], BF16)
        lnrb.__init__(None)
        lnmb.__init__(None)

    colrow = p.sb("colrow", [1, 128 * 12], BF16)
    colrowb = Buf(colrow)
    NC2 = 2 * (FFN // 128) + KT

    slopes = alibi_slopes(NH)

    def evac(eng, bank, fn, dstbuf=None):
        p.waits(eng, bank.need_for_read())
        tag = p.op(eng, fn, inc=True)
        bank.read(tag)
        if dstbuf is not None:
            dstbuf.wrote(tag)
        return tag

    def rw(slot):
        return rws[0][0:1, slot * 512:(slot + 1) * 512]

    def ln_fold_stream(roff, srcap, tc, dst, dstb, doff, wcol=None):
        b = fb[0]
        sqdone = []
        for k in range(KT):
            sl = k % 2
            xl = p.dma("sync", xt[sl][:, :], srcap[k, :, tc * 512:(tc + 1) * 512],
                       f"xts{sl}", deps=xtb[sl].need_for_write())
            xtb[sl].wrote(xl)
            p.bop("dve", lambda e, d=dst[k], s=xt[sl], co=doff: e.tensor_copy(
                d[:, co:co + 512], s[:, :]), reads=[xtb[sl]], writes=[dstb[k]])
            p.bop("pe", lambda e, s=dst[k], st=(k == 0), sp=(k == KT - 1),
                  co=doff, o=b.ap: e.matmul(o[0:1, 0:512], c_ones[:, 0:1],
                                            s[:, co:co + 512],
                                            start=st, stop=sp),
                  reads=[dstb[k]], writes=[b])
            s2_ = stbf[k % 2]
            s2b_ = stbfb[k % 2]
            p.bop("act", lambda e, d=s2_, i=dst[k], co=doff: e.activation(
                d[:, :], i[:, co:co + 512], AF.Square),
                reads=[dstb[k]], writes=[s2b_])
            p.bop("pe", lambda e, d=s2_, st=(k == 0), sp=(k == KT - 1),
                  o=fb[5].ap: e.matmul(o[0:1, 0:512], c_ones[:, 0:1], d[:, :],
                                       start=st, stop=sp),
                  reads=[s2b_], writes=[fb[5]])
            if wcol is not None:
                p.bop("dve", lambda e, d=dst[k], s=xt[sl], co=doff, k=k:
                      e.tensor_scalar(d[:, co:co + 512], s[:, :],
                                      wcol[:, k:k + 1], None, OP.mult),
                      reads=[xtb[sl]], writes=[dstb[k]])
        p.bop("dve", lambda e, o=rw(0), i=b.ap: e.tensor_copy(o, i[0:1, 0:512]),
              reads=[b], writes=[rwsb])
        p.bop("dve", lambda e, o=rw(1), i=fb[5].ap: e.tensor_copy(
            o, i[0:1, 0:512]), reads=[fb[5]], writes=[rwsb])
        _ln_rows(roff)

    def _ln_rows(roff):
        p.bop("dve", lambda e, o=rw(2), i=rw(0): e.tensor_scalar_mul(o, i, 1.0 / D),
              reads=[rwsb], writes=[rwsb])
        p.bop("dve", lambda e, o=rw(3), i=rw(2): e.tensor_mul(o, i, i),
              reads=[rwsb], writes=[rwsb])
        p.bop("dve", lambda e, o=rw(4), i=rw(1): e.tensor_scalar(
            o, i, 1.0 / D, None, OP.mult), reads=[rwsb], writes=[rwsb])
        p.bop("dve", lambda e, o=rw(4), i=rw(3): e.tensor_sub(o, o, i),
              reads=[rwsb], writes=[rwsb])
        p.bop("dve", lambda e, o=lnm[0][0:1, roff:roff + 512], i=rw(2):
              e.tensor_scalar_mul(o, i, -1.0), reads=[rwsb], writes=[lnmb])
        p.bop("act", lambda e, o=rw(4): e.activation(
            o, o, AF.Ln, bias=csb["c_cvb"][0:1, 0:1]),
            reads=[rwsb], writes=[rwsb])
        p.bop("act", lambda e, o=lnr[0][0:1, roff:roff + 512], i=rw(4):
              e.activation(o, i, AF.Exp, scale=-0.5), reads=[rwsb], writes=[lnrb])

    def ln_fold_chunk(roff, goff, store_bf=True, dst=None, dstb=None,
                      doff=0):
        if dst is None:
            ct, ctb, co = hbt, hbtb, 0
        else:
            ct, ctb, co = dst, dstb, doff
        for k in range(KT):
            p.bop("dve", lambda e, d=ct[k], s=xt[k], co=co: e.tensor_copy(
                d[:, co:co + 512], s[:, :]), reads=[xtb[k]], writes=[ctb[k]])
        if store_bf and dst is None:
            deps = list(xbfdb.need_for_write())
            for k in range(KT):
                deps += ctb[k].need_for_read()
            for k in range(KT):
                tg = p.dma("sync", xbfd[k, :, goff:goff + 512], ct[k][:, :],
                           "hbt_st", deps=deps if k == 0 else ())
            for k in range(KT):
                ctb[k].read(tg)
            xbfdb.wrote(tg)
        b = fb[0]
        for k in range(KT):
            p.bop("pe", lambda e, s=ct[k], st=(k == 0), sp=(k == KT - 1),
                  co=co, o=b.ap: e.matmul(o[0:1, 0:512], c_ones[:, 0:1],
                                          s[:, co:co + 512],
                                          start=st, stop=sp),
                  reads=[ctb[k]], writes=[b])
        p.bop("dve", lambda e, o=rw(0), i=b.ap: e.tensor_copy(o, i[0:1, 0:512]),
              reads=[b], writes=[rwsb])
        for k in range(KT):
            s = stbf[k % 2]
            sb_ = stbfb[k % 2]
            p.bop("act", lambda e, d=s, i=ct[k], co=co: e.activation(
                d[:, :], i[:, co:co + 512], AF.Square),
                reads=[ctb[k]], writes=[sb_])
            p.bop("pe", lambda e, d=s, st=(k == 0), sp=(k == KT - 1), o=b.ap:
                  e.matmul(o[0:1, 0:512], c_ones[:, 0:1], d[:, :],
                           start=st, stop=sp), reads=[sb_], writes=[b])
        p.bop("dve", lambda e, o=rw(1), i=b.ap: e.tensor_copy(o, i[0:1, 0:512]),
              reads=[b], writes=[rwsb])
        _ln_rows(roff)

    def colsum(wi, csl):
        b = fb[5]
        for k in range(KT):
            p.bop("pe", lambda e, k=k, w=wsl[wi]: e.matmul(
                b.ap[0:1, 0:128], c_ones[:, 0:1], w[:, k * 128:(k + 1) * 128],
                start=(k == 0), stop=(k == KT - 1)), reads=[wslb[wi]], writes=[b])
        return p.bop("dve", lambda e, o=colrow[0:1, csl * 128:(csl + 1) * 128]:
                     e.tensor_copy(o, b.ap[0:1, 0:128]),
                     reads=[b], writes=[colrowb])

    def phase_A():
        for sc in range(SK):
            is_own = sc >= PK
            srcx = x_own if is_own else x_prev
            t0 = (sc - PK) * 512 if is_own else sc * 512
            deps = []
            for k in range(KT):
                deps += xtb[k].need_for_write()
            for k in range(KT):
                xl = p.dma("sync", xt[k][:, :], srcx[k, :, t0:t0 + 512], "xt",
                           deps=deps if k == 0 else ())
            for k in range(KT):
                xtb[k].wrote(xl)
            ln_fold_chunk(sc * 512, sc * 512)
        for hg in range(NH // 4):
            heads = range(hg * 4, hg * 4 + 4)
            for hi, h in enumerate(heads):
                for wi2, (nm, wsrc) in enumerate([("k", WA["wka"]),
                                                  ("v", WA["wva"]),
                                                  ("q", WA["wqa"])]):
                    slot = hi * 3 + wi2
                    tg = p.dma("pool", wsl[slot][:, :], wsrc[h], f"wsl{slot}",
                               deps=wslb[slot].need_for_write())
                    wslb[slot].wrote(tg)
                    colsum(slot, slot)
            for sc in range(SK):
                is_own = sc >= PK
                roff = sc * 512
                deps = list(xbfdb.need_for_read())
                for k in range(KT):
                    deps += hbtb[k].need_for_write()
                for k in range(KT):
                    xbl = p.dma("sync", hbt[k][:, :],
                                xbfd[k, :, roff:roff + 512], "hbt",
                                deps=deps if k == 0 else ())
                for k in range(KT):
                    hbtb[k].wrote(xbl)
                xbfdb.read(xbl)
                for hi, h in enumerate(heads):
                    projs = [("k", hi * 3), ("v", hi * 3 + 1)]
                    if is_own:
                        projs.append(("q", hi * 3 + 2))
                    for nm, slot in projs:
                        bank = fb[3 if (h + sc) % 2 == 0 else 4]
                        for k in range(KT):
                            p.bop("pe", lambda e, k=k, o=bank.ap[:, 0:512],
                                  w=wsl[slot], r=hbt[k]: e.matmul(
                                      o, w[:, k * 128:(k + 1) * 128], r[:, :],
                                      start=(k == 0), stop=False),
                                  reads=[wslb[slot], hbtb[k]], writes=[bank])
                        p.bop("pe", lambda e, o=bank.ap[:, 0:512],
                              l=colrow[0:1, slot * 128:(slot + 1) * 128],
                              r=lnm[0][0:1, roff:roff + 512]: e.matmul(
                                  o, l, r, start=False, stop=True),
                              reads=[colrowb, lnmb], writes=[bank])
                        if nm == "v":
                            a_v_epilogue(h, sc, bank, roff)
                        else:
                            a_qk_epilogue(nm, h, sc, bank, roff)

    def a_qk_epilogue(nm, h, sc, bank, roff):
        sb_, sbb = stbf[2], stbfb[2]
        p.bop("act", lambda e, d=sb_, i=bank.ap: e.activation(
            d[:, :], i[:, 0:512], AF.Square), reads=[bank], writes=[sbb])
        b2 = fb[5]
        p.bop("pe", lambda e, d=sb_, o=b2.ap: e.matmul(
            o[0:1, 0:512], c_ones[:, 0:1], d[:, :], start=True, stop=True),
            reads=[sbb], writes=[b2])
        p.bop("dve", lambda e, o=rw(5), i=b2.ap: e.tensor_copy(o, i[0:1, 0:512]),
              reads=[b2], writes=[rwsb])
        p.bop("dve", lambda e, o=rw(6), r_=lnr[0][0:1, roff:roff + 512]:
              e.tensor_mul(o, r_, r_), reads=[lnrb, rwsb], writes=[rwsb])
        p.bop("dve", lambda e, o=rw(6), i=rw(5): e.tensor_mul(o, o, i),
              reads=[rwsb], writes=[rwsb])
        p.bop("act", lambda e, o=rw(6): e.activation(
            o, o, AF.Ln, bias=csb["c_cvb"][0:1, 0:1], scale=1.0 / 128),
            reads=[rwsb], writes=[rwsb])
        p.bop("act", lambda e, o=rw(6): e.activation(o, o, AF.Exp, scale=-0.5),
              reads=[rwsb], writes=[rwsb])
        p.bop("dve", lambda e, o=rw(6), r_=lnr[0][0:1, roff:roff + 512]:
              e.tensor_mul(o, o, r_), reads=[lnrb, rwsb], writes=[rwsb])
        p.bop("pe", lambda e, i=rw(6), o=b2.ap: e.matmul(
            o[:, 0:512], csb["c_onesf"][0:1, :], i, start=True, stop=True),
            reads=[rwsb], writes=[b2])
        sg, sgb = stg[2], stgb[2]
        p.bop("dve", lambda e, d=sg, i=b2.ap: e.tensor_copy(d[:, :], i[:, 0:512]),
              reads=[b2], writes=[sgb])
        wcol = c_qrws if nm == "q" else csb["c_krw"]
        wbuf_ = c_qrws_b if nm == "q" else None
        dst, dstb = stbf[3], stbfb[3]
        ft = p.bop("dve", lambda e, w=wcol, d=dst, s=sg, i=bank.ap:
                   e.scalar_tensor_tensor(
            d[:, :], i[:, 0:512], w[:, 0:1], s[:, :], OP.mult, OP.mult),
            reads=[bank, sgb] + ([wbuf_] if wbuf_ else []), writes=[dstb])
        if nm == "q":
            tg = p.dma("sync", qad[h, :, roff - PK * 512:roff - PK * 512 + 512],
                       dst[:, :], "stbf3", deps=[ft] + qadb.need_for_write())
            qadb.wrote(tg)
        else:
            tg = p.dma("sync", kad[h, :, roff:roff + 512], dst[:, :], "stbf3",
                       deps=[ft] + kadb.need_for_write())
            kadb.wrote(tg)
        dstb.read(tg)

    def a_v_epilogue(h, sc, bank, roff):
        b2 = fb[5]
        p.bop("pe", lambda e, r_=lnr[0][0:1, roff:roff + 512], o=b2.ap:
              e.matmul(o[:, 0:512], csb["c_onesf"][0:1, :], r_,
                       start=True, stop=True), reads=[lnrb], writes=[b2])
        sg, sgb = stg[2], stgb[2]
        p.bop("dve", lambda e, d=sg, i=b2.ap: e.tensor_copy(d[:, :], i[:, 0:512]),
              reads=[b2], writes=[sgb])
        dst, dstb = stbf[3], stbfb[3]
        p.bop("dve", lambda e, d=dst, s=sg, i=bank.ap: e.tensor_tensor(
            d[:, :], i[:, 0:512], s[:, :], OP.mult),
            reads=[bank, sgb], writes=[dstb])
        xbk = xb[sc % 2]
        for j in range(4):
            p.bop("pe", lambda e, j=j, d=dst, o=xbk.ap: e.transpose(
                o[:, j * 128:(j + 1) * 128], d[:, j * 128:(j + 1) * 128],
                c_id[:, :]), reads=[dstb], writes=[xbk])
        vs, vsb = stbf[2], stbfb[2]
        t = p.bop("dve", lambda e, d=vs, i=xbk.ap: e.tensor_copy(
            d[:, :], i[:, 0:512]), reads=[xbk], writes=[vsb])
        tg = p.dma("sync", vtd[h, :, sc * 512:(sc + 1) * 512], vs[:, :], "stbf2",
                   deps=[t] + vtdb.need_for_write())
        vsb.read(tg)
        vtdb.wrote(tg)

    with ExitStack() as g_es:
        alloc_generic(g_es)
        phase_A()
    p.barrier()

    with ExitStack() as a1_es:
        sba1 = lambda n, s, d=F32: a1_es.enter_context(nc.sbuf_tensor(n, s, d))
        ka_s = [sba1(f"ka{i}", [128, 2 * T], BF16) for i in range(2)]
        ka_b = [Buf(t) for t in ka_s]
        vt_s = [sba1(f"vt{i}", [128, ST * 128], BF16) for i in range(2)]
        vt_b = [Buf(t) for t in vt_s]
        qa_s = [sba1(f"qa{i}", [128, 512], BF16) for i in range(2)]
        qa_b = [Buf(t) for t in qa_s]
        NKX = PK + (QC // 4 - 1) + 1
        cstr = sba1("cstr", [128, 4 * NKX * 512])
        cstrb = Buf(cstr)
        zstr = sba1("zstr", [128, NKX * 512])
        zstrb = Buf(zstr)
        pstr = sba1("pstr", [128, 4 * NKX * 512], BF16)
        pstrb = Buf(pstr)
        ptb = sba1("ptb", [128, ST * 512], BF16)
        ptbb = Buf(ptb)
        mrow = sba1("mrow", [128, 32])
        mrowb = Buf(mrow)
        arow = sba1("arow", [1, 3 * 512])
        arowb = Buf(arow)
        ydst = sba1("ydst", [128, 512], BF16)
        ydstb = Buf(ydst)
        astg = [sba1(f"astg{i}", [128, 512]) for i in range(2)]
        astgb = [Buf(t) for t in astg]
        asbf = sba1("asbf", [128, 512], BF16)
        asbfb = Buf(asbf)

        def nkc_of(qc):
            return PK + qc // 4 + 1

        def phase_A1():
            for qcg in range(QC // 4):
                qcs = list(range(qcg * 4, qcg * 4 + 4))
                nkcg = PK + qcg + 1
                ct_ = p.dma("sync", cstr[:, 0:4 * nkcg * 512],
                            cstrd[:, CO[qcg]:CO[qcg] + CL[qcg]], "cstr",
                            deps=cstrb.need_for_write())
                cstrb.wrote(ct_)
                for h in range(NH):
                    par = h % 2
                    ktag = p.dma("sync", ka_s[par][:, :], kad[h], f"ka{par}",
                                 deps=ka_b[par].need_for_write() +
                                 kadb.need_for_read())
                    ka_b[par].wrote(ktag)
                    kadb.read(ktag)
                    vtag = p.dma("sync", vt_s[par][:, :], vtd[h], f"vt{par}",
                                 deps=vt_b[par].need_for_write() +
                                 vtdb.need_for_read())
                    vt_b[par].wrote(vtag)
                    vtdb.read(vtag)
                    qtag = p.dma("sync", qa_s[par][:, 0:512],
                                 qad[h, :, qcg * 512:(qcg + 1) * 512],
                                 f"qa{par}",
                                 deps=qa_b[par].need_for_write() +
                                 qadb.need_for_read())
                    qa_b[par].wrote(qtag)
                    qadb.read(qtag)
                    kab, vtb_, qab = ka_b[par], vt_b[par], qa_b[par]
                    kaa, vta, qaa = ka_s[par], vt_s[par], qa_s[par]
                    slope = float(slopes[h])
                    for qi, qc in enumerate(qcs):
                        nk = nkc_of(qc)
                        for kc in range(nk):
                            bk = fb[2 + kc % 2]
                            p.bop("pe", lambda e, qc=qc, kc=kc, o=bk.ap,
                                  qaa=qaa, kaa=kaa, qi=qi:
                                  e.matmul(o[:, 0:512],
                                           qaa[:, qi * 128:(qi + 1) * 128],
                                           kaa[:, kc * 512:(kc + 1) * 512],
                                           start=True, stop=True),
                                  reads=[qab, kab], writes=[bk])
                            p.bop("dve", lambda e, qi=qi, kc=kc, o=bk.ap,
                                  sl=slope, nk=nk: e.scalar_tensor_tensor(
                                      zstr[:, kc * 512:(kc + 1) * 512],
                                      cstr[:, (qi * nk + kc) * 512:
                                           (qi * nk + kc + 1) * 512],
                                      -sl, o[:, 0:512], OP.mult, OP.add),
                                  reads=[bk, cstrb], writes=[zstrb])
                        p.bop("dve", lambda e, nk=nk: e.reduce_max(
                            mrow[:, 0:1], zstr[:, 0:nk * 512], AX),
                            reads=[zstrb], writes=[mrowb])
                        p.bop("dve", lambda e: e.tensor_scalar_mul(
                            mrow[:, 1:2], mrow[:, 0:1], -1.0),
                            reads=[mrowb], writes=[mrowb])
                        poff = qi * nkcg * 512
                        p.bop("act", lambda e, nk=nk, po=poff: e.activation(
                            pstr[:, po:po + nk * 512], zstr[:, 0:nk * 512],
                            AF.Exp, bias=mrow[:, 1:2]),
                            reads=[zstrb, mrowb], writes=[pstrb])
                    nst_max = nkc_of(qcs[-1]) * 4
                    for st in range(nst_max):
                        xbk = xb[st % 2]
                        q0 = 0
                        while nkc_of(qcs[0] + q0) * 4 <= st:
                            q0 += 1
                        for qi in range(q0, 4):
                            poff = qi * nkcg * 512
                            p.bop("pe", lambda e, st=st, qi=qi, po=poff,
                                  o=xbk.ap: e.transpose(
                                      o[:, qi * 128:(qi + 1) * 128],
                                      pstr[:, po + st * 128:po + (st + 1) * 128],
                                      c_id[:, :]), reads=[pstrb], writes=[xbk])
                        p.bop("dve", lambda e, st=st, q0=q0, o=xbk.ap:
                              e.tensor_copy(
                                  ptb[:, st * 512 + q0 * 128:(st + 1) * 512],
                                  o[:, q0 * 128:512]), reads=[xbk], writes=[ptbb])
                    yk = fb[4]
                    sk_ = fb[5]
                    for st in range(nst_max):
                        q0 = 0
                        while nkc_of(qcs[0] + q0) * 4 <= st:
                            q0 += 1
                        p.bop("pe", lambda e, st=st, q0=q0, vta=vta:
                              e.matmul(yk.ap[:, q0 * 128:512],
                                       vta[:, st * 128:(st + 1) * 128],
                                       ptb[:, st * 512 + q0 * 128:(st + 1) * 512],
                                       start=(st == 0), stop=(st == nst_max - 1)),
                              reads=[vtb_, ptbb], writes=[yk])
                        p.bop("pe", lambda e, st=st, q0=q0:
                              e.matmul(sk_.ap[0:1, q0 * 128:512],
                                       c_ones[:, 0:1],
                                       ptb[:, st * 512 + q0 * 128:(st + 1) * 512],
                                       start=(st == 0), stop=(st == nst_max - 1)),
                              reads=[ptbb], writes=[sk_])
                    p.bop("dve", lambda e: e.tensor_copy(arow[0:1, 0:512],
                                                         sk_.ap[0:1, 0:512]),
                          reads=[sk_], writes=[arowb])
                    sq_, sqb_ = asbf, asbfb
                    p.bop("act", lambda e, sq_=sq_: e.activation(
                        sq_[:, :], yk.ap[:, 0:512], AF.Square),
                        reads=[yk], writes=[sqb_])
                    p.bop("pe", lambda e, sq_=sq_: e.matmul(
                        sk_.ap[0:1, 0:512], c_ones[:, 0:1], sq_[:, :],
                        start=True, stop=True), reads=[sqb_], writes=[sk_])
                    p.bop("dve", lambda e: e.tensor_copy(arow[0:1, 512:1024],
                                                         sk_.ap[0:1, 0:512]),
                          reads=[sk_], writes=[arowb])
                    p.bop("dve", lambda e: e.tensor_mul(
                        arow[0:1, 1024:1536], arow[0:1, 0:512], arow[0:1, 0:512]),
                        reads=[arowb], writes=[arowb])
                    p.bop("dve", lambda e: e.tensor_scalar(
                        arow[0:1, 1024:1536], arow[0:1, 1024:1536], EPS, None,
                        OP.mult), reads=[arowb], writes=[arowb])
                    p.bop("dve", lambda e: e.scalar_tensor_tensor(
                        arow[0:1, 1024:1536], arow[0:1, 512:1024], 1.0 / 128,
                        arow[0:1, 1024:1536], OP.mult, OP.add),
                        reads=[arowb], writes=[arowb])
                    p.bop("act", lambda e: e.activation(
                        arow[0:1, 1024:1536], arow[0:1, 1024:1536], AF.Ln),
                        reads=[arowb], writes=[arowb])
                    p.bop("act", lambda e: e.activation(
                        arow[0:1, 1024:1536], arow[0:1, 1024:1536], AF.Exp,
                        scale=-0.5), reads=[arowb], writes=[arowb])
                    fbk = fb[3]
                    p.bop("pe", lambda e: e.matmul(
                        fbk.ap[:, 0:512], csb["c_onesf"][0:1, :],
                        arow[0:1, 1024:1536], start=True, stop=True),
                        reads=[arowb], writes=[fbk])
                    sg2, sg2b = astg[0], astgb[0]
                    p.bop("dve", lambda e, sg2=sg2: e.tensor_scalar(
                        sg2[:, :], yk.ap[:, 0:512], csb["c_slw"][:, 0:1], None,
                        OP.mult), reads=[yk], writes=[sg2b])
                    ft = p.bop("dve", lambda e, sg2=sg2: e.tensor_tensor(
                        ydst[:, :], sg2[:, :], fbk.ap[:, 0:512], OP.mult),
                        reads=[sg2b, fbk], writes=[ydstb])
                    tg = p.dma("sync", yad[h, :, qcg * 512:(qcg + 1) * 512],
                               ydst[:, :], "ydst",
                               deps=[ft] + yadb.need_for_write())
                    ydstb.read(tg)
                    yadb.wrote(tg)

        phase_A1()
    p.barrier()

    if True:
        with ExitStack() as a2_es:
            alloc_generic(a2_es, nws=2)
            ya_s = [a2_es.enter_context(nc.sbuf_tensor(f"ya{k}", [128, T], BF16))
                    for k in range(KT)]
            ya_b = [Buf(t) for t in ya_s]
            deps = list(yadb.need_for_read())
            for k in range(KT):
                deps += ya_b[k].need_for_write()
            for k in range(KT):
                tg = p.dma("sync", ya_s[k][:, :], yad[k], "ya_l",
                           deps=deps if k == 0 else ())
            for k in range(KT):
                ya_b[k].wrote(tg)
            yadb.read(tg)
            for m in range(KT):
                slot = m % 2
                wt = p.dma("pool", wsl[slot][:, :], WA["wpa"][m], f"wsl{slot}",
                           deps=wslb[slot].need_for_write())
                wslb[slot].wrote(wt)
                for tc in range(TC):
                    bk = fb[tc % 2]
                    for k in range(KT):
                        p.bop("pe", lambda e, k=k, tc=tc, o=bk.ap, w=wsl[slot],
                              ya=ya_s[k]:
                              e.matmul(o[:, 0:512],
                                       w[:, k * 128:(k + 1) * 128],
                                       ya[:, tc * 512:(tc + 1) * 512],
                                       start=(k == 0), stop=(k == KT - 1)),
                              reads=[wslb[slot], ya_b[k]], writes=[bk])
                    xs, xsb = stg[tc % 2], stgb[tc % 2]
                    lt = p.dma("sync", xs[:, :], x_own[m, :, tc * 512:(tc + 1) * 512],
                               f"a2x{tc % 2}", deps=xsb.need_for_write())
                    xsb.wrote(lt)
                    os_, osb = stg[2 + tc % 2], stgb[2 + tc % 2]
                    et = p.bop("dve", lambda e, o=bk.ap, xs=xs, os_=os_:
                               e.tensor_tensor(
                        os_[:, :], xs[:, :], o[:, 0:512], OP.add),
                        reads=[bk, xsb], writes=[osb])
                    tg = p.dma("sync", r1[m, :, tc * 512:(tc + 1) * 512],
                               os_[:, :], f"a2o{tc % 2}",
                               deps=[et] + r1b.need_for_write())
                    osb.read(tg)
                    r1b.wrote(tg)
        p.barrier()

    with ExitStack() as q_es:
        sbq = lambda n, s, d=F32: q_es.enter_context(nc.sbuf_tensor(n, s, d))
        qw = sbq("qw", [128, FFN])
        qwb = Buf(qw)
        qa2 = sbq("qa2", [128, FFN], BF16)
        qa2b = Buf(qa2)
        qn_ = sbq("qn", [128, FFN], BF16)
        qnb = Buf(qn_)
        gac = sbq("gac", [128, 48])
        gacb = Buf(gac)
        trow = sbq("trow", [1, 8])
        trowb = Buf(trow)
        tcol = sbq("tcol", [128, 2])
        tcolb = Buf(tcol)
        csrow = sbq("csrow", [1, 128], BF16)
        csrowb = Buf(csrow)

        def quantize(wsrc, wdst, wdstb, mm, kk, cs0):
            wid = kk * 128
            nelem = mm * 128 * wid
            for m in range(mm):
                tg = p.dma("sync", qw[:, 0:wid], wsrc[m], "qw",
                           deps=qwb.need_for_write())
                qwb.wrote(tg)
                p.bop("act", lambda e, m=m, w=wid: e.activation(
                    qw[:, 0:w], qw[:, 0:w], AF.Abs,
                    accum_out=gac[:, m:m + 1]), reads=[qwb], writes=[qwb, gacb])
            p.bop("dve", lambda e, mm=mm: e.reduce_sum(
                gac[:, 47:48], gac[:, 0:mm], AX), reads=[gacb], writes=[gacb])
            b = fb[0]
            p.bop("pe", lambda e, o=b.ap: e.matmul(
                o[0:1, 0:1], csb["c_onesc"][:, 0:1], gac[:, 47:48],
                start=True, stop=True), reads=[gacb], writes=[b])
            p.bop("dve", lambda e, o=b.ap, n=nelem: e.tensor_scalar(
                trow[0:1, 0:1], o[0:1, 0:1], 0.5 / n, None, OP.mult),
                reads=[b], writes=[trowb])
            p.bop("dve", lambda e: e.tensor_scalar(
                trow[0:1, 0:1], trow[0:1, 0:1], 0.5e-8, None, OP.add),
                reads=[trowb], writes=[trowb])
            b2 = fb[1]
            p.bop("pe", lambda e, o=b2.ap: e.matmul(
                o[:, 0:1], csb["c_onesf"][0:1, :], trow[0:1, 0:1],
                start=True, stop=True), reads=[trowb], writes=[b2])
            p.bop("dve", lambda e, i=b2.ap: e.tensor_copy(tcol[:, 0:1],
                                                          i[:, 0:1]),
                  reads=[b2], writes=[tcolb])
            p.bop("dve", lambda e: e.tensor_scalar_mul(
                tcol[:, 1:2], tcol[:, 0:1], -1.0), reads=[tcolb], writes=[tcolb])
            for m in range(mm):
                tg = p.dma("sync", qw[:, 0:wid], wsrc[m], "qw",
                           deps=qwb.need_for_write())
                qwb.wrote(tg)
                p.bop("dve", lambda e, w=wid: e.tensor_scalar(
                    qa2[:, 0:w], qw[:, 0:w], tcol[:, 0:1], None, OP.is_ge),
                    reads=[qwb, tcolb], writes=[qa2b])
                ft = p.bop("dve", lambda e, w=wid: e.scalar_tensor_tensor(
                    qn_[:, 0:w], qw[:, 0:w], tcol[:, 1:2], qa2[:, 0:w],
                    OP.is_le, OP.subtract), reads=[qwb, tcolb, qa2b],
                    writes=[qnb])
                tg = p.dma("sync", wdst[m], qn_[:, 0:wid], "qn",
                           deps=[ft] + wdstb.need_for_write())
                qnb.read(tg)
                wdstb.wrote(tg)
                bcs = fb[2]
                for k in range(kk):
                    p.bop("pe", lambda e, k=k, o=bcs.ap, st=(k == 0),
                          sp=(k == kk - 1): e.matmul(
                              o[0:1, 0:128], c_ones[:, 0:1],
                              qn_[:, k * 128:(k + 1) * 128], start=st, stop=sp),
                          reads=[qnb], writes=[bcs])
                ct_ = p.bop("dve", lambda e, o=csrow[0:1, 0:128],
                            i=bcs.ap: e.tensor_copy(o, i[0:1, 0:128]),
                            reads=[bcs], writes=[csrowb])
                tg = p.dma("sync", csd[0:1, (cs0 + m) * 128:(cs0 + m + 1) * 128],
                           csrow[0:1, 0:128], "csr",
                           deps=[ct_] + csdb.need_for_write())
                csrowb.read(tg)
                csdb.wrote(tg)

        quantize(fw1, wq1d, wq1db, FFN // 128, KT, 0)
        quantize(fw2, wq2d, wq2db, FFN // 128, KT, FFN // 128)
        quantize(fw3, wq3d, wq3db, KT, FFN // 128, 2 * (FFN // 128))
    p.barrier()

    def ffn_phase(tag, xin, out, outb, FF, w_gate, w_val, w_out, gate_silu,
                  quantized, cs_g0, cs_v0, extras, wcol=None, bcolbf=None,
                  wcolbf=None):
        MM = FF // 128
        NQ = 4 if FF == FFN else 8
        MQ = MM // NQ
        with ExitStack() as f_es:
            sbf = lambda n, s, d=F32: f_es.enter_context(
                nc.sbuf_tensor(f"{n}_{tag}", s, d))
            alloc_generic(f_es, nws=4, n_xt=2, with_hbt=False,
                          rows_len=T)
            xbff = [sbf(f"xbf{k}", [128, T], BF16) for k in range(KT)]
            xbffb = [Buf(t) for t in xbff]
            rbt = [sbf(f"rbt{t}", [128, 512]) for t in range(TC)]
            rbtb = [Buf(t) for t in rbt]
            hstr = [sbf(f"hs{i}", [128, T], BF16) for i in range(MQ)]
            hstrb = [Buf(t) for t in hstr]
            sgt = [sbf(f"sgt{i}", [128, 512]) for i in range(4)]
            sgtb = [Buf(t) for t in sgt]
            bwc = sbf("bwc", [128, 4])
            bwcb = Buf(bwc)
            crq = [sbf(f"crq{i}", [1, MQ * 128], BF16) for i in range(2)]
            crqb = [Buf(t) for t in crq]
            wob = sbf("wob", [128, MQ * 128], BF16)
            wobb = Buf(wob)
            for tc in range(TC):
                ln_fold_stream(tc * 512, xin, tc, xbff, xbffb, tc * 512,
                               wcol=wcol)
                bk = fb[1]
                p.bop("pe", lambda e, tc=tc, o=bk.ap,
                      r_=lnr[0][0:1, tc * 512:(tc + 1) * 512]: e.matmul(
                          o[:, 0:512], csb["c_onesf"][0:1, :], r_,
                          start=True, stop=True), reads=[lnrb], writes=[bk])
                p.bop("dve", lambda e, d=rbt[tc], i=bk.ap: e.tensor_copy(
                    d[:, :], i[:, 0:512]), reads=[bk], writes=[rbtb[tc]])
            for q in range(4):
                for mi in range(MQ):
                    m = q * MQ + mi
                    gsl, vsl = mi % 2, 2 + mi % 2
                    dmaq = "sync" if quantized else "pool"
                    sfx = "s" if quantized else ""
                    gt = p.dma(dmaq, wsl[gsl][:, :], w_gate[m], f"wsl{gsl}{sfx}",
                               deps=wslb[gsl].need_for_write())
                    wslb[gsl].wrote(gt)
                    vt = p.dma(dmaq, wsl[vsl][:, :], w_val[m], f"wsl{vsl}{sfx}",
                               deps=wslb[vsl].need_for_write())
                    wslb[vsl].wrote(vt)
                    if quantized:
                        if mi == 0:
                            for ci, c0 in ((0, cs_g0), (1, cs_v0)):
                                ct_ = p.dma("sync", crq[ci][0:1, :],
                                            csd[0:1, (c0 + q * MQ) * 128:
                                                (c0 + q * MQ + MQ) * 128],
                                            f"crq{ci}",
                                            deps=crqb[ci].need_for_write() +
                                            csdb.need_for_read())
                                crqb[ci].wrote(ct_)
                                csdb.read(ct_)
                        csg = (crq[0][0:1, mi * 128:(mi + 1) * 128], crqb[0])
                        csv = (crq[1][0:1, mi * 128:(mi + 1) * 128], crqb[1])
                    else:
                        for wi, sl, c0 in ((0, gsl, 8), (1, vsl, 9)):
                            bcs = fb[4]
                            lh = wcolbf if wcolbf is not None else c_ones
                            lsl = (lambda k: lh[:, k:k + 1]) if wcolbf is not None \
                                else (lambda k: c_ones[:, 0:1])
                            for k in range(KT):
                                p.bop("pe", lambda e, k=k, w=wsl[sl], o=bcs.ap,
                                      l=lsl(k), st=(k == 0), sp=(k == KT - 1):
                                      e.matmul(o[0:1, 0:128], l,
                                               w[:, k * 128:(k + 1) * 128],
                                               start=st, stop=sp),
                                      reads=[wslb[sl]], writes=[bcs])
                            p.bop("dve", lambda e, o=colrow[0:1, c0 * 128:
                                                           (c0 + 1) * 128],
                                  i=bcs.ap: e.tensor_copy(o, i[0:1, 0:128]),
                                  reads=[bcs], writes=[colrowb])
                            if bcolbf is not None:
                                bcs2 = fb[5]
                                for k in range(KT):
                                    p.bop("pe", lambda e, k=k, w=wsl[sl], o=bcs2.ap,
                                          st=(k == 0), sp=(k == KT - 1):
                                          e.matmul(o[:, 0:1],
                                                   w[:, k * 128:(k + 1) * 128],
                                                   bcolbf[:, k:k + 1],
                                                   start=st, stop=sp),
                                          reads=[wslb[sl]], writes=[bcs2])
                                p.bop("dve", lambda e, o=bwc[:, wi:wi + 1],
                                      i=bcs2.ap: e.tensor_copy(o, i[:, 0:1]),
                                      reads=[bcs2], writes=[bwcb])
                        csg = (colrow[0:1, 8 * 128:9 * 128], colrowb)
                        csv = (colrow[0:1, 9 * 128:10 * 128], colrowb)
                    for tcp in range((TC + 1) // 2):
                        tcl = [t for t in (2 * tcp, 2 * tcp + 1) if t < TC]
                        pg = [fb[0], fb[1]]
                        pv = [fb[2], fb[3]]
                        for k in range(KT):
                            for ti, tc in enumerate(tcl):
                                p.bop("pe", lambda e, k=k, w=wsl[gsl],
                                      o=pg[ti].ap, tc=tc, xb_=xbff[k]:
                                      e.matmul(o[:, 0:512],
                                               w[:, k * 128:(k + 1) * 128],
                                               xb_[:, tc * 512:(tc + 1) * 512],
                                               start=(k == 0), stop=False),
                                      reads=[wslb[gsl], xbffb[k]],
                                      writes=[pg[ti]])
                            for ti, tc in enumerate(tcl):
                                p.bop("pe", lambda e, k=k, w=wsl[vsl],
                                      o=pv[ti].ap, tc=tc, xb_=xbff[k]:
                                      e.matmul(o[:, 0:512],
                                               w[:, k * 128:(k + 1) * 128],
                                               xb_[:, tc * 512:(tc + 1) * 512],
                                               start=(k == 0), stop=False),
                                      reads=[wslb[vsl], xbffb[k]],
                                      writes=[pv[ti]])
                        for ti, tc in enumerate(tcl):
                            for bk, cs in ((pg[ti], csg), (pv[ti], csv)):
                                p.bop("pe", lambda e, o=bk.ap, l=cs[0],
                                      r_=lnm[0][0:1, tc * 512:(tc + 1) * 512]:
                                      e.matmul(o[:, 0:512], l, r_, start=False,
                                               stop=True),
                                      reads=[cs[1], lnmb], writes=[bk])
                        for ti, tc in enumerate(tcl):
                            bg, bv = pg[ti], pv[ti]
                            e1, e1b = sgt[ti % 2], sgtb[ti % 2]
                            e2, e2b = sgt[2 + ti % 2], sgtb[2 + ti % 2]
                            p.bop("dve", lambda e, d=e1, i=bg.ap, r_=rbt[tc]:
                                  e.tensor_tensor(d[:, :], i[:, 0:512],
                                                  r_[:, :], OP.mult),
                                  reads=[bg, rbtb[tc]], writes=[e1b])
                            af_ = AF.Silu if gate_silu else AF.Sigmoid
                            sc_ = -1.0 if quantized else 1.0
                            if bcolbf is not None:
                                p.bop("act", lambda e, d=e1, af=af_, sc=sc_:
                                      e.activation(d[:, :], d[:, :], af,
                                                   bias=bwc[:, 0:1], scale=sc),
                                      reads=[e1b, bwcb], writes=[e1b])
                            else:
                                p.bop("act", lambda e, d=e1, af=af_, sc=sc_:
                                      e.activation(d[:, :], d[:, :], af,
                                                   scale=sc),
                                      reads=[e1b], writes=[e1b])
                            p.bop("dve", lambda e, d=e2, i=bv.ap, r_=rbt[tc]:
                                  e.tensor_tensor(d[:, :], i[:, 0:512],
                                                  r_[:, :], OP.mult),
                                  reads=[bv, rbtb[tc]], writes=[e2b])
                            if bcolbf is not None:
                                p.bop("dve", lambda e, d=hstr[mi], tc=tc, e1=e1,
                                      e2=e2: e.scalar_tensor_tensor(
                                          d[:, tc * 512:(tc + 1) * 512],
                                          e2[:, :], bwc[:, 1:2], e1[:, :],
                                          OP.add, OP.mult),
                                      reads=[e1b, e2b, bwcb],
                                      writes=[hstrb[mi]])
                            else:
                                p.bop("dve", lambda e, d=hstr[mi], tc=tc, e1=e1,
                                      e2=e2: e.tensor_tensor(
                                          d[:, tc * 512:(tc + 1) * 512],
                                          e1[:, :], e2[:, :], OP.mult),
                                      reads=[e1b, e2b], writes=[hstrb[mi]])
                for mo in range(KT):
                    wt = p.dma("sync" if quantized else "pool", wob[:, :],
                               w_out[mo, :, q * MQ * 128:(q + 1) * MQ * 128],
                               "wobs" if quantized else "wob",
                               deps=wobb.need_for_write())
                    wobb.wrote(wt)
                    pb4 = [fb[0], fb[1], fb[2], fb[3]]
                    for ki in range(MQ):
                        for tc in range(TC):
                            p.bop("pe", lambda e, ki=ki, o=pb4[tc].ap, tc=tc:
                                  e.matmul(o[:, 0:512],
                                           wob[:, ki * 128:(ki + 1) * 128],
                                           hstr[ki][:, tc * 512:(tc + 1) * 512],
                                           start=(ki == 0), stop=(ki == MQ - 1)),
                                  reads=[wobb, hstrb[ki]], writes=[pb4[tc]])
                    for tc in range(TC):
                        bk = pb4[tc]
                        s1, s1b = stg[tc % 2], stgb[tc % 2]
                        s2, s2b = stg[2 + tc % 2], stgb[2 + tc % 2]
                        if q == 0:
                            et = p.bop("dve", lambda e, d=s2, i=bk.ap:
                                       e.tensor_copy(d[:, :], i[:, 0:512]),
                                       reads=[bk], writes=[s2b])
                            tg = p.dma("sync",
                                       accd[mo, :, tc * 512:(tc + 1) * 512],
                                       s2[:, :], f"fo{tc % 2}",
                                       deps=[et] + accdb.need_for_write())
                            s2b.read(tg)
                            accdb.wrote(tg)
                        else:
                            lt = p.dma("sync", s1[:, :],
                                       accd[mo, :, tc * 512:(tc + 1) * 512],
                                       f"fa{tc % 2}",
                                       deps=s1b.need_for_write() +
                                       accdb.need_for_read())
                            s1b.wrote(lt)
                            accdb.read(lt)
                            et = p.bop("dve", lambda e, d=s2, a=s1, i=bk.ap:
                                       e.tensor_tensor(d[:, :], a[:, :],
                                                       i[:, 0:512], OP.add),
                                       reads=[bk, s1b], writes=[s2b])
                            if q < 3:
                                tg = p.dma("sync",
                                           accd[mo, :, tc * 512:(tc + 1) * 512],
                                           s2[:, :], f"fo{tc % 2}",
                                           deps=[et] + accdb.need_for_write())
                                s2b.read(tg)
                                accdb.wrote(tg)
                            else:
                                for xs in [xin] + extras:
                                    lt = p.dma("sync", s1[:, :],
                                               xs[mo, :, tc * 512:(tc + 1) * 512],
                                               f"fa{tc % 2}",
                                               deps=s1b.need_for_write())
                                    s1b.wrote(lt)
                                    et = p.bop("dve", lambda e, d=s2, a=s1:
                                               e.tensor_add(d[:, :], d[:, :],
                                                            a[:, :]),
                                               reads=[s1b, s2b], writes=[s2b])
                                tg = p.dma("sync",
                                           out[mo, :, tc * 512:(tc + 1) * 512],
                                           s2[:, :], f"fo{tc % 2}",
                                           deps=[et] + outb.need_for_write())
                                s2b.read(tg)
                                outb.wrote(tg)

    ffn_phase("f1", r1, r2, r2b, FFN, wq1d, wq2d, wq3d, False, True,
              0, FFN // 128, [])
    p.barrier()

    halosb = p.sb("halosb", [128, 64])
    halosbb = Buf(halosb)
    mt_ = p.bop("dve", lambda e: e.memset(halosb[:, :], 0.0), writes=[halosbb])
    with ExitStack() as b0_es:
        alloc_generic(b0_es, nws=0)
        sb0 = lambda n, s, d=F32: b0_es.enter_context(nc.sbuf_tensor(n, s, d))
        xbt = [sb0(f"xbt{k}", [128, 512]) for k in range(KT)]
        xbtb = [Buf(t) for t in xbt]
        h1t = [sb0(f"h1t{k}", [128, 512], BF16) for k in range(KT)]
        h1tb = [Buf(t) for t in h1t]
        for tc in range(TC):
            deps = []
            for k in range(KT):
                deps += xtb[k].need_for_write()
            for k in range(KT):
                xl = p.dma("sync", xt[k][:, :], r2[k, :, tc * 512:(tc + 1) * 512],
                           "xt", deps=deps if k == 0 else ())
            for k in range(KT):
                xtb[k].wrote(xl)
            ln_fold_chunk(tc * 512, 0, store_bf=False)
            ro = tc * 512
            p.bop("dve", lambda e, o=rw(5), r_=lnr[0][0:1, ro:ro + 512]:
                  e.tensor_mul(o, r_, r_), reads=[lnrb, rwsb], writes=[rwsb])
            p.bop("dve", lambda e, o=rw(5): e.tensor_scalar(
                o, o, -EPS, 1.0 + EPS, OP.mult, OP.add),
                reads=[rwsb], writes=[rwsb])
            p.bop("act", lambda e, o=rw(5): e.activation(o, o, AF.Ln),
                  reads=[rwsb], writes=[rwsb])
            p.bop("act", lambda e, o=rw(5): e.activation(o, o, AF.Exp,
                                                         scale=-0.5),
                  reads=[rwsb], writes=[rwsb])
            p.bop("dve", lambda e, o=rw(6), r_=lnr[0][0:1, ro:ro + 512],
                  nm=lnm[0][0:1, ro:ro + 512]: e.tensor_tensor(
                      o, r_, nm, OP.mult), reads=[lnrb, lnmb, rwsb],
                  writes=[rwsb])
            for bi, off in ((1, None), (2, 6), (3, 5)):
                bk = fb[bi]
                if off is None:
                    p.bop("pe", lambda e, o=bk.ap,
                          r_=lnr[0][0:1, ro:ro + 512]: e.matmul(
                              o[:, 0:512], csb["c_onesf"][0:1, :], r_,
                              start=True, stop=True),
                          reads=[lnrb], writes=[bk])
                else:
                    p.bop("pe", lambda e, o=bk.ap, i=rw(off): e.matmul(
                        o[:, 0:512], csb["c_onesf"][0:1, :], i,
                        start=True, stop=True), reads=[rwsb], writes=[bk])
            sb_s2 = stg[0]
            p.bop("dve", lambda e, d=sb_s2, i=fb[3].ap: e.tensor_copy(
                d[:, :], i[:, 0:512]), reads=[fb[3]], writes=[stgb[0]])
            for k in range(KT):
                p.bop("dve", lambda e, d=xbt[k], s=xt[k], i=fb[1].ap:
                      e.tensor_tensor(d[:, :], s[:, :], i[:, 0:512], OP.mult),
                      reads=[xtb[k], fb[1]], writes=[xbtb[k]])
                et = p.bop("dve", lambda e, d=xbt[k], i=fb[2].ap:
                           e.tensor_tensor(d[:, :], d[:, :], i[:, 0:512],
                                           OP.add), reads=[fb[2], xbtb[k]],
                           writes=[xbtb[k]])
                tg = p.dma("sync", xbd[k, :, tc * 512:(tc + 1) * 512],
                           xbt[k][:, :], "xbt", deps=([et] +
                           xbdb.need_for_write() if k == 0 else [et]))
                xbtb[k].read(tg)
                p.bop("dve", lambda e, d=stg[1], s=xbt[k], s2=sb_s2:
                      e.tensor_tensor(d[:, :], s[:, :], s2[:, :], OP.mult),
                      reads=[xbtb[k], stgb[0]], writes=[stgb[1]])
                ht_ = p.bop("dve", lambda e, d=h1t[k], s=stg[1], k=k:
                            e.tensor_scalar(d[:, :], s[:, :],
                                            csb["c_l1w"][:, k:k + 1],
                                            csb["c_l1b"][:, k:k + 1],
                                            OP.mult, OP.add),
                            reads=[stgb[1]], writes=[h1tb[k]])
                tg = p.dma("sync", hbd[k, :, 3 + tc * 512:3 + (tc + 1) * 512],
                           h1t[k][:, :], "h1t",
                           deps=[ht_] + (hbdb.need_for_write() if k == 0 else []))
                h1tb[k].read(tg)
                if tc == TC - 1:
                    p.bop("dve", lambda e, k=k, s=h1t[k]: e.tensor_copy(
                        halosb[:, k * 3:k * 3 + 3], s[:, 509:512]),
                        reads=[h1tb[k]], writes=[halosbb])
            xbdb.wrote(tg)
            hbdb.wrote(tg)
    ht_ = p.dma("sync", cc1i[:, :], halosb[:, :], "cc1",
                deps=halosbb.need_for_read() + cc1ib.need_for_write())
    halosbb.read(ht_)
    cc1ib.wrote(ht_)
    p.barrier()

    def collective(cin, cout):
        p.cnt["CC"] += 1
        h = p.sem["CC"]
        p.raw("pool", lambda e, h=h, ci=cin, co=cout: e.collective_compute(
            "AllGather", OP.bypass, replica_groups=groups,
            ins=[ci.opt()], outs=[co.opt()]).then_inc(h, 1))
        return ("CC", p.cnt["CC"])

    cct1 = collective(cc1i, cc1o)
    tg = p.dma("sync", halosb[:, 0:64], cc1o[0:128, :], "cc1l", deps=[cct1])
    halosbb.wrote(tg)
    hm = p.bop("dve", lambda e: e.tensor_tensor(
        halosb[:, 0:64], halosb[:, 0:64], csb["c_mask"][:, 0:64], OP.mult),
        reads=[halosbb], writes=[halosbb])
    with ExitStack() as hc_es:
        halobf = hc_es.enter_context(nc.sbuf_tensor("halobf", [128, 64], BF16))
        halobfb = Buf(halobf)
        hb_ = p.bop("dve", lambda e: e.tensor_copy(halobf[:, :], halosb[:, :]),
                    reads=[halosbb], writes=[halobfb])
        for k in range(KT):
            tg = p.dma("sync", hbd[k, :, 0:3], halobf[:, k * 3:k * 3 + 3],
                       "halost", deps=[hb_] + (hbdb.need_for_write()
                                              if k == 0 else []))
            halobfb.read(tg)
        hbdb.wrote(tg)
    p.barrier()

    NB_ = NB
    afd, afdb = dsc("afd", (16, T))
    with ExitStack() as b1_es:
        sb1 = lambda n, s, d=F32: b1_es.enter_context(nc.sbuf_tensor(n, s, d))
        alloc_generic(b1_es, nws=8, n_xt=0, rows_len=64)
        h1in = [sb1(f"h1in{k}", [128, 515], BF16) for k in range(KT)]
        h1inb = [Buf(t) for t in h1in]
        xct = [sb1(f"xct{k}", [128, 512], BF16) for k in range(KT)]
        xctb = [Buf(t) for t in xct]
        lgam = sb1("lgam", [16, T])
        lgamb = Buf(lgam)
        cblk = sb1("cblk", [16, T])
        cblkb = Buf(cblk)
        cbl2 = sb1("cbl2", [16, T])
        cbl2b = Buf(cbl2)
        afull = sb1("afull", [16, T])
        afullb = Buf(afull)
        bsr = sb1("bsr", [16, 64])
        bsrb = Buf(bsr)
        ctn = sb1("ctn", [128, NB_ * 16])
        ctnb = Buf(ctn)
        carry = sb1("carry", [128, 16])
        carryb = Buf(carry)
        gcolt = sb1("gcolt", [128, 1])
        gcoltb = Buf(gcolt)
        wggbf = sb1("wggbf", [128, KT * NH], BF16)
        wggbfb = Buf(wggbf)
        ufm = sb1("ufm", [128, 128], BF16)
        ufmb = Buf(ufm)
        uts = sb1("uts", [128, 128], BF16)
        utsb = Buf(uts)
        wt_ = sb1("wt", [128, 128], BF16)
        wtb = Buf(wt_)
        mst = sb1("mst", [128, 128])
        mstb = Buf(mst)
        kbt = sb1("kbt", [128, 512], BF16)
        kbtb = Buf(kbt)
        vft = sb1("vft", [128, 512])
        vftb = Buf(vft)
        igt = sb1("igt", [128, 512])
        igtb = Buf(igt)

        wg_ = p.dma("pool", wggbf[:, :], wgg, "wggbf")
        wggbfb.wrote(wg_)

        def b_l2(bk, out_t, out_b, _bf):
            sq_, sqb_ = stbf[0], stbfb[0]
            p.bop("act", lambda e, d=sq_, i=bk.ap: e.activation(
                d[:, :], i[:, 0:512], AF.Square), reads=[bk], writes=[sqb_])
            bs = fb[4]
            p.bop("pe", lambda e, d=sq_, o=bs.ap: e.matmul(
                o[0:1, 0:512], c_ones[:, 0:1], d[:, :], start=True, stop=True),
                reads=[sqb_], writes=[bs])
            p.bop("dve", lambda e, o=rw(0), i=bs.ap: e.tensor_copy(
                o, i[0:1, 0:512]), reads=[bs], writes=[rwsb])
            p.bop("act", lambda e, o=rw(0): e.activation(
                o, o, AF.Ln, bias=csb["c_cvb"][0:1, 1:2]),
                reads=[rwsb], writes=[rwsb])
            p.bop("act", lambda e, o=rw(0): e.activation(
                o, o, AF.Exp, scale=-0.5), reads=[rwsb], writes=[rwsb])
            bb = fb[5]
            p.bop("pe", lambda e, o=bb.ap, i=rw(0): e.matmul(
                o[:, 0:512], csb["c_onesf"][0:1, :], i, start=True, stop=True),
                reads=[rwsb], writes=[bb])
            s1_, s1b_ = stg[3], stgb[3]
            p.bop("dve", lambda e, d=s1_, i=bb.ap: e.tensor_copy(
                d[:, :], i[:, 0:512]), reads=[bb], writes=[s1b_])
            p.bop("dve", lambda e, d=out_t, i=bk.ap, r_=s1_: e.tensor_tensor(
                d[:, :], i[:, 0:512], r_[:, :], OP.mult),
                reads=[bk, s1b_], writes=[out_b])

        def b_vln(bk, out_t, out_b):
            sq_, sqb_ = stbf[0], stbfb[0]
            p.bop("act", lambda e, d=sq_, i=bk.ap: e.activation(
                d[:, :], i[:, 0:512], AF.Square), reads=[bk], writes=[sqb_])
            bs = fb[4]
            p.bop("pe", lambda e, d=sq_, o=bs.ap: e.matmul(
                o[0:1, 0:512], c_ones[:, 0:1], d[:, :], start=True, stop=True),
                reads=[sqb_], writes=[bs])
            p.bop("dve", lambda e, o=rw(1), i=bs.ap: e.tensor_copy(
                o, i[0:1, 0:512]), reads=[bs], writes=[rwsb])
            s0_, s0b_ = stg[3], stgb[3]
            p.bop("dve", lambda e, d=s0_, i=bk.ap: e.tensor_copy(
                d[:, :], i[:, 0:512]), reads=[bk], writes=[s0b_])
            p.bop("pe", lambda e, d=s0_, o=bs.ap: e.matmul(
                o[0:1, 0:512], csb["c_onesc"][:, 0:1], d[:, :],
                start=True, stop=True), reads=[s0b_], writes=[bs])
            p.bop("dve", lambda e, o=rw(0), i=bs.ap: e.tensor_copy(
                o, i[0:1, 0:512]), reads=[bs], writes=[rwsb])
            p.bop("dve", lambda e, o=rw(2), i=rw(0): e.tensor_scalar_mul(
                o, i, 1.0 / 128), reads=[rwsb], writes=[rwsb])
            p.bop("dve", lambda e, o=rw(3), i=rw(2): e.tensor_mul(o, i, i),
                  reads=[rwsb], writes=[rwsb])
            p.bop("dve", lambda e, o=rw(4), i=rw(1): e.tensor_scalar(
                o, i, 1.0 / 128, None, OP.mult), reads=[rwsb], writes=[rwsb])
            p.bop("dve", lambda e, o=rw(4), i=rw(3): e.tensor_sub(o, o, i),
                  reads=[rwsb], writes=[rwsb])
            p.bop("act", lambda e, o=rw(4): e.activation(
                o, o, AF.Ln, bias=csb["c_cvb"][0:1, 0:1]),
                reads=[rwsb], writes=[rwsb])
            p.bop("act", lambda e, o=rw(4): e.activation(
                o, o, AF.Exp, scale=-0.5), reads=[rwsb], writes=[rwsb])
            p.bop("dve", lambda e, o=rw(5), a=rw(2), b_=rw(4): e.tensor_mul(
                o, a, b_), reads=[rwsb], writes=[rwsb])
            p.bop("dve", lambda e, o=rw(5), i=rw(5): e.tensor_scalar_mul(
                o, i, -1.0), reads=[rwsb], writes=[rwsb])
            b1_, b2_ = fb[4], fb[5]
            p.bop("pe", lambda e, o=b1_.ap, i=rw(4): e.matmul(
                o[:, 0:512], csb["c_onesf"][0:1, :], i, start=True, stop=True),
                reads=[rwsb], writes=[b1_])
            p.bop("pe", lambda e, o=b2_.ap, i=rw(5): e.matmul(
                o[:, 0:512], csb["c_onesf"][0:1, :], i, start=True, stop=True),
                reads=[rwsb], writes=[b2_])
            s1_, s1b_ = stg[3], stgb[3]
            p.bop("dve", lambda e, d=s1_, i=b1_.ap: e.tensor_copy(
                d[:, :], i[:, 0:512]), reads=[b1_], writes=[s1b_])
            p.bop("dve", lambda e, d=out_t, i=bk.ap, r_=s1_: e.tensor_tensor(
                d[:, :], i[:, 0:512], r_[:, :], OP.mult),
                reads=[bk, s1b_], writes=[out_b])
            p.bop("dve", lambda e, d=out_t, i=b2_.ap: e.tensor_tensor(
                d[:, :], d[:, :], i[:, 0:512], OP.add),
                reads=[b2_, out_b], writes=[out_b])
            p.bop("dve", lambda e, d=out_t: e.tensor_scalar(
                d[:, :], d[:, :], csb["c_vnw"][:, 0:1], csb["c_vnb"][:, 0:1],
                OP.mult, OP.add), reads=[out_b], writes=[out_b])

        for tc in range(TC):
            deps = []
            for k in range(KT):
                deps += h1inb[k].need_for_write()
            for k in range(KT):
                hl = p.dma("sync", h1in[k][:, :],
                           hbd[k, :, tc * 512:tc * 512 + 515], "h1in",
                           deps=(deps + hbdb.need_for_read()) if k == 0 else ())
            for k in range(KT):
                h1inb[k].wrote(hl)
            hbdb.read(hl)
            for k in range(KT):
                acc, accb = stg[k % 2], stgb[k % 2]
                p.bop("dve", lambda e, d=acc, s=h1in[k], k=k: e.tensor_scalar(
                    d[:, :], s[:, 0:512], csb["c_cw"][:, k * 4:k * 4 + 1], None,
                    OP.mult), reads=[h1inb[k]], writes=[accb])
                for j in range(1, 4):
                    p.bop("dve", lambda e, d=acc, s=h1in[k], k=k, j=j:
                          e.scalar_tensor_tensor(
                              d[:, :], s[:, j:512 + j],
                              csb["c_cw"][:, k * 4 + j:k * 4 + j + 1],
                              d[:, :], OP.mult, OP.add),
                          reads=[h1inb[k], accb], writes=[accb])
                ct_ = p.bop("act", lambda e, d=xct[k], s=acc, k=k: e.activation(
                    d[:, :], s[:, :], AF.Silu, bias=csb["c_cb"][:, k:k + 1]),
                    reads=[accb], writes=[xctb[k]])
                tg = p.dma("sync", xcd[k, :, tc * 512:(tc + 1) * 512],
                           xct[k][:, :], "xcst",
                           deps=[ct_] + (xcdb.need_for_write() if k == 0 else []))
                xctb[k].read(tg)
            xcdb.wrote(tg)
            for tk in range(4):
                bk = fb[4]
                for k in range(KT):
                    p.bop("pe", lambda e, k=k, tk=tk, o=bk.ap, s=xct[k]:
                          e.matmul(o[:, 0:16], s[:, tk * 128:(tk + 1) * 128],
                                   wggbf[:, k * NH:(k + 1) * NH],
                                   start=(k == 0), stop=(k == KT - 1)),
                          reads=[xctb[k], wggbfb], writes=[bk])
                z_, zb_ = stg[2], stgb[2]
                p.bop("dve", lambda e, d=z_, i=bk.ap: e.tensor_tensor(
                    d[:, 0:16], i[:, 0:16], csb["c_bgb"][:, :], OP.add),
                    reads=[bk], writes=[zb_])
                a_, ab_ = stg[3], stgb[3]
                p.bop("dve", lambda e, d=a_, s=z_: e.tensor_scalar(
                    d[:, 0:16], s[:, 0:16], -1.0, None, OP.mult),
                    reads=[zb_], writes=[ab_])
                p.bop("dve", lambda e, d=a_, s=z_: e.tensor_max(
                    d[:, 0:16], d[:, 0:16], s[:, 0:16]),
                    reads=[zb_, ab_], writes=[ab_])
                p.bop("act", lambda e, d=a_: e.activation(
                    d[:, 0:16], d[:, 0:16], AF.Exp, scale=-1.0),
                    reads=[ab_], writes=[ab_])
                p.bop("act", lambda e, d=a_: e.activation(
                    d[:, 0:16], d[:, 0:16], AF.Ln, bias=1.0),
                    reads=[ab_], writes=[ab_])
                p.bop("dve", lambda e, d=z_, s=z_: e.tensor_scalar(
                    d[:, 16:32], s[:, 0:16], 0.0, None, OP.min),
                    reads=[zb_], writes=[zb_])
                lg_ = p.bop("dve", lambda e, d=z_, a=a_: e.tensor_tensor(
                    d[:, 0:16], d[:, 16:32], a[:, 0:16], OP.subtract),
                    reads=[zb_, ab_], writes=[zb_])
                xbk = fb[5]
                p.bop("pe", lambda e, s=z_, o=xbk.ap: e.transpose(
                    o[0:16, 0:128], s[:, 0:16], csb["c_idf"][:, :]),
                    reads=[zb_], writes=[xbk])
                p.bop("dve", lambda e, i=xbk.ap,
                      o=lgam[0:16, tc * 512 + tk * 128:tc * 512 + (tk + 1) * 128]:
                      e.tensor_copy(o, i[0:16, 0:128]),
                      reads=[xbk], writes=[lgamb])
        def hs_step(sv, sb_, dv, db_, d_):
            p.bop("dve", lambda e, s=sv, dd=dv, d_=d_: e.tensor_add(
                dd[:, :, d_:128], s[:, :, d_:128], s[:, :, 0:128 - d_]),
                reads=[sb_], writes=[db_])
            p.bop("dve", lambda e, s=sv, dd=dv, d_=d_: e.tensor_copy(
                dd[:, :, 0:d_], s[:, :, 0:d_]), reads=[sb_], writes=[db_])

        v_lg = lgam[0:16, 0:T].rearrange("p (n j) -> p n j", j=128)
        v_c1 = cblk[0:16, 0:T].rearrange("p (n j) -> p n j", j=128)
        v_c2 = cbl2[0:16, 0:T].rearrange("p (n j) -> p n j", j=128)
        seq = [(v_lg, lgamb), (v_c1, cblkb), (v_c2, cbl2b)]
        cur = 0
        for st_i, d_ in enumerate([1, 2, 4, 8, 16, 32, 64]):
            svv, svb = seq[cur]
            dst_i = 1 if cur != 1 else 2
            if st_i == 6:
                dst_i = 1
            dvv, dvb = seq[dst_i]
            hs_step(svv, svb, dvv, dvb, d_)
            cur = dst_i
        p.bop("dve", lambda e, o=bsr[0:16, 0:NB_],
              s=cblk[0:16, 0:T].rearrange("p (n j) -> p n j", j=128)[:, :, 127:128]:
              e.tensor_copy(o, s.rearrange("p n j -> p (n j)")),
              reads=[cblkb], writes=[bsrb])
        cur_o, oth_o = 0, 16
        d_ = 1
        while d_ < NB_:
            p.bop("dve", lambda e, d_=d_, a=cur_o, b_=oth_o: e.tensor_add(
                bsr[0:16, b_ + d_:b_ + NB_], bsr[0:16, a + d_:a + NB_],
                bsr[0:16, a:a + NB_ - d_]), reads=[bsrb], writes=[bsrb])
            p.bop("dve", lambda e, d_=d_, a=cur_o, b_=oth_o: e.tensor_copy(
                bsr[0:16, b_:b_ + d_], bsr[0:16, a:a + d_]),
                reads=[bsrb], writes=[bsrb])
            cur_o, oth_o = oth_o, cur_o
            d_ *= 2
        p.bop("dve", lambda e, a=cur_o, o=bsr[0:16, 32:32 + NB_],
              s=cblk[0:16, 0:T].rearrange("p (n j) -> p n j", j=128)[:, :, 127:128]:
              e.tensor_sub(o, bsr[0:16, a:a + NB_],
                           s.rearrange("p n j -> p (n j)")),
              reads=[bsrb, cblkb], writes=[bsrb])
        p.bop("dve", lambda e,
              o=cbl2[0:16, 0:T].rearrange("p (n j) -> p n j", j=128),
              s=cblk[0:16, 0:T].rearrange("p (n j) -> p n j", j=128),
              of=bsr[0:16, 32:32 + NB_].rearrange(
                  "p (n j) -> p n j", j=1).broadcast_to((16, NB_, 128)):
              e.tensor_tensor(o, s, of, OP.add),
              reads=[bsrb, cblkb], writes=[cbl2b])
        af_ = p.bop("act", lambda e: e.activation(
            afull[0:16, 0:T], cbl2[0:16, 0:T], AF.Exp),
            reads=[cbl2b], writes=[afullb])
        tg = p.dma("sync", afd[0:16, :], afull[0:16, 0:T], "afd",
                   deps=[af_] + afdb.need_for_write())
        afullb.read(tg)
        afdb.wrote(tg)
        for nb in range(NB_):
            xbk = fb[5]
            p.bop("pe", lambda e, nb=nb, o=xbk.ap: e.transpose(
                o[:, 0:16], cblk[0:16, nb * 128:(nb + 1) * 128],
                csb["c_idf"][0:16, 0:16]), reads=[cblkb], writes=[xbk])
            p.bop("dve", lambda e, nb=nb, i=xbk.ap: e.tensor_scalar_mul(
                ctn[:, nb * 16:(nb + 1) * 16], i[:, 0:16], -1.0),
                reads=[xbk], writes=[ctnb])
        for hg in range(NH // 2):
            heads = [hg * 2, hg * 2 + 1]
            wt8 = {}
            for hi, h in enumerate(heads):
                for wi2, (nm, wsrc) in enumerate(
                        [("k", WB["wkb"]), ("v", WB["wvb"]),
                         ("ig", WB["wig"]), ("q", WB["wqb"])]):
                    slot = hi * 4 + wi2
                    tg = p.dma("pool", wsl[slot][:, :], wsrc[h], f"wsl{slot}",
                               deps=wslb[slot].need_for_write())
                    wslb[slot].wrote(tg)
                    wt8[(h, nm)] = tg
            for tc in range(TC):
                deps = list(hbdb.need_for_read())
                for k in range(KT):
                    deps += hbtb[k].need_for_write()
                for k in range(KT):
                    hl = p.dma("sync", hbt[k][:, :],
                               hbd[k, :, 3 + tc * 512:3 + (tc + 1) * 512],
                               "hbt", deps=deps if k == 0 else ())
                for k in range(KT):
                    hbtb[k].wrote(hl)
                hbdb.read(hl)
                deps = list(xcdb.need_for_read())
                for k in range(KT):
                    deps += xctb[k].need_for_write()
                for k in range(KT):
                    xl = p.dma("sync", xct[k][:, :],
                               xcd[k, :, tc * 512:(tc + 1) * 512],
                               "xct", deps=deps if k == 0 else ())
                for k in range(KT):
                    xctb[k].wrote(xl)
                xcdb.read(xl)
                for hi, h in enumerate(heads):
                    bk_k, bk_v, bk_ig, bk_q = fb[0], fb[1], fb[2], fb[3]
                    for (bk, nm, rhs_t, rhs_b) in (
                            (bk_k, "k", hbt, hbtb), (bk_v, "v", hbt, hbtb),
                            (bk_ig, "ig", xct, xctb), (bk_q, "q", hbt, hbtb)):
                        slot = hi * 4 + {"k": 0, "v": 1, "ig": 2, "q": 3}[nm]
                        for k in range(KT):
                            p.bop("pe", lambda e, k=k, w=wsl[slot], o=bk.ap,
                                  r_=rhs_t[k]: e.matmul(
                                      o[:, 0:512], w[:, k * 128:(k + 1) * 128],
                                      r_[:, :], start=(k == 0),
                                      stop=(k == KT - 1)),
                                  reads=[wslb[slot], rhs_b[k]], writes=[bk])
                    b_l2(bk_k, kbt, kbtb, True)
                    b_vln(bk_v, vft, vftb)
                    p.bop("act", lambda e, d=igt, i=bk_ig.ap, h=h:
                          e.activation(d[:, :], i[:, 0:512], AF.Sigmoid,
                                       bias=csb["c_bigb"][:, h:h + 1]),
                          reads=[bk_ig], writes=[igtb])
                    b_l2(bk_q, stbf[3], stbfb[3], True)
                    qt_ = stbfb[3].writer
                    tg = p.dma("sync", qbd[h, :, tc * 512:(tc + 1) * 512],
                               stbf[3][:, :], "qbst",
                               deps=[qt_] + qbdb.need_for_write())
                    stbfb[3].read(tg)
                    qbdb.wrote(tg)
                    p.bop("dve", lambda e, d=stg[2], a=igt, b_=kbt:
                          e.tensor_tensor(d[:, :], a[:, :], b_[:, :], OP.mult),
                          reads=[igtb, kbtb], writes=[stgb[2]])
                    ut_ = p.bop("dve", lambda e, d=stbf[2], a=stg[2], b_=vft:
                                e.tensor_tensor(d[:, :], a[:, :], b_[:, :],
                                                OP.mult),
                                reads=[stgb[2], vftb], writes=[stbfb[2]])
                    tg = p.dma("sync", umd[h, :, tc * 512:(tc + 1) * 512],
                               stbf[2][:, :], "umst",
                               deps=[ut_] + umdb.need_for_write())
                    stbfb[2].read(tg)
                    umdb.wrote(tg)
        p.bop("dve", lambda e: e.memset(carry[:, :], 0.0), writes=[carryb])
        for nb in range(NB_):
            for h in range(NH):
                ul = p.dma("sync", ufm[:, :], umd[h, :, nb * 128:(nb + 1) * 128],
                           "ufm", deps=ufmb.need_for_write() +
                           umdb.need_for_read())
                ufmb.wrote(ul)
                umdb.read(ul)
                bkg = fb[2]
                p.bop("pe", lambda e, nb=nb, h=h, o=bkg.ap: e.matmul(
                    o[:, 0:1], csb["c_onesf"][0:1, :],
                    ctn[0:1, nb * 16 + h:nb * 16 + h + 1],
                    start=True, stop=True), reads=[ctnb], writes=[bkg])
                p.bop("act", lambda e, i=bkg.ap: e.activation(
                    gcolt[:, 0:1], i[:, 0:1], AF.Exp, scale=-1.0),
                    reads=[bkg], writes=[gcoltb])
                p.bop("dve", lambda e, h=h: e.scalar_tensor_tensor(
                    ufm[:, 0:1], carry[:, h:h + 1], gcolt[:, 0:1],
                    ufm[:, 0:1], OP.mult, OP.add),
                    reads=[carryb, gcoltb, ufmb], writes=[ufmb])
                xbk = xb[nb % 2]
                p.bop("pe", lambda e, o=xbk.ap: e.transpose(
                    o[:, 0:128], ufm[:, :], c_id[:, :]),
                    reads=[ufmb], writes=[xbk])
                p.bop("dve", lambda e, i=xbk.ap: e.tensor_copy(
                    uts[:, :], i[:, 0:128]), reads=[xbk], writes=[utsb])
                bkc = fb[4 + nb % 2]
                p.bop("pe", lambda e, nb=nb, h=h, o=bkc.ap: e.matmul(
                    o[:, 0:128], csb["c_ohf"][0:16, h * 128:(h + 1) * 128],
                    cblk[0:16, nb * 128:(nb + 1) * 128],
                    start=True, stop=True), reads=[cblkb], writes=[bkc])
                zt_, ztb_ = stg[2], stgb[2]
                p.bop("dve", lambda e, i=bkc.ap, d=zt_: e.tensor_tensor(
                    d[:, 0:128], i[:, 0:128], csb["c_trin"][:, :], OP.add),
                    reads=[bkc], writes=[ztb_])
                p.bop("act", lambda e, nb=nb, h=h, s=zt_: e.activation(
                    wt_[:, :], s[:, 0:128], AF.Exp,
                    bias=ctn[:, nb * 16 + h:nb * 16 + h + 1]),
                    reads=[ztb_, ctnb], writes=[wtb])
                bkm = fb[nb % 2]
                p.bop("pe", lambda e, o=bkm.ap: e.matmul(
                    o[:, 0:128], uts[:, :], wt_[:, :], start=True, stop=True),
                    reads=[utsb, wtb], writes=[bkm])
                mt_ = p.bop("dve", lambda e, i=bkm.ap: e.tensor_copy(
                    mst[:, :], i[:, 0:128]), reads=[bkm], writes=[mstb])
                p.bop("dve", lambda e, h=h: e.tensor_copy(
                    carry[:, h:h + 1], mst[:, 127:128]),
                    reads=[mstb], writes=[carryb])
                tg = p.dma("sync", memd[h, :, nb * 128:(nb + 1) * 128],
                           mst[:, :], "memst",
                           deps=[mt_] + memdb.need_for_write())
                mstb.read(tg)
                memdb.wrote(tg)
        tg = p.dma("sync", cc2i[:, :], carry[:, :], "cc2s",
                   deps=carryb.need_for_read() + cc2ib.need_for_write())
        carryb.read(tg)
        cc2ib.wrote(tg)
    p.barrier()

    cct2 = collective(cc2i, cc2o)
    p.barrier()

    with ExitStack() as b2_es:
        sb2 = lambda n, s, d=F32: b2_es.enter_context(nc.sbuf_tensor(n, s, d))
        alloc_generic(b2_es, nws=2, n_xt=0, with_hbt=False,
                      rows_len=64)
        statei = sb2("statei", [128, 16])
        stateib = Buf(statei)
        afl2 = sb2("afl2", [16, T])
        afl2b = Buf(afl2)
        ogs = [sb2(f"ogs{k}", [128, 512], BF16) for k in range(KT)]
        ogsb = [Buf(t) for t in ogs]
        xct2 = [sb2(f"xct2{k}", [128, 512], BF16) for k in range(KT)]
        xct2b = [Buf(t) for t in xct2]
        mg = sb2("mg", [128, 512])
        mgb = Buf(mg)
        m2 = sb2("m2", [128, 512])
        m2b = Buf(m2)
        qbt2 = sb2("qbt2", [128, 512], BF16)
        qbt2b = Buf(qbt2)
        tg = p.dma("sync", statei[:, :], cc2o[0:128, :], "cc2l", deps=[cct2])
        stateib.wrote(tg)
        p.bop("dve", lambda e: e.tensor_tensor(
            statei[:, :], statei[:, :], csb["c_mask"][:, 0:16], OP.mult),
            reads=[stateib], writes=[stateib])
        tg = p.dma("sync", afl2[0:16, :], afd[0:16, :], "afl2",
                   deps=afdb.need_for_read())
        afl2b.wrote(tg)
        afdb.read(tg)

        def dd_norm(src_t, src_b, wcol_ap, bcol_ap, out_t, out_b, out_bf16):
            sq_, sqb_ = stbf[0], stbfb[0]
            p.bop("act", lambda e, d=sq_, s=src_t: e.activation(
                d[:, :], s[:, :], AF.Square), reads=[src_b], writes=[sqb_])
            bs = fb[4]
            p.bop("pe", lambda e, s=src_t, o=bs.ap: e.matmul(
                o[0:1, 0:512], csb["c_onesc"][:, 0:1], s[:, :],
                start=True, stop=True), reads=[src_b], writes=[bs])
            p.bop("dve", lambda e, o=rw(0), i=bs.ap: e.tensor_copy(
                o, i[0:1, 0:512]), reads=[bs], writes=[rwsb])
            p.bop("pe", lambda e, d=sq_, o=bs.ap: e.matmul(
                o[0:1, 0:512], c_ones[:, 0:1], d[:, :], start=True, stop=True),
                reads=[sqb_], writes=[bs])
            p.bop("dve", lambda e, o=rw(1), i=bs.ap: e.tensor_copy(
                o, i[0:1, 0:512]), reads=[bs], writes=[rwsb])
            p.bop("dve", lambda e, o=rw(2), i=rw(0): e.tensor_scalar_mul(
                o, i, 1.0 / 128), reads=[rwsb], writes=[rwsb])
            p.bop("dve", lambda e, o=rw(3), i=rw(2): e.tensor_mul(o, i, i),
                  reads=[rwsb], writes=[rwsb])
            p.bop("dve", lambda e, o=rw(4), i=rw(1): e.tensor_scalar(
                o, i, 1.0 / 128, None, OP.mult), reads=[rwsb], writes=[rwsb])
            p.bop("dve", lambda e, o=rw(4), i=rw(3): e.tensor_sub(o, o, i),
                  reads=[rwsb], writes=[rwsb])
            p.bop("act", lambda e, o=rw(4): e.activation(
                o, o, AF.Ln, bias=csb["c_cvb"][0:1, 0:1]),
                reads=[rwsb], writes=[rwsb])
            p.bop("act", lambda e, o=rw(4): e.activation(
                o, o, AF.Exp, scale=-0.5), reads=[rwsb], writes=[rwsb])
            p.bop("dve", lambda e, o=rw(5), a=rw(2), b_=rw(4): e.tensor_mul(
                o, a, b_), reads=[rwsb], writes=[rwsb])
            p.bop("dve", lambda e, o=rw(5), i=rw(5): e.tensor_scalar_mul(
                o, i, -1.0), reads=[rwsb], writes=[rwsb])
            b1_ = fb[2]
            b2_ = fb[3]
            p.bop("pe", lambda e, o=b1_.ap, i=rw(4): e.matmul(
                o[:, 0:512], csb["c_onesf"][0:1, :], i, start=True, stop=True),
                reads=[rwsb], writes=[b1_])
            p.bop("pe", lambda e, o=b2_.ap, i=rw(5): e.matmul(
                o[:, 0:512], csb["c_onesf"][0:1, :], i, start=True, stop=True),
                reads=[rwsb], writes=[b2_])
            s1_, s1b_ = stg[2], stgb[2]
            p.bop("dve", lambda e, d=s1_, i=b1_.ap: e.tensor_copy(
                d[:, :], i[:, 0:512]), reads=[b1_], writes=[s1b_])
            s3_, s3b_ = stg[3], stgb[3]
            p.bop("dve", lambda e, d=s3_, s=src_t, r_=s1_: e.tensor_tensor(
                d[:, :], s[:, :], r_[:, :], OP.mult),
                reads=[src_b, s1b_], writes=[s3b_])
            p.bop("dve", lambda e, d=s3_, i=b2_.ap: e.tensor_tensor(
                d[:, :], d[:, :], i[:, 0:512], OP.add),
                reads=[b2_, s3b_], writes=[s3b_])
            p.bop("dve", lambda e, d=out_t, s=s3_, w=wcol_ap, b_=bcol_ap:
                  e.tensor_scalar(d[:, :], s[:, :], w, b_, OP.mult, OP.add),
                  reads=[s3b_], writes=[out_b])

        for tc in range(TC):
            deps = list(xcdb.need_for_read())
            for k in range(KT):
                deps += xct2b[k].need_for_write()
            for k in range(KT):
                xl = p.dma("sync", xct2[k][:, :],
                           xcd[k, :, tc * 512:(tc + 1) * 512], "xct2",
                           deps=deps if k == 0 else ())
            for k in range(KT):
                xct2b[k].wrote(xl)
            xcdb.read(xl)
            for h in range(NH):
                ml = p.dma("sync", mg[:, :], memd[h, :, tc * 512:(tc + 1) * 512],
                           "mgl", deps=mgb.need_for_write() +
                           memdb.need_for_read())
                mgb.wrote(ml)
                memdb.read(ml)
                ba = fb[5]
                p.bop("pe", lambda e, h=h, tc=tc, o=ba.ap: e.matmul(
                    o[:, 0:512], csb["c_ohf"][0:16, h * 128:(h + 1) * 128],
                    afl2[0:16, tc * 512:(tc + 1) * 512],
                    start=True, stop=True), reads=[afl2b], writes=[ba])
                p.bop("dve", lambda e, h=h, i=ba.ap: e.scalar_tensor_tensor(
                    mg[:, :], i[:, 0:512], statei[:, h:h + 1], mg[:, :],
                    OP.mult, OP.add), reads=[ba, stateib, mgb], writes=[mgb])
                dd_norm(mg, mgb, csb["c_mnw"][:, 0:1], csb["c_mnb"][:, 0:1],
                        m2, m2b, False)
                ql = p.dma("sync", qbt2[:, :], qbd[h, :, tc * 512:(tc + 1) * 512],
                           "qbl", deps=qbt2b.need_for_write() +
                           qbdb.need_for_read())
                qbt2b.wrote(ql)
                qbdb.read(ql)
                p.bop("dve", lambda e: e.tensor_tensor(
                    m2[:, :], m2[:, :], qbt2[:, :], OP.mult),
                    reads=[qbt2b, m2b], writes=[m2b])
                dd_norm(m2, m2b, csb["c_gnw"][:, h:h + 1],
                        csb["c_gnb"][:, h:h + 1], ogs[h], ogsb[h], True)
            for m_ in range(KT):
                slot = m_ % 2
                wt = p.dma("pool", wsl[slot][:, :], WB["wog"][m_], f"wsl{slot}",
                           deps=wslb[slot].need_for_write())
                wslb[slot].wrote(wt)
                bk = fb[m_ % 2]
                for k in range(KT):
                    p.bop("pe", lambda e, k=k, w=wsl[slot], o=bk.ap,
                          r_=xct2[k]: e.matmul(
                              o[:, 0:512], w[:, k * 128:(k + 1) * 128],
                              r_[:, :], start=(k == 0), stop=(k == KT - 1)),
                          reads=[wslb[slot], xct2b[k]], writes=[bk])
                g_, gb_ = stg[0], stgb[0]
                p.bop("act", lambda e, d=g_, i=bk.ap, m_=m_: e.activation(
                    d[:, :], i[:, 0:512], AF.Sigmoid,
                    bias=csb["c_bogb"][:, m_:m_ + 1]),
                    reads=[bk], writes=[gb_])
                p.bop("dve", lambda e, d=ogs[m_], g=g_: e.tensor_tensor(
                    d[:, :], d[:, :], g[:, :], OP.mult),
                    reads=[gb_, ogsb[m_]], writes=[ogsb[m_]])
            for mo in range(KT):
                slot = mo % 2
                wt = p.dma("pool", wsl[slot][:, :], WB["wob"][mo], f"wsl{slot}",
                           deps=wslb[slot].need_for_write())
                wslb[slot].wrote(wt)
                bk = fb[2 + mo % 2]
                for k in range(KT):
                    p.bop("pe", lambda e, k=k, w=wsl[slot], o=bk.ap,
                          r_=ogs[k]: e.matmul(
                              o[:, 0:512], w[:, k * 128:(k + 1) * 128],
                              r_[:, :], start=(k == 0), stop=(k == KT - 1)),
                          reads=[wslb[slot], ogsb[k]], writes=[bk])
                s1_, s1b_ = stg[1], stgb[1]
                lt = p.dma("sync", s1_[:, :], xbd[mo, :, tc * 512:(tc + 1) * 512],
                           "xbl", deps=s1b_.need_for_write() +
                           xbdb.need_for_read())
                s1b_.wrote(lt)
                xbdb.read(lt)
                s2_, s2b_ = stg[2], stgb[2]
                et = p.bop("dve", lambda e, d=s2_, a=s1_, i=bk.ap:
                           e.tensor_tensor(d[:, :], a[:, :], i[:, 0:512],
                                           OP.add), reads=[bk, s1b_],
                           writes=[s2b_])
                tg = p.dma("sync", hmid[mo, :, tc * 512:(tc + 1) * 512],
                           s2_[:, :], "hmst", deps=[et] +
                           hmidb.need_for_write())
                s2b_.read(tg)
                hmidb.wrote(tg)
    p.barrier()

    ffn_phase("b4", hmid, r3, r3b, FFM, wgate, wval, wout, True, False,
              0, 0, [r2], wcol=csb["c_l2w"], bcolbf=c_l2bbf, wcolbf=c_l2wbf)
    p.barrier()
    yb_ = Buf(y)
    ffn_phase("f2", r3, y, yb_, FFN, wq1d, wq2d, wq3d, False, True,
              0, FFN // 128, [])
    p.barrier()

    for s in list(p.cnt):
        if s in ("PE", "DV", "AC", "CC"):
            continue
        p.wait("sync", (s, p.cnt[s]))

    p.finish()
    p.es.close()
    return nc, dict(T=T, TC=TC, QC=QC, SK=SK, PK=PK, NB=NB, ST=ST)


def _tile_w(w):
    din, dout = w.shape
    kk, mm = din // 128, dout // 128
    return np.ascontiguousarray(
        w.reshape(kk, 128, mm, 128).transpose(2, 1, 0, 3).reshape(mm, 128, din))


def _col16(v):
    return np.ascontiguousarray(np.asarray(v).reshape(-1, 128).T)


def make_core_inputs(inputs, core, T=2048):
    b, half = core // 2, core % 2
    QC = T // 128
    SK = 2 * T // 512
    PK = T // 512
    x = np.asarray(inputs["L0_attn_a_x"])
    xo = np.ascontiguousarray(x[b, half * T:(half + 1) * T].T)
    xp = np.ascontiguousarray(x[b, (half - 1) * T:half * T].T) if half == 1 else xo
    m = {
        "x_own": xo.reshape(KT, 128, T),
        "x_prev": xp.reshape(KT, 128, T),
        "wqa": _tile_w(inputs["L0_attn_a_Wq"]),
        "wka": _tile_w(inputs["L0_attn_a_Wk"]),
        "wva": _tile_w(inputs["L0_attn_a_Wv"]),
        "wpa": _tile_w(inputs["L0_attn_a_Wproj"]),
        "wqb": _tile_w(inputs["L0_attn_b_Wq"]),
        "wkb": _tile_w(inputs["L0_attn_b_Wk"]),
        "wvb": _tile_w(inputs["L0_attn_b_Wv"]),
        "wob": _tile_w(inputs["L0_attn_b_Wo"]),
        "wig": _tile_w(inputs["L0_attn_b_Wig"]),
        "wog": _tile_w(inputs["L0_attn_b_Wog"]),
        "wgate": _tile_w(inputs["L0_attn_b_Wgate"]),
        "wval": _tile_w(inputs["L0_attn_b_Wval"]),
        "wout": _tile_w(inputs["L0_attn_b_Wout"]),
        "fw1": _tile_w(np.ascontiguousarray(inputs["L0_ffn_w1"].T)),
        "fw2": _tile_w(np.ascontiguousarray(inputs["L0_ffn_w2"].T)),
        "fw3": _tile_w(np.ascontiguousarray(inputs["L0_ffn_w3"].T)),
        "wgg": np.ascontiguousarray(
            inputs["L0_attn_b_Wg"].reshape(KT, 128, NH)),
        "qrw": inputs["L0_attn_a_q_rms_w"].reshape(128, 1),
        "krw": inputs["L0_attn_a_k_rms_w"].reshape(128, 1),
        "slw": inputs["L0_attn_a_subln_w"].reshape(128, 1),
        "vnw": inputs["L0_attn_b_vnw"].reshape(128, 1),
        "vnb": inputs["L0_attn_b_vnb"].reshape(128, 1),
        "mnw": inputs["L0_attn_b_mnw"].reshape(128, 1),
        "mnb": inputs["L0_attn_b_mnb"].reshape(128, 1),
        "cw": np.ascontiguousarray(
            inputs["L0_attn_b_conv_w"][:, 0, :].reshape(KT, 128, 4)
            .transpose(1, 0, 2).reshape(128, KT * 4)),
        "cb": _col16(inputs["L0_attn_b_conv_b"]),
        "bigb": _col16(inputs["L0_attn_b_big"]),
        "bogb": _col16(inputs["L0_attn_b_bog"]),
        "bgb": np.ascontiguousarray(
            np.broadcast_to(inputs["L0_attn_b_bg"][None, :], (128, NH))),
        "gnw": _col16(inputs["L0_attn_b_gnw"]),
        "gnb": _col16(inputs["L0_attn_b_gnb"]),
        "l1w": _col16(inputs["L0_attn_b_ln1w"]),
        "l1b": _col16(inputs["L0_attn_b_ln1b"]),
        "l2w": _col16(inputs["L0_attn_b_ln2w"]),
        "l2b": _col16(inputs["L0_attn_b_ln2b"]),
    }
    m = {k: np.ascontiguousarray(v, dtype=np.float32) for k, v in m.items()}

    sl = alibi_slopes(NH)
    PK_ = T // 512
    QG = (T // 128) // 4
    CL = [4 * (PK_ + g + 1) * 512 for g in range(QG)]
    s_glob = np.concatenate([np.arange(T) + (half - 1) * T,
                             np.arange(T) + half * T]).astype(np.float64)
    if half == 0:
        s_glob[:T] = -1.0e12
    cstr = np.zeros((128, sum(CL)), np.float32)
    pp = np.arange(128)
    off = 0
    for g in range(QG):
        nk = PK_ + g + 1
        for qi in range(4):
            qc = g * 4 + qi
            tg_ = half * T + qc * 128 + pp
            for kc in range(nk):
                sg_ = s_glob[kc * 512:(kc + 1) * 512]
                dlt = tg_[:, None] - sg_[None, :]
                cc = np.where(dlt >= 0, dlt, 1.0e9)
                cstr[:, off + (qi * nk + kc) * 512:
                     off + (qi * nk + kc + 1) * 512] = cc
        off += CL[g]
    m["cstrd"] = cstr
    oh = np.zeros((16, NH * 128), np.float32)
    for h in range(NH):
        oh[h, h * 128:(h + 1) * 128] = 1.0
    m["onesbf"] = np.ones((128, 128), BF)
    m["onesf"] = np.ones((1, 128), np.float32)
    m["onesc"] = np.ones((128, 1), np.float32)
    m["onehotf"] = oh
    m["identf"] = np.eye(128, dtype=np.float32)
    s_ = np.arange(128)
    m["trineg"] = np.where(s_[:, None] <= s_[None, :], 0.0, NEG).astype(np.float32)
    cv = np.zeros(8, np.float32)
    cv[0], cv[1], cv[2] = 1e-5, 1e-12, 1e-8
    m["cvb"] = np.ascontiguousarray(np.broadcast_to(cv[None, :], (128, 8)))
    m["ident"] = np.eye(128).astype(BF)
    m["maskprev"] = np.full((128, 64), float(half), np.float32)
    return m


_CACHE = {}


def kernel(**inputs):
    T = 2048
    n_cores = 8
    if "nc" not in _CACHE:
        _CACHE["nc"], _CACHE["meta"] = build(T, n_cores)
    nc = _CACHE["nc"]
    in_maps = [make_core_inputs(inputs, c, T) for c in range(n_cores)]
    res = run_bass_kernel_spmd(nc, in_maps, core_ids=list(range(n_cores)))
    out = np.empty((4, 4096, 2048), np.float32)
    for c in range(n_cores):
        b, half = c // 2, c % 2
        yc = res.results[c]["y"].reshape(D, T)
        out[b, half * T:(half + 1) * T] = yc.T
    return out


# version 05
# speedup vs baseline: 1.1969x; 1.0261x over previous
import math
import numpy as np
import ml_dtypes
import concourse.bass as bass
import concourse.mybir as mybir
from concourse.bass_utils import run_bass_kernel_spmd
from contextlib import ExitStack

F32 = mybir.dt.float32
BF16 = mybir.dt.bfloat16
OP = mybir.AluOpType
AF = mybir.ActivationFunctionType
AX = mybir.AxisListType.X

D = 2048
KT = 16
NH = 16
HD = 128
FFN = 5632
FFM = 8192
EPS = 1e-5
NEG = -1.0e30
BF = ml_dtypes.bfloat16


def alibi_slopes(n=16):
    start = 2 ** (-(2 ** -(math.log2(n) - 3)))
    return np.array([start * start ** i for i in range(n)], dtype=np.float64)


ENGS = ("pe", "dve", "act", "sync", "pool")
ESEM = {"pe": "PE", "dve": "DV", "act": "AC", "sync": "SQ", "pool": "GQ"}


class Prog:
    def __init__(self, nc):
        self.nc = nc
        self.streams = {e: [] for e in ENGS}
        self.es = ExitStack()
        self.sem = {}
        self.cnt = {}
        for name in ("SQ", "GQ", "PE", "DV", "AC", "CC"):
            self.sem[name] = self.es.enter_context(nc.semaphore(name))
            self.cnt[name] = 0
        self.nops = 0
        self.waited = {e: {} for e in ENGS}

    def raw(self, eng, fn):
        self.streams[eng].append(fn)
        self.nops += 1

    def op(self, eng, fn, inc=True, inc_n=None):
        if not inc:
            self.raw(eng, fn)
            return (ESEM[eng], self.cnt[ESEM[eng]])
        s = ESEM[eng]
        n = inc_n if inc_n is not None else (16 if eng in ("sync", "pool") else 1)
        self.cnt[s] += n
        h = self.sem[s]
        self.raw(eng, lambda e, fn=fn, h=h, n=n: fn(e).then_inc(h, n))
        return (s, self.cnt[s])

    def bop(self, eng, fn, reads=(), writes=(), inc=True):
        deps = []
        for b in reads:
            deps += b.need_for_read()
        for b in writes:
            deps += b.need_for_write()
        self.waits(eng, deps)
        tag = self.op(eng, fn, inc=inc)
        for b in reads:
            b.read(tag)
        for b in writes:
            b.wrote(tag)
        return tag

    def wait(self, eng, tag):
        if tag is None:
            return
        s, t = tag
        if t <= 0:
            return
        assert t <= self.cnt[s], f"future wait {s}:{t} > {self.cnt[s]}"
        if self.waited[eng].get(s, 0) >= t:
            return
        self.waited[eng][s] = t
        h = self.sem[s]
        self.raw(eng, lambda e, h=h, t=t: e.wait_ge(h, t))

    def waits(self, eng, tags):
        best = {}
        for tg in tags:
            if tg is None:
                continue
            s, t = tg
            if t <= 0:
                continue
            best[s] = max(best.get(s, 0), t)
        for s, t in best.items():
            self.wait(eng, (s, t))

    def tag(self, eng):
        return (ESEM[eng], self.cnt[ESEM[eng]])

    def dsem(self, key):
        if key not in self.sem:
            self.sem[key] = self.es.enter_context(self.nc.semaphore(key))
            self.cnt[key] = 0
        return key

    def dma(self, eng, dst, src, key, deps=()):
        self.waits(eng, deps)
        k = self.dsem(key)
        self.cnt[k] += 16
        h = self.sem[k]
        self.raw(eng, lambda e, d=dst, s=src, h=h: e.dma_start(d, s).then_inc(h, 16))
        return (k, self.cnt[k])

    def sb(self, name, shape, dtype=F32):
        return self.es.enter_context(self.nc.sbuf_tensor(name, shape, dtype))

    def barrier(self):
        tags = [(s, self.cnt[s]) for s in self.cnt]
        for e in ENGS:
            self.waits(e, tags)

    def finish(self):
        nc = self.nc
        with nc.Block() as block:
            @block.tensor
            def _(t):
                for fn in self.streams["pe"]:
                    fn(t)

            @block.vector
            def _(v):
                for fn in self.streams["dve"]:
                    fn(v)

            @block.scalar
            def _(s):
                for fn in self.streams["act"]:
                    fn(s)

            @block.sync
            def _(sy):
                for fn in self.streams["sync"]:
                    fn(sy)

            @block.gpsimd
            def _(g):
                for fn in self.streams["pool"]:
                    fn(g)


class Buf:

    def __init__(self, ap):
        self.ap = ap
        self.writer = None
        self.readers = []

    def need_for_write(self):
        return ([self.writer] if self.writer else []) + self.readers

    def need_for_read(self):
        return [self.writer] if self.writer else []

    def wrote(self, tag):
        self.writer = tag
        self.readers = []

    def read(self, tag):
        for i, (s, t) in enumerate(self.readers):
            if s == tag[0]:
                self.readers[i] = (s, max(t, tag[1]))
                return
        self.readers.append(tag)


def build(T=2048, n_cores=8):
    groups = [[2 * i, 2 * i + 1] for i in range(n_cores // 2)]
    TC = T // 512
    QC = T // 128
    SK = 2 * T // 512
    PK = T // 512
    NB = T // 128
    ST = 2 * T // 128

    nc = bass.Bass("TRN2", target_bir_lowering=False)
    p = Prog(nc)

    def di(name, shape, dtype=F32):
        return nc.dram_tensor(name, list(shape), dtype, kind="ExternalInput").ap()

    x_own = di("x_own", (KT, 128, T))
    x_prev = di("x_prev", (KT, 128, T))
    WA = {n: di(n, (KT, 128, D)) for n in ("wqa", "wka", "wva", "wpa")}
    WB = {n: di(n, (KT, 128, D)) for n in ("wqb", "wkb", "wvb", "wob", "wig", "wog")}
    wgg = di("wgg", (KT, 128, NH))
    wgate = di("wgate", (FFM // 128, 128, D))
    wval = di("wval", (FFM // 128, 128, D))
    wout = di("wout", (KT, 128, FFM))
    fw1 = di("fw1", (FFN // 128, 128, D))
    fw2 = di("fw2", (FFN // 128, 128, D))
    fw3 = di("fw3", (KT, 128, FFN))
    smalls = {}
    for n, w in [("qrw", 1), ("krw", 1), ("slw", 1), ("vnw", 1), ("vnb", 1),
                 ("mnw", 1), ("mnb", 1), ("cw", KT * 4), ("cb", KT),
                 ("bigb", KT), ("bogb", KT), ("bgb", NH), ("gnw", KT),
                 ("gnb", KT), ("l1w", KT), ("l1b", KT), ("l2w", KT),
                 ("l2b", KT)]:
        smalls[n] = di(n, (128, w))
    CL = [4 * (PK + g + 1) * 512 for g in range(QC // 4)]
    CO = [sum(CL[:g]) for g in range(QC // 4)]
    cstrd = di("cstrd", (128, sum(CL)))
    onesbf = di("onesbf", (128, 128), BF16)
    onesf = di("onesf", (1, 128))
    onesc = di("onesc", (128, 1))
    onehotf = di("onehotf", (16, NH * 128))
    identf = di("identf", (128, 128))
    trineg = di("trineg", (128, 128))
    cvb = di("cvb", (128, 8))
    ident = di("ident", (128, 128), BF16)
    maskprev = di("maskprev", (128, 64))

    y = nc.dram_tensor("y", [KT, 128, T], F32, kind="ExternalOutput").ap()

    def dsc(name, shape, dtype=F32):
        ap = nc.dram_tensor(name, list(shape), dtype).ap()
        return ap, Buf(ap)

    r1, r1b = dsc("r1", (KT, 128, T))
    r2, r2b = dsc("r2", (KT, 128, T))
    r3, r3b = dsc("r3", (KT, 128, T))
    xbd, xbdb = dsc("xbd", (KT, 128, T))
    hmid, hmidb = dsc("hmid", (KT, 128, T))
    accd, accdb = dsc("accd", (KT, 128, T))
    xbfd, xbfdb = dsc("xbfd", (KT, 128, 2 * T), BF16)
    hbd, hbdb = dsc("hbd", (KT, 128, T + 3), BF16)
    xcd, xcdb = dsc("xcd", (KT, 128, T), BF16)
    qad, qadb = dsc("qad", (NH, 128, T), BF16)
    kad, kadb = dsc("kad", (NH, 128, 2 * T), BF16)
    vtd, vtdb = dsc("vtd", (NH, 128, ST * 128), BF16)
    yad, yadb = dsc("yad", (KT, 128, T), BF16)
    qbd, qbdb = dsc("qbd", (NH, 128, T), BF16)
    umd, umdb = dsc("umd", (NH, 128, T), BF16)
    memd, memdb = dsc("memd", (NH, 128, T))
    wq1d, wq1db = dsc("wq1d", (FFN // 128, 128, D), BF16)
    wq2d, wq2db = dsc("wq2d", (FFN // 128, 128, D), BF16)
    wq3d, wq3db = dsc("wq3d", (KT, 128, FFN), BF16)
    csd, csdb = dsc("csd", (1, 128 * (2 * (FFN // 128) + KT)), BF16)
    cc1i, cc1ib = dsc("cc1i", (128, 64))
    cc1o, cc1ob = dsc("cc1o", (2 * 128, 64))
    cc2i, cc2ib = dsc("cc2i", (128, 16))
    cc2o, cc2ob = dsc("cc2o", (2 * 128, 16))

    csb = {}
    creq = [
            ("c_ones", onesbf, [128, 128], BF16),
            ("c_onesf", onesf, [1, 128], F32),
            ("c_onesc", onesc, [128, 1], F32),
            ("c_ohf", onehotf, [16, NH * 128], F32),
            ("c_idf", identf, [128, 128], F32),
            ("c_trin", trineg, [128, 128], F32),
            ("c_cvb", cvb, [128, 8], F32),
            ("c_id", ident, [128, 128], BF16),
            ("c_mask", maskprev, [128, 64], F32)]
    for n, src in smalls.items():
        creq.append(("c_" + n, src, list(src.shape), F32))
    for name, src, shape, dt in creq:
        t = p.sb(name, shape, dt)
        ctag = p.dma("sync", t[:, :] if len(shape) == 2 else t[:], src, "cst")
        csb[name] = t
    const_tag = ctag
    c_ones = csb["c_ones"]
    c_id = csb["c_id"]
    c_qrws = p.sb("c_qrws", [128, 1])
    p.wait("dve", const_tag)
    qrws_tag = p.op("dve", lambda e: e.tensor_scalar_mul(
        c_qrws[:, :], csb["c_qrw"][:, :], 2 ** -3.5), inc=True)
    c_qrws_b = Buf(c_qrws)
    c_qrws_b.wrote(qrws_tag)
    c_l2wbf = p.sb("c_l2wbf", [128, KT], BF16)
    c_l2bbf = p.sb("c_l2bbf", [128, KT], BF16)
    p.wait("dve", const_tag)
    t_ = p.op("dve", lambda e: e.tensor_copy(c_l2wbf[:, :], csb["c_l2w"][:, :]))
    t_ = p.op("dve", lambda e: e.tensor_copy(c_l2bbf[:, :], csb["c_l2b"][:, :]))
    c_l2wbf_b = Buf(c_l2wbf)
    c_l2wbf_b.wrote(t_)
    c_l2bbf_b = Buf(c_l2bbf)
    c_l2bbf_b.wrote(t_)

    fb = [Buf(p.es.enter_context(nc.psum_tensor(f"pb{i}", [128, 512], F32)).ap())
          for i in range(6)]
    xb = [Buf(p.es.enter_context(nc.psum_tensor(f"px{i}", [128, 1024], BF16)).ap())
          for i in range(2)]

    NWS = 12
    xt, xtb = [None] * KT, [None] * KT
    hbt, hbtb = [None] * KT, [None] * KT
    wsl, wslb = [None] * NWS, [None] * NWS
    stg, stgb = [None] * 4, [None] * 4
    stbf, stbfb = [None] * 4, [None] * 4
    rws = [None]
    rwsb = Buf(None)
    lnr, lnm = [None], [None]
    lnrb, lnmb = Buf(None), Buf(None)
    _gen_ctr = [0]

    def alloc_generic(es, nws=NWS, n_xt=KT, with_hbt=True, rows_len=None):
        g = _gen_ctr[0]
        _gen_ctr[0] += 1
        sbx = lambda n, s, d=F32: es.enter_context(
            nc.sbuf_tensor(f"{n}_g{g}", s, d))
        for k in range(KT):
            if k < n_xt:
                xt[k] = sbx(f"xt{k}", [128, 512])
                xtb[k] = Buf(xt[k])
            else:
                xt[k], xtb[k] = None, None
            if with_hbt:
                hbt[k] = sbx(f"hbt{k}", [128, 512], BF16)
                hbtb[k] = Buf(hbt[k])
            else:
                hbt[k], hbtb[k] = None, None
        for i in range(nws):
            wsl[i] = sbx(f"wsl{i}", [128, D], BF16)
            wslb[i] = Buf(wsl[i])
        for i in range(nws, NWS):
            wsl[i] = None
            wslb[i] = None
        for i in range(4):
            stg[i] = sbx(f"stg{i}", [128, 512])
            stgb[i] = Buf(stg[i])
            stbf[i] = sbx(f"stbf{i}", [128, 512], BF16)
            stbfb[i] = Buf(stbf[i])
        rws[0] = sbx("rws", [1, 7 * 512])
        rwsb.__init__(None)
        rl = rows_len if rows_len is not None else 2 * T
        lnr[0] = sbx("lnr", [1, rl])
        lnm[0] = sbx("lnm", [1, rl], BF16)
        lnrb.__init__(None)
        lnmb.__init__(None)

    colrow = p.sb("colrow", [1, 128 * 12], BF16)
    colrowb = Buf(colrow)
    NC2 = 2 * (FFN // 128) + KT

    slopes = alibi_slopes(NH)

    def evac(eng, bank, fn, dstbuf=None):
        p.waits(eng, bank.need_for_read())
        tag = p.op(eng, fn, inc=True)
        bank.read(tag)
        if dstbuf is not None:
            dstbuf.wrote(tag)
        return tag

    def rw(slot):
        return rws[0][0:1, slot * 512:(slot + 1) * 512]

    def ln_fold_stream(roff, srcap, tc, dst, dstb, doff, wcol=None):
        b = fb[0]
        sqdone = []
        for k in range(KT):
            sl = k % 2
            xl = p.dma("sync", xt[sl][:, :], srcap[k, :, tc * 512:(tc + 1) * 512],
                       f"xts{sl}", deps=xtb[sl].need_for_write())
            xtb[sl].wrote(xl)
            p.bop("dve", lambda e, d=dst[k], s=xt[sl], co=doff: e.tensor_copy(
                d[:, co:co + 512], s[:, :]), reads=[xtb[sl]], writes=[dstb[k]])
            p.bop("pe", lambda e, s=dst[k], st=(k == 0), sp=(k == KT - 1),
                  co=doff, o=b.ap: e.matmul(o[0:1, 0:512], c_ones[:, 0:1],
                                            s[:, co:co + 512],
                                            start=st, stop=sp),
                  reads=[dstb[k]], writes=[b])
            s2_ = stbf[k % 2]
            s2b_ = stbfb[k % 2]
            p.bop("act", lambda e, d=s2_, i=dst[k], co=doff: e.activation(
                d[:, :], i[:, co:co + 512], AF.Square),
                reads=[dstb[k]], writes=[s2b_])
            p.bop("pe", lambda e, d=s2_, st=(k == 0), sp=(k == KT - 1),
                  o=fb[5].ap: e.matmul(o[0:1, 0:512], c_ones[:, 0:1], d[:, :],
                                       start=st, stop=sp),
                  reads=[s2b_], writes=[fb[5]])
            if wcol is not None:
                p.bop("dve", lambda e, d=dst[k], s=xt[sl], co=doff, k=k:
                      e.tensor_scalar(d[:, co:co + 512], s[:, :],
                                      wcol[:, k:k + 1], None, OP.mult),
                      reads=[xtb[sl]], writes=[dstb[k]])
        p.bop("dve", lambda e, o=rw(0), i=b.ap: e.tensor_copy(o, i[0:1, 0:512]),
              reads=[b], writes=[rwsb])
        p.bop("dve", lambda e, o=rw(1), i=fb[5].ap: e.tensor_copy(
            o, i[0:1, 0:512]), reads=[fb[5]], writes=[rwsb])
        _ln_rows(roff)

    def _ln_rows(roff):
        p.bop("dve", lambda e, o=rw(2), i=rw(0): e.tensor_scalar_mul(o, i, 1.0 / D),
              reads=[rwsb], writes=[rwsb])
        p.bop("dve", lambda e, o=rw(3), i=rw(2): e.tensor_mul(o, i, i),
              reads=[rwsb], writes=[rwsb])
        p.bop("dve", lambda e, o=rw(4), i=rw(1): e.tensor_scalar(
            o, i, 1.0 / D, None, OP.mult), reads=[rwsb], writes=[rwsb])
        p.bop("dve", lambda e, o=rw(4), i=rw(3): e.tensor_sub(o, o, i),
              reads=[rwsb], writes=[rwsb])
        p.bop("dve", lambda e, o=lnm[0][0:1, roff:roff + 512], i=rw(2):
              e.tensor_scalar_mul(o, i, -1.0), reads=[rwsb], writes=[lnmb])
        p.bop("act", lambda e, o=rw(4): e.activation(
            o, o, AF.Ln, bias=csb["c_cvb"][0:1, 0:1]),
            reads=[rwsb], writes=[rwsb])
        p.bop("act", lambda e, o=lnr[0][0:1, roff:roff + 512], i=rw(4):
              e.activation(o, i, AF.Exp, scale=-0.5), reads=[rwsb], writes=[lnrb])

    def ln_fold_chunk(roff, goff, store_bf=True, dst=None, dstb=None,
                      doff=0):
        if dst is None:
            ct, ctb, co = hbt, hbtb, 0
        else:
            ct, ctb, co = dst, dstb, doff
        for k in range(KT):
            p.bop("dve", lambda e, d=ct[k], s=xt[k], co=co: e.tensor_copy(
                d[:, co:co + 512], s[:, :]), reads=[xtb[k]], writes=[ctb[k]])
        if store_bf and dst is None:
            deps = list(xbfdb.need_for_write())
            for k in range(KT):
                deps += ctb[k].need_for_read()
            for k in range(KT):
                tg = p.dma("sync", xbfd[k, :, goff:goff + 512], ct[k][:, :],
                           "hbt_st", deps=deps if k == 0 else ())
            for k in range(KT):
                ctb[k].read(tg)
            xbfdb.wrote(tg)
        b = fb[0]
        for k in range(KT):
            p.bop("pe", lambda e, s=ct[k], st=(k == 0), sp=(k == KT - 1),
                  co=co, o=b.ap: e.matmul(o[0:1, 0:512], c_ones[:, 0:1],
                                          s[:, co:co + 512],
                                          start=st, stop=sp),
                  reads=[ctb[k]], writes=[b])
        p.bop("dve", lambda e, o=rw(0), i=b.ap: e.tensor_copy(o, i[0:1, 0:512]),
              reads=[b], writes=[rwsb])
        for k in range(KT):
            s = stbf[k % 2]
            sb_ = stbfb[k % 2]
            p.bop("act", lambda e, d=s, i=ct[k], co=co: e.activation(
                d[:, :], i[:, co:co + 512], AF.Square),
                reads=[ctb[k]], writes=[sb_])
            p.bop("pe", lambda e, d=s, st=(k == 0), sp=(k == KT - 1), o=b.ap:
                  e.matmul(o[0:1, 0:512], c_ones[:, 0:1], d[:, :],
                           start=st, stop=sp), reads=[sb_], writes=[b])
        p.bop("dve", lambda e, o=rw(1), i=b.ap: e.tensor_copy(o, i[0:1, 0:512]),
              reads=[b], writes=[rwsb])
        _ln_rows(roff)

    def colsum(wi, csl):
        b = fb[5]
        for k in range(KT):
            p.bop("pe", lambda e, k=k, w=wsl[wi]: e.matmul(
                b.ap[0:1, 0:128], c_ones[:, 0:1], w[:, k * 128:(k + 1) * 128],
                start=(k == 0), stop=(k == KT - 1)), reads=[wslb[wi]], writes=[b])
        return p.bop("dve", lambda e, o=colrow[0:1, csl * 128:(csl + 1) * 128]:
                     e.tensor_copy(o, b.ap[0:1, 0:128]),
                     reads=[b], writes=[colrowb])

    def phase_A():
        for sc in range(SK):
            is_own = sc >= PK
            srcx = x_own if is_own else x_prev
            t0 = (sc - PK) * 512 if is_own else sc * 512
            deps = []
            for k in range(KT):
                deps += xtb[k].need_for_write()
            for k in range(KT):
                xl = p.dma("sync", xt[k][:, :], srcx[k, :, t0:t0 + 512], "xt",
                           deps=deps if k == 0 else ())
            for k in range(KT):
                xtb[k].wrote(xl)
            ln_fold_chunk(sc * 512, sc * 512)
        for hg in range(NH // 4):
            heads = range(hg * 4, hg * 4 + 4)
            for hi, h in enumerate(heads):
                for wi2, (nm, wsrc) in enumerate([("k", WA["wka"]),
                                                  ("v", WA["wva"]),
                                                  ("q", WA["wqa"])]):
                    slot = hi * 3 + wi2
                    tg = p.dma("pool", wsl[slot][:, :], wsrc[h], f"wsl{slot}",
                               deps=wslb[slot].need_for_write())
                    wslb[slot].wrote(tg)
                    colsum(slot, slot)
            for sc in range(SK):
                is_own = sc >= PK
                roff = sc * 512
                deps = list(xbfdb.need_for_read())
                for k in range(KT):
                    deps += hbtb[k].need_for_write()
                for k in range(KT):
                    xbl = p.dma("sync", hbt[k][:, :],
                                xbfd[k, :, roff:roff + 512], "hbt",
                                deps=deps if k == 0 else ())
                for k in range(KT):
                    hbtb[k].wrote(xbl)
                xbfdb.read(xbl)
                for hi, h in enumerate(heads):
                    projs = [("k", hi * 3), ("v", hi * 3 + 1)]
                    if is_own:
                        projs.append(("q", hi * 3 + 2))
                    for nm, slot in projs:
                        bank = fb[3 if (h + sc) % 2 == 0 else 4]
                        for k in range(KT):
                            p.bop("pe", lambda e, k=k, o=bank.ap[:, 0:512],
                                  w=wsl[slot], r=hbt[k]: e.matmul(
                                      o, w[:, k * 128:(k + 1) * 128], r[:, :],
                                      start=(k == 0), stop=False),
                                  reads=[wslb[slot], hbtb[k]], writes=[bank])
                        p.bop("pe", lambda e, o=bank.ap[:, 0:512],
                              l=colrow[0:1, slot * 128:(slot + 1) * 128],
                              r=lnm[0][0:1, roff:roff + 512]: e.matmul(
                                  o, l, r, start=False, stop=True),
                              reads=[colrowb, lnmb], writes=[bank])
                        if nm == "v":
                            a_v_epilogue(h, sc, bank, roff)
                        else:
                            a_qk_epilogue(nm, h, sc, bank, roff)

    def a_qk_epilogue(nm, h, sc, bank, roff):
        sb_, sbb = stbf[2], stbfb[2]
        p.bop("act", lambda e, d=sb_, i=bank.ap: e.activation(
            d[:, :], i[:, 0:512], AF.Square), reads=[bank], writes=[sbb])
        b2 = fb[5]
        p.bop("pe", lambda e, d=sb_, o=b2.ap: e.matmul(
            o[0:1, 0:512], c_ones[:, 0:1], d[:, :], start=True, stop=True),
            reads=[sbb], writes=[b2])
        p.bop("dve", lambda e, o=rw(5), i=b2.ap: e.tensor_copy(o, i[0:1, 0:512]),
              reads=[b2], writes=[rwsb])
        p.bop("dve", lambda e, o=rw(6), r_=lnr[0][0:1, roff:roff + 512]:
              e.tensor_mul(o, r_, r_), reads=[lnrb, rwsb], writes=[rwsb])
        p.bop("dve", lambda e, o=rw(6), i=rw(5): e.tensor_mul(o, o, i),
              reads=[rwsb], writes=[rwsb])
        p.bop("act", lambda e, o=rw(6): e.activation(
            o, o, AF.Ln, bias=csb["c_cvb"][0:1, 0:1], scale=1.0 / 128),
            reads=[rwsb], writes=[rwsb])
        p.bop("act", lambda e, o=rw(6): e.activation(o, o, AF.Exp, scale=-0.5),
              reads=[rwsb], writes=[rwsb])
        p.bop("dve", lambda e, o=rw(6), r_=lnr[0][0:1, roff:roff + 512]:
              e.tensor_mul(o, o, r_), reads=[lnrb, rwsb], writes=[rwsb])
        p.bop("pe", lambda e, i=rw(6), o=b2.ap: e.matmul(
            o[:, 0:512], csb["c_onesf"][0:1, :], i, start=True, stop=True),
            reads=[rwsb], writes=[b2])
        sg, sgb = stg[2], stgb[2]
        p.bop("dve", lambda e, d=sg, i=b2.ap: e.tensor_copy(d[:, :], i[:, 0:512]),
              reads=[b2], writes=[sgb])
        wcol = c_qrws if nm == "q" else csb["c_krw"]
        wbuf_ = c_qrws_b if nm == "q" else None
        dst, dstb = stbf[3], stbfb[3]
        ft = p.bop("dve", lambda e, w=wcol, d=dst, s=sg, i=bank.ap:
                   e.scalar_tensor_tensor(
            d[:, :], i[:, 0:512], w[:, 0:1], s[:, :], OP.mult, OP.mult),
            reads=[bank, sgb] + ([wbuf_] if wbuf_ else []), writes=[dstb])
        if nm == "q":
            tg = p.dma("sync", qad[h, :, roff - PK * 512:roff - PK * 512 + 512],
                       dst[:, :], "stbf3", deps=[ft] + qadb.need_for_write())
            qadb.wrote(tg)
        else:
            tg = p.dma("sync", kad[h, :, roff:roff + 512], dst[:, :], "stbf3",
                       deps=[ft] + kadb.need_for_write())
            kadb.wrote(tg)
        dstb.read(tg)

    def a_v_epilogue(h, sc, bank, roff):
        b2 = fb[5]
        p.bop("pe", lambda e, r_=lnr[0][0:1, roff:roff + 512], o=b2.ap:
              e.matmul(o[:, 0:512], csb["c_onesf"][0:1, :], r_,
                       start=True, stop=True), reads=[lnrb], writes=[b2])
        sg, sgb = stg[2], stgb[2]
        p.bop("dve", lambda e, d=sg, i=b2.ap: e.tensor_copy(d[:, :], i[:, 0:512]),
              reads=[b2], writes=[sgb])
        dst, dstb = stbf[3], stbfb[3]
        p.bop("dve", lambda e, d=dst, s=sg, i=bank.ap: e.tensor_tensor(
            d[:, :], i[:, 0:512], s[:, :], OP.mult),
            reads=[bank, sgb], writes=[dstb])
        xbk = xb[sc % 2]
        for j in range(4):
            p.bop("pe", lambda e, j=j, d=dst, o=xbk.ap: e.transpose(
                o[:, j * 128:(j + 1) * 128], d[:, j * 128:(j + 1) * 128],
                c_id[:, :]), reads=[dstb], writes=[xbk])
        vs, vsb = stbf[2], stbfb[2]
        t = p.bop("dve", lambda e, d=vs, i=xbk.ap: e.tensor_copy(
            d[:, :], i[:, 0:512]), reads=[xbk], writes=[vsb])
        tg = p.dma("sync", vtd[h, :, sc * 512:(sc + 1) * 512], vs[:, :], "stbf2",
                   deps=[t] + vtdb.need_for_write())
        vsb.read(tg)
        vtdb.wrote(tg)

    with ExitStack() as g_es:
        alloc_generic(g_es)
        phase_A()
    p.barrier()

    with ExitStack() as a1_es:
        sba1 = lambda n, s, d=F32: a1_es.enter_context(nc.sbuf_tensor(n, s, d))
        ka_s = [sba1(f"ka{i}", [128, 2 * T], BF16) for i in range(2)]
        ka_b = [Buf(t) for t in ka_s]
        vt_s = [sba1(f"vt{i}", [128, ST * 128], BF16) for i in range(2)]
        vt_b = [Buf(t) for t in vt_s]
        qa_s = [sba1(f"qa{i}", [128, 512], BF16) for i in range(2)]
        qa_b = [Buf(t) for t in qa_s]
        NKX = PK + (QC // 4 - 1) + 1
        cstr = sba1("cstr", [128, 4 * NKX * 512])
        cstrb = Buf(cstr)
        zstr = sba1("zstr", [128, NKX * 512])
        zstrb = Buf(zstr)
        pstr = sba1("pstr", [128, 4 * NKX * 512], BF16)
        pstrb = Buf(pstr)
        ptb = sba1("ptb", [128, ST * 512], BF16)
        ptbb = Buf(ptb)
        mrow = sba1("mrow", [128, 32])
        mrowb = Buf(mrow)
        arow = sba1("arow", [1, 3 * 512])
        arowb = Buf(arow)
        ydst = sba1("ydst", [128, 512], BF16)
        ydstb = Buf(ydst)
        astg = [sba1(f"astg{i}", [128, 512]) for i in range(2)]
        astgb = [Buf(t) for t in astg]
        asbf = sba1("asbf", [128, 512], BF16)
        asbfb = Buf(asbf)

        def nkc_of(qc):
            return PK + qc // 4 + 1

        def phase_A1():
            for qcg in range(QC // 4):
                qcs = list(range(qcg * 4, qcg * 4 + 4))
                nkcg = PK + qcg + 1
                ct_ = p.dma("sync", cstr[:, 0:4 * nkcg * 512],
                            cstrd[:, CO[qcg]:CO[qcg] + CL[qcg]], "cstr",
                            deps=cstrb.need_for_write())
                cstrb.wrote(ct_)
                for h in range(NH):
                    par = h % 2
                    ktag = p.dma("sync", ka_s[par][:, :], kad[h], f"ka{par}",
                                 deps=ka_b[par].need_for_write() +
                                 kadb.need_for_read())
                    ka_b[par].wrote(ktag)
                    kadb.read(ktag)
                    vtag = p.dma("sync", vt_s[par][:, :], vtd[h], f"vt{par}",
                                 deps=vt_b[par].need_for_write() +
                                 vtdb.need_for_read())
                    vt_b[par].wrote(vtag)
                    vtdb.read(vtag)
                    qtag = p.dma("sync", qa_s[par][:, 0:512],
                                 qad[h, :, qcg * 512:(qcg + 1) * 512],
                                 f"qa{par}",
                                 deps=qa_b[par].need_for_write() +
                                 qadb.need_for_read())
                    qa_b[par].wrote(qtag)
                    qadb.read(qtag)
                    kab, vtb_, qab = ka_b[par], vt_b[par], qa_b[par]
                    kaa, vta, qaa = ka_s[par], vt_s[par], qa_s[par]
                    slope = float(slopes[h])
                    for qi, qc in enumerate(qcs):
                        nk = nkc_of(qc)
                        for kc in range(nk):
                            bk = fb[2 + kc % 2]
                            p.bop("pe", lambda e, qc=qc, kc=kc, o=bk.ap,
                                  qaa=qaa, kaa=kaa, qi=qi:
                                  e.matmul(o[:, 0:512],
                                           qaa[:, qi * 128:(qi + 1) * 128],
                                           kaa[:, kc * 512:(kc + 1) * 512],
                                           start=True, stop=True),
                                  reads=[qab, kab], writes=[bk])
                            p.bop("dve", lambda e, qi=qi, kc=kc, o=bk.ap,
                                  sl=slope, nk=nk: e.scalar_tensor_tensor(
                                      zstr[:, kc * 512:(kc + 1) * 512],
                                      cstr[:, (qi * nk + kc) * 512:
                                           (qi * nk + kc + 1) * 512],
                                      -sl, o[:, 0:512], OP.mult, OP.add),
                                  reads=[bk, cstrb], writes=[zstrb])
                            p.bop("dve", lambda e, kc=kc: e.reduce_max(
                                mrow[:, 2 + kc:3 + kc],
                                zstr[:, kc * 512:(kc + 1) * 512], AX),
                                reads=[zstrb], writes=[mrowb])
                        p.bop("dve", lambda e, nk=nk: e.reduce_max(
                            mrow[:, 0:1], mrow[:, 2:2 + nk], AX),
                            reads=[mrowb], writes=[mrowb])
                        p.bop("dve", lambda e: e.tensor_scalar_mul(
                            mrow[:, 1:2], mrow[:, 0:1], -1.0),
                            reads=[mrowb], writes=[mrowb])
                        poff = qi * nkcg * 512
                        p.bop("act", lambda e, nk=nk, po=poff: e.activation(
                            pstr[:, po:po + nk * 512], zstr[:, 0:nk * 512],
                            AF.Exp, bias=mrow[:, 1:2]),
                            reads=[zstrb, mrowb], writes=[pstrb])
                    nst_max = nkc_of(qcs[-1]) * 4
                    for st in range(nst_max):
                        xbk = xb[st % 2]
                        q0 = 0
                        while nkc_of(qcs[0] + q0) * 4 <= st:
                            q0 += 1
                        for qi in range(q0, 4):
                            poff = qi * nkcg * 512
                            p.bop("pe", lambda e, st=st, qi=qi, po=poff,
                                  o=xbk.ap: e.transpose(
                                      o[:, qi * 128:(qi + 1) * 128],
                                      pstr[:, po + st * 128:po + (st + 1) * 128],
                                      c_id[:, :]), reads=[pstrb], writes=[xbk])
                        p.bop("dve", lambda e, st=st, q0=q0, o=xbk.ap:
                              e.tensor_copy(
                                  ptb[:, st * 512 + q0 * 128:(st + 1) * 512],
                                  o[:, q0 * 128:512]), reads=[xbk], writes=[ptbb])
                    yk = fb[4]
                    sk_ = fb[5]
                    for st in range(nst_max):
                        q0 = 0
                        while nkc_of(qcs[0] + q0) * 4 <= st:
                            q0 += 1
                        p.bop("pe", lambda e, st=st, q0=q0, vta=vta:
                              e.matmul(yk.ap[:, q0 * 128:512],
                                       vta[:, st * 128:(st + 1) * 128],
                                       ptb[:, st * 512 + q0 * 128:(st + 1) * 512],
                                       start=(st == 0), stop=(st == nst_max - 1)),
                              reads=[vtb_, ptbb], writes=[yk])
                        p.bop("pe", lambda e, st=st, q0=q0:
                              e.matmul(sk_.ap[0:1, q0 * 128:512],
                                       c_ones[:, 0:1],
                                       ptb[:, st * 512 + q0 * 128:(st + 1) * 512],
                                       start=(st == 0), stop=(st == nst_max - 1)),
                              reads=[ptbb], writes=[sk_])
                    p.bop("dve", lambda e: e.tensor_copy(arow[0:1, 0:512],
                                                         sk_.ap[0:1, 0:512]),
                          reads=[sk_], writes=[arowb])
                    sq_, sqb_ = asbf, asbfb
                    p.bop("act", lambda e, sq_=sq_: e.activation(
                        sq_[:, :], yk.ap[:, 0:512], AF.Square),
                        reads=[yk], writes=[sqb_])
                    p.bop("pe", lambda e, sq_=sq_: e.matmul(
                        sk_.ap[0:1, 0:512], c_ones[:, 0:1], sq_[:, :],
                        start=True, stop=True), reads=[sqb_], writes=[sk_])
                    p.bop("dve", lambda e: e.tensor_copy(arow[0:1, 512:1024],
                                                         sk_.ap[0:1, 0:512]),
                          reads=[sk_], writes=[arowb])
                    p.bop("dve", lambda e: e.tensor_mul(
                        arow[0:1, 1024:1536], arow[0:1, 0:512], arow[0:1, 0:512]),
                        reads=[arowb], writes=[arowb])
                    p.bop("dve", lambda e: e.tensor_scalar(
                        arow[0:1, 1024:1536], arow[0:1, 1024:1536], EPS, None,
                        OP.mult), reads=[arowb], writes=[arowb])
                    p.bop("dve", lambda e: e.scalar_tensor_tensor(
                        arow[0:1, 1024:1536], arow[0:1, 512:1024], 1.0 / 128,
                        arow[0:1, 1024:1536], OP.mult, OP.add),
                        reads=[arowb], writes=[arowb])
                    p.bop("act", lambda e: e.activation(
                        arow[0:1, 1024:1536], arow[0:1, 1024:1536], AF.Ln),
                        reads=[arowb], writes=[arowb])
                    p.bop("act", lambda e: e.activation(
                        arow[0:1, 1024:1536], arow[0:1, 1024:1536], AF.Exp,
                        scale=-0.5), reads=[arowb], writes=[arowb])
                    fbk = fb[3]
                    p.bop("pe", lambda e: e.matmul(
                        fbk.ap[:, 0:512], csb["c_onesf"][0:1, :],
                        arow[0:1, 1024:1536], start=True, stop=True),
                        reads=[arowb], writes=[fbk])
                    sg2, sg2b = astg[0], astgb[0]
                    p.bop("dve", lambda e, sg2=sg2: e.tensor_scalar(
                        sg2[:, :], yk.ap[:, 0:512], csb["c_slw"][:, 0:1], None,
                        OP.mult), reads=[yk], writes=[sg2b])
                    ft = p.bop("dve", lambda e, sg2=sg2: e.tensor_tensor(
                        ydst[:, :], sg2[:, :], fbk.ap[:, 0:512], OP.mult),
                        reads=[sg2b, fbk], writes=[ydstb])
                    tg = p.dma("sync", yad[h, :, qcg * 512:(qcg + 1) * 512],
                               ydst[:, :], "ydst",
                               deps=[ft] + yadb.need_for_write())
                    ydstb.read(tg)
                    yadb.wrote(tg)

        phase_A1()
    p.barrier()

    if True:
        with ExitStack() as a2_es:
            alloc_generic(a2_es, nws=2)
            ya_s = [a2_es.enter_context(nc.sbuf_tensor(f"ya{k}", [128, T], BF16))
                    for k in range(KT)]
            ya_b = [Buf(t) for t in ya_s]
            deps = list(yadb.need_for_read())
            for k in range(KT):
                deps += ya_b[k].need_for_write()
            for k in range(KT):
                tg = p.dma("sync", ya_s[k][:, :], yad[k], "ya_l",
                           deps=deps if k == 0 else ())
            for k in range(KT):
                ya_b[k].wrote(tg)
            yadb.read(tg)
            for m in range(KT):
                slot = m % 2
                wt = p.dma("pool", wsl[slot][:, :], WA["wpa"][m], f"wsl{slot}",
                           deps=wslb[slot].need_for_write())
                wslb[slot].wrote(wt)
                for tc in range(TC):
                    bk = fb[tc % 2]
                    for k in range(KT):
                        p.bop("pe", lambda e, k=k, tc=tc, o=bk.ap, w=wsl[slot],
                              ya=ya_s[k]:
                              e.matmul(o[:, 0:512],
                                       w[:, k * 128:(k + 1) * 128],
                                       ya[:, tc * 512:(tc + 1) * 512],
                                       start=(k == 0), stop=(k == KT - 1)),
                              reads=[wslb[slot], ya_b[k]], writes=[bk])
                    xs, xsb = stg[tc % 2], stgb[tc % 2]
                    lt = p.dma("sync", xs[:, :], x_own[m, :, tc * 512:(tc + 1) * 512],
                               f"a2x{tc % 2}", deps=xsb.need_for_write())
                    xsb.wrote(lt)
                    os_, osb = stg[2 + tc % 2], stgb[2 + tc % 2]
                    et = p.bop("dve", lambda e, o=bk.ap, xs=xs, os_=os_:
                               e.tensor_tensor(
                        os_[:, :], xs[:, :], o[:, 0:512], OP.add),
                        reads=[bk, xsb], writes=[osb])
                    tg = p.dma("sync", r1[m, :, tc * 512:(tc + 1) * 512],
                               os_[:, :], f"a2o{tc % 2}",
                               deps=[et] + r1b.need_for_write())
                    osb.read(tg)
                    r1b.wrote(tg)
        p.barrier()

    with ExitStack() as q_es:
        sbq = lambda n, s, d=F32: q_es.enter_context(nc.sbuf_tensor(n, s, d))
        qw2 = [sbq(f"qw{i}", [128, FFN]) for i in range(2)]
        qw2b = [Buf(t) for t in qw2]
        qa2 = sbq("qa2", [128, FFN], BF16)
        qa2b = Buf(qa2)
        qn_ = sbq("qn", [128, FFN], BF16)
        qnb = Buf(qn_)
        gac = sbq("gac", [128, 48])
        gacb = Buf(gac)
        trow = sbq("trow", [1, 8])
        trowb = Buf(trow)
        tcol = sbq("tcol", [128, 2])
        tcolb = Buf(tcol)
        csrow = sbq("csrow", [1, 128], BF16)
        csrowb = Buf(csrow)

        def quantize(wsrc, wdst, wdstb, mm, kk, cs0):
            wid = kk * 128
            nelem = mm * 128 * wid
            for m in range(mm):
                sl_ = m % 2
                qw, qwb = qw2[sl_], qw2b[sl_]
                tg = p.dma("sync", qw[:, 0:wid], wsrc[m], f"qw{sl_}",
                           deps=qwb.need_for_write())
                qwb.wrote(tg)
                p.bop("act", lambda e, m=m, w=wid, qw=qw: e.activation(
                    qw[:, 0:w], qw[:, 0:w], AF.Abs,
                    accum_out=gac[:, m:m + 1]), reads=[qwb], writes=[qwb, gacb])
            p.bop("dve", lambda e, mm=mm: e.reduce_sum(
                gac[:, 47:48], gac[:, 0:mm], AX), reads=[gacb], writes=[gacb])
            b = fb[0]
            p.bop("pe", lambda e, o=b.ap: e.matmul(
                o[0:1, 0:1], csb["c_onesc"][:, 0:1], gac[:, 47:48],
                start=True, stop=True), reads=[gacb], writes=[b])
            p.bop("dve", lambda e, o=b.ap, n=nelem: e.tensor_scalar(
                trow[0:1, 0:1], o[0:1, 0:1], 0.5 / n, None, OP.mult),
                reads=[b], writes=[trowb])
            p.bop("dve", lambda e: e.tensor_scalar(
                trow[0:1, 0:1], trow[0:1, 0:1], 0.5e-8, None, OP.add),
                reads=[trowb], writes=[trowb])
            b2 = fb[1]
            p.bop("pe", lambda e, o=b2.ap: e.matmul(
                o[:, 0:1], csb["c_onesf"][0:1, :], trow[0:1, 0:1],
                start=True, stop=True), reads=[trowb], writes=[b2])
            p.bop("dve", lambda e, i=b2.ap: e.tensor_copy(tcol[:, 0:1],
                                                          i[:, 0:1]),
                  reads=[b2], writes=[tcolb])
            p.bop("dve", lambda e: e.tensor_scalar_mul(
                tcol[:, 1:2], tcol[:, 0:1], -1.0), reads=[tcolb], writes=[tcolb])
            for m in range(mm):
                sl_ = m % 2
                qw, qwb = qw2[sl_], qw2b[sl_]
                tg = p.dma("sync", qw[:, 0:wid], wsrc[m], f"qw{sl_}",
                           deps=qwb.need_for_write())
                qwb.wrote(tg)
                p.bop("dve", lambda e, w=wid, qw=qw: e.tensor_scalar(
                    qa2[:, 0:w], qw[:, 0:w], tcol[:, 0:1], None, OP.is_ge),
                    reads=[qwb, tcolb], writes=[qa2b])
                ft = p.bop("dve", lambda e, w=wid, qw=qw: e.scalar_tensor_tensor(
                    qn_[:, 0:w], qw[:, 0:w], tcol[:, 1:2], qa2[:, 0:w],
                    OP.is_le, OP.subtract), reads=[qwb, tcolb, qa2b],
                    writes=[qnb])
                tg = p.dma("sync", wdst[m], qn_[:, 0:wid], "qn",
                           deps=[ft] + wdstb.need_for_write())
                qnb.read(tg)
                wdstb.wrote(tg)
                bcs = fb[2]
                for k in range(kk):
                    p.bop("pe", lambda e, k=k, o=bcs.ap, st=(k == 0),
                          sp=(k == kk - 1): e.matmul(
                              o[0:1, 0:128], c_ones[:, 0:1],
                              qn_[:, k * 128:(k + 1) * 128], start=st, stop=sp),
                          reads=[qnb], writes=[bcs])
                ct_ = p.bop("dve", lambda e, o=csrow[0:1, 0:128],
                            i=bcs.ap: e.tensor_copy(o, i[0:1, 0:128]),
                            reads=[bcs], writes=[csrowb])
                tg = p.dma("sync", csd[0:1, (cs0 + m) * 128:(cs0 + m + 1) * 128],
                           csrow[0:1, 0:128], "csr",
                           deps=[ct_] + csdb.need_for_write())
                csrowb.read(tg)
                csdb.wrote(tg)

        quantize(fw1, wq1d, wq1db, FFN // 128, KT, 0)
        quantize(fw2, wq2d, wq2db, FFN // 128, KT, FFN // 128)
        quantize(fw3, wq3d, wq3db, KT, FFN // 128, 2 * (FFN // 128))
    p.barrier()

    def ffn_phase(tag, xin, out, outb, FF, w_gate, w_val, w_out, gate_silu,
                  quantized, cs_g0, cs_v0, extras, wcol=None, bcolbf=None,
                  wcolbf=None):
        MM = FF // 128
        NQ = 4 if FF == FFN else 8
        MQ = MM // NQ
        with ExitStack() as f_es:
            sbf = lambda n, s, d=F32: f_es.enter_context(
                nc.sbuf_tensor(f"{n}_{tag}", s, d))
            alloc_generic(f_es, nws=4, n_xt=2, with_hbt=False,
                          rows_len=T)
            xbff = [sbf(f"xbf{k}", [128, T], BF16) for k in range(KT)]
            xbffb = [Buf(t) for t in xbff]
            rbt = [sbf(f"rbt{t}", [128, 512]) for t in range(TC)]
            rbtb = [Buf(t) for t in rbt]
            hstr = [sbf(f"hs{i}", [128, T], BF16) for i in range(MQ)]
            hstrb = [Buf(t) for t in hstr]
            sgt = [sbf(f"sgt{i}", [128, 512]) for i in range(4)]
            sgtb = [Buf(t) for t in sgt]
            bwc = sbf("bwc", [128, 4])
            bwcb = Buf(bwc)
            crq = [sbf(f"crq{i}", [1, MQ * 128], BF16) for i in range(2)]
            crqb = [Buf(t) for t in crq]
            wob2 = [sbf(f"wob{i}", [128, MQ * 128], BF16) for i in range(2)]
            wob2b = [Buf(t) for t in wob2]
            for tc in range(TC):
                ln_fold_stream(tc * 512, xin, tc, xbff, xbffb, tc * 512,
                               wcol=wcol)
                bk = fb[1]
                p.bop("pe", lambda e, tc=tc, o=bk.ap,
                      r_=lnr[0][0:1, tc * 512:(tc + 1) * 512]: e.matmul(
                          o[:, 0:512], csb["c_onesf"][0:1, :], r_,
                          start=True, stop=True), reads=[lnrb], writes=[bk])
                p.bop("dve", lambda e, d=rbt[tc], i=bk.ap: e.tensor_copy(
                    d[:, :], i[:, 0:512]), reads=[bk], writes=[rbtb[tc]])
            for q in range(4):
                for mi in range(MQ):
                    m = q * MQ + mi
                    gsl, vsl = mi % 2, 2 + mi % 2
                    dmaq = "sync" if quantized else "pool"
                    sfx = "s" if quantized else ""
                    gt = p.dma(dmaq, wsl[gsl][:, :], w_gate[m], f"wsl{gsl}{sfx}",
                               deps=wslb[gsl].need_for_write())
                    wslb[gsl].wrote(gt)
                    vt = p.dma(dmaq, wsl[vsl][:, :], w_val[m], f"wsl{vsl}{sfx}",
                               deps=wslb[vsl].need_for_write())
                    wslb[vsl].wrote(vt)
                    if quantized:
                        if mi == 0:
                            for ci, c0 in ((0, cs_g0), (1, cs_v0)):
                                ct_ = p.dma("sync", crq[ci][0:1, :],
                                            csd[0:1, (c0 + q * MQ) * 128:
                                                (c0 + q * MQ + MQ) * 128],
                                            f"crq{ci}",
                                            deps=crqb[ci].need_for_write() +
                                            csdb.need_for_read())
                                crqb[ci].wrote(ct_)
                                csdb.read(ct_)
                        csg = (crq[0][0:1, mi * 128:(mi + 1) * 128], crqb[0])
                        csv = (crq[1][0:1, mi * 128:(mi + 1) * 128], crqb[1])
                    else:
                        for wi, sl, c0 in ((0, gsl, 8), (1, vsl, 9)):
                            bcs = fb[4]
                            lh = wcolbf if wcolbf is not None else c_ones
                            lsl = (lambda k: lh[:, k:k + 1]) if wcolbf is not None \
                                else (lambda k: c_ones[:, 0:1])
                            for k in range(KT):
                                p.bop("pe", lambda e, k=k, w=wsl[sl], o=bcs.ap,
                                      l=lsl(k), st=(k == 0), sp=(k == KT - 1):
                                      e.matmul(o[0:1, 0:128], l,
                                               w[:, k * 128:(k + 1) * 128],
                                               start=st, stop=sp),
                                      reads=[wslb[sl]], writes=[bcs])
                            p.bop("dve", lambda e, o=colrow[0:1, c0 * 128:
                                                           (c0 + 1) * 128],
                                  i=bcs.ap: e.tensor_copy(o, i[0:1, 0:128]),
                                  reads=[bcs], writes=[colrowb])
                            if bcolbf is not None:
                                bcs2 = fb[5]
                                for k in range(KT):
                                    p.bop("pe", lambda e, k=k, w=wsl[sl], o=bcs2.ap,
                                          st=(k == 0), sp=(k == KT - 1):
                                          e.matmul(o[:, 0:1],
                                                   w[:, k * 128:(k + 1) * 128],
                                                   bcolbf[:, k:k + 1],
                                                   start=st, stop=sp),
                                          reads=[wslb[sl]], writes=[bcs2])
                                p.bop("dve", lambda e, o=bwc[:, wi:wi + 1],
                                      i=bcs2.ap: e.tensor_copy(o, i[:, 0:1]),
                                      reads=[bcs2], writes=[bwcb])
                        csg = (colrow[0:1, 8 * 128:9 * 128], colrowb)
                        csv = (colrow[0:1, 9 * 128:10 * 128], colrowb)
                    for tcp in range((TC + 1) // 2):
                        tcl = [t for t in (2 * tcp, 2 * tcp + 1) if t < TC]
                        pg = [fb[0], fb[1]]
                        pv = [fb[2], fb[3]]
                        for k in range(KT):
                            for ti, tc in enumerate(tcl):
                                p.bop("pe", lambda e, k=k, w=wsl[gsl],
                                      o=pg[ti].ap, tc=tc, xb_=xbff[k]:
                                      e.matmul(o[:, 0:512],
                                               w[:, k * 128:(k + 1) * 128],
                                               xb_[:, tc * 512:(tc + 1) * 512],
                                               start=(k == 0), stop=False),
                                      reads=[wslb[gsl], xbffb[k]],
                                      writes=[pg[ti]])
                            for ti, tc in enumerate(tcl):
                                p.bop("pe", lambda e, k=k, w=wsl[vsl],
                                      o=pv[ti].ap, tc=tc, xb_=xbff[k]:
                                      e.matmul(o[:, 0:512],
                                               w[:, k * 128:(k + 1) * 128],
                                               xb_[:, tc * 512:(tc + 1) * 512],
                                               start=(k == 0), stop=False),
                                      reads=[wslb[vsl], xbffb[k]],
                                      writes=[pv[ti]])
                        for ti, tc in enumerate(tcl):
                            for bk, cs in ((pg[ti], csg), (pv[ti], csv)):
                                p.bop("pe", lambda e, o=bk.ap, l=cs[0],
                                      r_=lnm[0][0:1, tc * 512:(tc + 1) * 512]:
                                      e.matmul(o[:, 0:512], l, r_, start=False,
                                               stop=True),
                                      reads=[cs[1], lnmb], writes=[bk])
                        for ti, tc in enumerate(tcl):
                            bg, bv = pg[ti], pv[ti]
                            e1, e1b = sgt[ti % 2], sgtb[ti % 2]
                            e2, e2b = sgt[2 + ti % 2], sgtb[2 + ti % 2]
                            p.bop("dve", lambda e, d=e1, i=bg.ap, r_=rbt[tc]:
                                  e.tensor_tensor(d[:, :], i[:, 0:512],
                                                  r_[:, :], OP.mult),
                                  reads=[bg, rbtb[tc]], writes=[e1b])
                            af_ = AF.Silu if gate_silu else AF.Sigmoid
                            sc_ = -1.0 if quantized else 1.0
                            if bcolbf is not None:
                                p.bop("act", lambda e, d=e1, af=af_, sc=sc_:
                                      e.activation(d[:, :], d[:, :], af,
                                                   bias=bwc[:, 0:1], scale=sc),
                                      reads=[e1b, bwcb], writes=[e1b])
                            else:
                                p.bop("act", lambda e, d=e1, af=af_, sc=sc_:
                                      e.activation(d[:, :], d[:, :], af,
                                                   scale=sc),
                                      reads=[e1b], writes=[e1b])
                            p.bop("dve", lambda e, d=e2, i=bv.ap, r_=rbt[tc]:
                                  e.tensor_tensor(d[:, :], i[:, 0:512],
                                                  r_[:, :], OP.mult),
                                  reads=[bv, rbtb[tc]], writes=[e2b])
                            if bcolbf is not None:
                                p.bop("dve", lambda e, d=hstr[mi], tc=tc, e1=e1,
                                      e2=e2: e.scalar_tensor_tensor(
                                          d[:, tc * 512:(tc + 1) * 512],
                                          e2[:, :], bwc[:, 1:2], e1[:, :],
                                          OP.add, OP.mult),
                                      reads=[e1b, e2b, bwcb],
                                      writes=[hstrb[mi]])
                            else:
                                p.bop("dve", lambda e, d=hstr[mi], tc=tc, e1=e1,
                                      e2=e2: e.tensor_tensor(
                                          d[:, tc * 512:(tc + 1) * 512],
                                          e1[:, :], e2[:, :], OP.mult),
                                      reads=[e1b, e2b], writes=[hstrb[mi]])
                for mo in range(KT):
                    sl_ = mo % 2
                    if mo == 0:
                        wt = p.dma("sync" if quantized else "pool",
                                   wob2[0][:, :],
                                   w_out[0, :, q * MQ * 128:(q + 1) * MQ * 128],
                                   "wobs0" if quantized else "wob0",
                                   deps=wob2b[0].need_for_write())
                        wob2b[0].wrote(wt)
                    if mo + 1 < KT:
                        nsl = (mo + 1) % 2
                        wt = p.dma("sync" if quantized else "pool",
                                   wob2[nsl][:, :],
                                   w_out[mo + 1, :,
                                         q * MQ * 128:(q + 1) * MQ * 128],
                                   f"wobs{nsl}" if quantized else f"wob{nsl}",
                                   deps=wob2b[nsl].need_for_write())
                        wob2b[nsl].wrote(wt)
                    pb4 = [fb[0], fb[1], fb[2], fb[3]]
                    for ki in range(MQ):
                        for tc in range(TC):
                            p.bop("pe", lambda e, ki=ki, o=pb4[tc].ap, tc=tc,
                                  w=wob2[sl_]:
                                  e.matmul(o[:, 0:512],
                                           w[:, ki * 128:(ki + 1) * 128],
                                           hstr[ki][:, tc * 512:(tc + 1) * 512],
                                           start=(ki == 0), stop=(ki == MQ - 1)),
                                  reads=[wob2b[sl_], hstrb[ki]],
                                  writes=[pb4[tc]])
                    for tc in range(TC):
                        bk = pb4[tc]
                        s1, s1b = stg[tc % 2], stgb[tc % 2]
                        s2, s2b = stg[2 + tc % 2], stgb[2 + tc % 2]
                        if q == 0:
                            et = p.bop("dve", lambda e, d=s2, i=bk.ap:
                                       e.tensor_copy(d[:, :], i[:, 0:512]),
                                       reads=[bk], writes=[s2b])
                            tg = p.dma("sync",
                                       accd[mo, :, tc * 512:(tc + 1) * 512],
                                       s2[:, :], f"fo{tc % 2}",
                                       deps=[et] + accdb.need_for_write())
                            s2b.read(tg)
                            accdb.wrote(tg)
                        else:
                            lt = p.dma("sync", s1[:, :],
                                       accd[mo, :, tc * 512:(tc + 1) * 512],
                                       f"fa{tc % 2}",
                                       deps=s1b.need_for_write() +
                                       accdb.need_for_read())
                            s1b.wrote(lt)
                            accdb.read(lt)
                            et = p.bop("dve", lambda e, d=s2, a=s1, i=bk.ap:
                                       e.tensor_tensor(d[:, :], a[:, :],
                                                       i[:, 0:512], OP.add),
                                       reads=[bk, s1b], writes=[s2b])
                            if q < 3:
                                tg = p.dma("sync",
                                           accd[mo, :, tc * 512:(tc + 1) * 512],
                                           s2[:, :], f"fo{tc % 2}",
                                           deps=[et] + accdb.need_for_write())
                                s2b.read(tg)
                                accdb.wrote(tg)
                            else:
                                for xs in [xin] + extras:
                                    lt = p.dma("sync", s1[:, :],
                                               xs[mo, :, tc * 512:(tc + 1) * 512],
                                               f"fa{tc % 2}",
                                               deps=s1b.need_for_write())
                                    s1b.wrote(lt)
                                    et = p.bop("dve", lambda e, d=s2, a=s1:
                                               e.tensor_add(d[:, :], d[:, :],
                                                            a[:, :]),
                                               reads=[s1b, s2b], writes=[s2b])
                                tg = p.dma("sync",
                                           out[mo, :, tc * 512:(tc + 1) * 512],
                                           s2[:, :], f"fo{tc % 2}",
                                           deps=[et] + outb.need_for_write())
                                s2b.read(tg)
                                outb.wrote(tg)

    ffn_phase("f1", r1, r2, r2b, FFN, wq1d, wq2d, wq3d, False, True,
              0, FFN // 128, [])
    p.barrier()

    halosb = p.sb("halosb", [128, 64])
    halosbb = Buf(halosb)
    mt_ = p.bop("dve", lambda e: e.memset(halosb[:, :], 0.0), writes=[halosbb])
    with ExitStack() as b0_es:
        alloc_generic(b0_es, nws=0)
        sb0 = lambda n, s, d=F32: b0_es.enter_context(nc.sbuf_tensor(n, s, d))
        xbt = [sb0(f"xbt{k}", [128, 512]) for k in range(KT)]
        xbtb = [Buf(t) for t in xbt]
        h1t = [sb0(f"h1t{k}", [128, 512], BF16) for k in range(KT)]
        h1tb = [Buf(t) for t in h1t]
        for tc in range(TC):
            deps = []
            for k in range(KT):
                deps += xtb[k].need_for_write()
            for k in range(KT):
                xl = p.dma("sync", xt[k][:, :], r2[k, :, tc * 512:(tc + 1) * 512],
                           "xt", deps=deps if k == 0 else ())
            for k in range(KT):
                xtb[k].wrote(xl)
            ln_fold_chunk(tc * 512, 0, store_bf=False)
            ro = tc * 512
            p.bop("dve", lambda e, o=rw(5), r_=lnr[0][0:1, ro:ro + 512]:
                  e.tensor_mul(o, r_, r_), reads=[lnrb, rwsb], writes=[rwsb])
            p.bop("dve", lambda e, o=rw(5): e.tensor_scalar(
                o, o, -EPS, 1.0 + EPS, OP.mult, OP.add),
                reads=[rwsb], writes=[rwsb])
            p.bop("act", lambda e, o=rw(5): e.activation(o, o, AF.Ln),
                  reads=[rwsb], writes=[rwsb])
            p.bop("act", lambda e, o=rw(5): e.activation(o, o, AF.Exp,
                                                         scale=-0.5),
                  reads=[rwsb], writes=[rwsb])
            p.bop("dve", lambda e, o=rw(6), r_=lnr[0][0:1, ro:ro + 512],
                  nm=lnm[0][0:1, ro:ro + 512]: e.tensor_tensor(
                      o, r_, nm, OP.mult), reads=[lnrb, lnmb, rwsb],
                  writes=[rwsb])
            for bi, off in ((1, None), (2, 6), (3, 5)):
                bk = fb[bi]
                if off is None:
                    p.bop("pe", lambda e, o=bk.ap,
                          r_=lnr[0][0:1, ro:ro + 512]: e.matmul(
                              o[:, 0:512], csb["c_onesf"][0:1, :], r_,
                              start=True, stop=True),
                          reads=[lnrb], writes=[bk])
                else:
                    p.bop("pe", lambda e, o=bk.ap, i=rw(off): e.matmul(
                        o[:, 0:512], csb["c_onesf"][0:1, :], i,
                        start=True, stop=True), reads=[rwsb], writes=[bk])
            sb_s2 = stg[0]
            p.bop("dve", lambda e, d=sb_s2, i=fb[3].ap: e.tensor_copy(
                d[:, :], i[:, 0:512]), reads=[fb[3]], writes=[stgb[0]])
            for k in range(KT):
                p.bop("dve", lambda e, d=xbt[k], s=xt[k], i=fb[1].ap:
                      e.tensor_tensor(d[:, :], s[:, :], i[:, 0:512], OP.mult),
                      reads=[xtb[k], fb[1]], writes=[xbtb[k]])
                et = p.bop("dve", lambda e, d=xbt[k], i=fb[2].ap:
                           e.tensor_tensor(d[:, :], d[:, :], i[:, 0:512],
                                           OP.add), reads=[fb[2], xbtb[k]],
                           writes=[xbtb[k]])
                tg = p.dma("sync", xbd[k, :, tc * 512:(tc + 1) * 512],
                           xbt[k][:, :], "xbt", deps=([et] +
                           xbdb.need_for_write() if k == 0 else [et]))
                xbtb[k].read(tg)
                p.bop("dve", lambda e, d=stg[1], s=xbt[k], s2=sb_s2:
                      e.tensor_tensor(d[:, :], s[:, :], s2[:, :], OP.mult),
                      reads=[xbtb[k], stgb[0]], writes=[stgb[1]])
                ht_ = p.bop("dve", lambda e, d=h1t[k], s=stg[1], k=k:
                            e.tensor_scalar(d[:, :], s[:, :],
                                            csb["c_l1w"][:, k:k + 1],
                                            csb["c_l1b"][:, k:k + 1],
                                            OP.mult, OP.add),
                            reads=[stgb[1]], writes=[h1tb[k]])
                tg = p.dma("sync", hbd[k, :, 3 + tc * 512:3 + (tc + 1) * 512],
                           h1t[k][:, :], "h1t",
                           deps=[ht_] + (hbdb.need_for_write() if k == 0 else []))
                h1tb[k].read(tg)
                if tc == TC - 1:
                    p.bop("dve", lambda e, k=k, s=h1t[k]: e.tensor_copy(
                        halosb[:, k * 3:k * 3 + 3], s[:, 509:512]),
                        reads=[h1tb[k]], writes=[halosbb])
            xbdb.wrote(tg)
            hbdb.wrote(tg)
    ht_ = p.dma("sync", cc1i[:, :], halosb[:, :], "cc1",
                deps=halosbb.need_for_read() + cc1ib.need_for_write())
    halosbb.read(ht_)
    cc1ib.wrote(ht_)
    p.barrier()

    def collective(cin, cout):
        p.cnt["CC"] += 1
        h = p.sem["CC"]
        p.raw("pool", lambda e, h=h, ci=cin, co=cout: e.collective_compute(
            "AllGather", OP.bypass, replica_groups=groups,
            ins=[ci.opt()], outs=[co.opt()]).then_inc(h, 1))
        return ("CC", p.cnt["CC"])

    cct1 = collective(cc1i, cc1o)
    tg = p.dma("sync", halosb[:, 0:64], cc1o[0:128, :], "cc1l", deps=[cct1])
    halosbb.wrote(tg)
    hm = p.bop("dve", lambda e: e.tensor_tensor(
        halosb[:, 0:64], halosb[:, 0:64], csb["c_mask"][:, 0:64], OP.mult),
        reads=[halosbb], writes=[halosbb])
    with ExitStack() as hc_es:
        halobf = hc_es.enter_context(nc.sbuf_tensor("halobf", [128, 64], BF16))
        halobfb = Buf(halobf)
        hb_ = p.bop("dve", lambda e: e.tensor_copy(halobf[:, :], halosb[:, :]),
                    reads=[halosbb], writes=[halobfb])
        for k in range(KT):
            tg = p.dma("sync", hbd[k, :, 0:3], halobf[:, k * 3:k * 3 + 3],
                       "halost", deps=[hb_] + (hbdb.need_for_write()
                                              if k == 0 else []))
            halobfb.read(tg)
        hbdb.wrote(tg)
    p.barrier()

    NB_ = NB
    afd, afdb = dsc("afd", (16, T))
    with ExitStack() as b1_es:
        sb1 = lambda n, s, d=F32: b1_es.enter_context(nc.sbuf_tensor(n, s, d))
        alloc_generic(b1_es, nws=8, n_xt=0, rows_len=64)
        h1in = [sb1(f"h1in{k}", [128, 515], BF16) for k in range(KT)]
        h1inb = [Buf(t) for t in h1in]
        xct = [sb1(f"xct{k}", [128, 512], BF16) for k in range(KT)]
        xctb = [Buf(t) for t in xct]
        lgam = sb1("lgam", [16, T])
        lgamb = Buf(lgam)
        cblk = sb1("cblk", [16, T])
        cblkb = Buf(cblk)
        cbl2 = sb1("cbl2", [16, T])
        cbl2b = Buf(cbl2)
        afull = sb1("afull", [16, T])
        afullb = Buf(afull)
        bsr = sb1("bsr", [16, 64])
        bsrb = Buf(bsr)
        ctn = sb1("ctn", [128, NB_ * 16])
        ctnb = Buf(ctn)
        carry = sb1("carry", [128, 16])
        carryb = Buf(carry)
        gcolt = sb1("gcolt", [128, 1])
        gcoltb = Buf(gcolt)
        wggbf = sb1("wggbf", [128, KT * NH], BF16)
        wggbfb = Buf(wggbf)
        ufm2 = [sb1(f"ufm{i}", [128, 128], BF16) for i in range(4)]
        ufm2b = [Buf(t) for t in ufm2]
        uts2 = [sb1(f"uts{i}", [128, 128], BF16) for i in range(2)]
        uts2b = [Buf(t) for t in uts2]
        wt2 = [sb1(f"wt{i}", [128, 128], BF16) for i in range(2)]
        wt2b = [Buf(t) for t in wt2]
        mst2 = [sb1(f"mst{i}", [128, 128]) for i in range(2)]
        mst2b = [Buf(t) for t in mst2]
        gcol2 = [sb1(f"gcol{i}", [128, 1]) for i in range(2)]
        gcol2b = [Buf(t) for t in gcol2]
        kbt = sb1("kbt", [128, 512], BF16)
        kbtb = Buf(kbt)
        vft = sb1("vft", [128, 512])
        vftb = Buf(vft)
        igt = sb1("igt", [128, 512])
        igtb = Buf(igt)

        wg_ = p.dma("pool", wggbf[:, :], wgg, "wggbf")
        wggbfb.wrote(wg_)

        def b_l2(bk, out_t, out_b, _bf):
            sq_, sqb_ = stbf[0], stbfb[0]
            p.bop("act", lambda e, d=sq_, i=bk.ap: e.activation(
                d[:, :], i[:, 0:512], AF.Square), reads=[bk], writes=[sqb_])
            bs = fb[4]
            p.bop("pe", lambda e, d=sq_, o=bs.ap: e.matmul(
                o[0:1, 0:512], c_ones[:, 0:1], d[:, :], start=True, stop=True),
                reads=[sqb_], writes=[bs])
            p.bop("dve", lambda e, o=rw(0), i=bs.ap: e.tensor_copy(
                o, i[0:1, 0:512]), reads=[bs], writes=[rwsb])
            p.bop("act", lambda e, o=rw(0): e.activation(
                o, o, AF.Ln, bias=csb["c_cvb"][0:1, 1:2]),
                reads=[rwsb], writes=[rwsb])
            p.bop("act", lambda e, o=rw(0): e.activation(
                o, o, AF.Exp, scale=-0.5), reads=[rwsb], writes=[rwsb])
            bb = fb[5]
            p.bop("pe", lambda e, o=bb.ap, i=rw(0): e.matmul(
                o[:, 0:512], csb["c_onesf"][0:1, :], i, start=True, stop=True),
                reads=[rwsb], writes=[bb])
            s1_, s1b_ = stg[3], stgb[3]
            p.bop("dve", lambda e, d=s1_, i=bb.ap: e.tensor_copy(
                d[:, :], i[:, 0:512]), reads=[bb], writes=[s1b_])
            p.bop("dve", lambda e, d=out_t, i=bk.ap, r_=s1_: e.tensor_tensor(
                d[:, :], i[:, 0:512], r_[:, :], OP.mult),
                reads=[bk, s1b_], writes=[out_b])

        def b_vln(bk, out_t, out_b):
            sq_, sqb_ = stbf[0], stbfb[0]
            p.bop("act", lambda e, d=sq_, i=bk.ap: e.activation(
                d[:, :], i[:, 0:512], AF.Square), reads=[bk], writes=[sqb_])
            bs = fb[4]
            p.bop("pe", lambda e, d=sq_, o=bs.ap: e.matmul(
                o[0:1, 0:512], c_ones[:, 0:1], d[:, :], start=True, stop=True),
                reads=[sqb_], writes=[bs])
            p.bop("dve", lambda e, o=rw(1), i=bs.ap: e.tensor_copy(
                o, i[0:1, 0:512]), reads=[bs], writes=[rwsb])
            s0_, s0b_ = stg[3], stgb[3]
            p.bop("dve", lambda e, d=s0_, i=bk.ap: e.tensor_copy(
                d[:, :], i[:, 0:512]), reads=[bk], writes=[s0b_])
            p.bop("pe", lambda e, d=s0_, o=bs.ap: e.matmul(
                o[0:1, 0:512], csb["c_onesc"][:, 0:1], d[:, :],
                start=True, stop=True), reads=[s0b_], writes=[bs])
            p.bop("dve", lambda e, o=rw(0), i=bs.ap: e.tensor_copy(
                o, i[0:1, 0:512]), reads=[bs], writes=[rwsb])
            p.bop("dve", lambda e, o=rw(2), i=rw(0): e.tensor_scalar_mul(
                o, i, 1.0 / 128), reads=[rwsb], writes=[rwsb])
            p.bop("dve", lambda e, o=rw(3), i=rw(2): e.tensor_mul(o, i, i),
                  reads=[rwsb], writes=[rwsb])
            p.bop("dve", lambda e, o=rw(4), i=rw(1): e.tensor_scalar(
                o, i, 1.0 / 128, None, OP.mult), reads=[rwsb], writes=[rwsb])
            p.bop("dve", lambda e, o=rw(4), i=rw(3): e.tensor_sub(o, o, i),
                  reads=[rwsb], writes=[rwsb])
            p.bop("act", lambda e, o=rw(4): e.activation(
                o, o, AF.Ln, bias=csb["c_cvb"][0:1, 0:1]),
                reads=[rwsb], writes=[rwsb])
            p.bop("act", lambda e, o=rw(4): e.activation(
                o, o, AF.Exp, scale=-0.5), reads=[rwsb], writes=[rwsb])
            p.bop("dve", lambda e, o=rw(5), a=rw(2), b_=rw(4): e.tensor_mul(
                o, a, b_), reads=[rwsb], writes=[rwsb])
            p.bop("dve", lambda e, o=rw(5), i=rw(5): e.tensor_scalar_mul(
                o, i, -1.0), reads=[rwsb], writes=[rwsb])
            b1_, b2_ = fb[4], fb[5]
            p.bop("pe", lambda e, o=b1_.ap, i=rw(4): e.matmul(
                o[:, 0:512], csb["c_onesf"][0:1, :], i, start=True, stop=True),
                reads=[rwsb], writes=[b1_])
            p.bop("pe", lambda e, o=b2_.ap, i=rw(5): e.matmul(
                o[:, 0:512], csb["c_onesf"][0:1, :], i, start=True, stop=True),
                reads=[rwsb], writes=[b2_])
            s1_, s1b_ = stg[3], stgb[3]
            p.bop("dve", lambda e, d=s1_, i=b1_.ap: e.tensor_copy(
                d[:, :], i[:, 0:512]), reads=[b1_], writes=[s1b_])
            p.bop("dve", lambda e, d=out_t, i=bk.ap, r_=s1_: e.tensor_tensor(
                d[:, :], i[:, 0:512], r_[:, :], OP.mult),
                reads=[bk, s1b_], writes=[out_b])
            p.bop("dve", lambda e, d=out_t, i=b2_.ap: e.tensor_tensor(
                d[:, :], d[:, :], i[:, 0:512], OP.add),
                reads=[b2_, out_b], writes=[out_b])
            p.bop("dve", lambda e, d=out_t: e.tensor_scalar(
                d[:, :], d[:, :], csb["c_vnw"][:, 0:1], csb["c_vnb"][:, 0:1],
                OP.mult, OP.add), reads=[out_b], writes=[out_b])

        for tc in range(TC):
            deps = []
            for k in range(KT):
                deps += h1inb[k].need_for_write()
            for k in range(KT):
                hl = p.dma("sync", h1in[k][:, :],
                           hbd[k, :, tc * 512:tc * 512 + 515], "h1in",
                           deps=(deps + hbdb.need_for_read()) if k == 0 else ())
            for k in range(KT):
                h1inb[k].wrote(hl)
            hbdb.read(hl)
            for k in range(KT):
                acc, accb = stg[k % 2], stgb[k % 2]
                p.bop("dve", lambda e, d=acc, s=h1in[k], k=k: e.tensor_scalar(
                    d[:, :], s[:, 0:512], csb["c_cw"][:, k * 4:k * 4 + 1], None,
                    OP.mult), reads=[h1inb[k]], writes=[accb])
                for j in range(1, 4):
                    p.bop("dve", lambda e, d=acc, s=h1in[k], k=k, j=j:
                          e.scalar_tensor_tensor(
                              d[:, :], s[:, j:512 + j],
                              csb["c_cw"][:, k * 4 + j:k * 4 + j + 1],
                              d[:, :], OP.mult, OP.add),
                          reads=[h1inb[k], accb], writes=[accb])
                ct_ = p.bop("act", lambda e, d=xct[k], s=acc, k=k: e.activation(
                    d[:, :], s[:, :], AF.Silu, bias=csb["c_cb"][:, k:k + 1]),
                    reads=[accb], writes=[xctb[k]])
                tg = p.dma("sync", xcd[k, :, tc * 512:(tc + 1) * 512],
                           xct[k][:, :], "xcst",
                           deps=[ct_] + (xcdb.need_for_write() if k == 0 else []))
                xctb[k].read(tg)
            xcdb.wrote(tg)
            for tk in range(4):
                bk = fb[4]
                for k in range(KT):
                    p.bop("pe", lambda e, k=k, tk=tk, o=bk.ap, s=xct[k]:
                          e.matmul(o[:, 0:16], s[:, tk * 128:(tk + 1) * 128],
                                   wggbf[:, k * NH:(k + 1) * NH],
                                   start=(k == 0), stop=(k == KT - 1)),
                          reads=[xctb[k], wggbfb], writes=[bk])
                z_, zb_ = stg[2], stgb[2]
                p.bop("dve", lambda e, d=z_, i=bk.ap: e.tensor_tensor(
                    d[:, 0:16], i[:, 0:16], csb["c_bgb"][:, :], OP.add),
                    reads=[bk], writes=[zb_])
                a_, ab_ = stg[3], stgb[3]
                p.bop("dve", lambda e, d=a_, s=z_: e.tensor_scalar(
                    d[:, 0:16], s[:, 0:16], -1.0, None, OP.mult),
                    reads=[zb_], writes=[ab_])
                p.bop("dve", lambda e, d=a_, s=z_: e.tensor_max(
                    d[:, 0:16], d[:, 0:16], s[:, 0:16]),
                    reads=[zb_, ab_], writes=[ab_])
                p.bop("act", lambda e, d=a_: e.activation(
                    d[:, 0:16], d[:, 0:16], AF.Exp, scale=-1.0),
                    reads=[ab_], writes=[ab_])
                p.bop("act", lambda e, d=a_: e.activation(
                    d[:, 0:16], d[:, 0:16], AF.Ln, bias=1.0),
                    reads=[ab_], writes=[ab_])
                p.bop("dve", lambda e, d=z_, s=z_: e.tensor_scalar(
                    d[:, 16:32], s[:, 0:16], 0.0, None, OP.min),
                    reads=[zb_], writes=[zb_])
                lg_ = p.bop("dve", lambda e, d=z_, a=a_: e.tensor_tensor(
                    d[:, 0:16], d[:, 16:32], a[:, 0:16], OP.subtract),
                    reads=[zb_, ab_], writes=[zb_])
                xbk = fb[5]
                p.bop("pe", lambda e, s=z_, o=xbk.ap: e.transpose(
                    o[0:16, 0:128], s[:, 0:16], csb["c_idf"][:, :]),
                    reads=[zb_], writes=[xbk])
                p.bop("dve", lambda e, i=xbk.ap,
                      o=lgam[0:16, tc * 512 + tk * 128:tc * 512 + (tk + 1) * 128]:
                      e.tensor_copy(o, i[0:16, 0:128]),
                      reads=[xbk], writes=[lgamb])
        def hs_step(sv, sb_, dv, db_, d_):
            p.bop("dve", lambda e, s=sv, dd=dv, d_=d_: e.tensor_add(
                dd[:, :, d_:128], s[:, :, d_:128], s[:, :, 0:128 - d_]),
                reads=[sb_], writes=[db_])
            p.bop("dve", lambda e, s=sv, dd=dv, d_=d_: e.tensor_copy(
                dd[:, :, 0:d_], s[:, :, 0:d_]), reads=[sb_], writes=[db_])

        v_lg = lgam[0:16, 0:T].rearrange("p (n j) -> p n j", j=128)
        v_c1 = cblk[0:16, 0:T].rearrange("p (n j) -> p n j", j=128)
        v_c2 = cbl2[0:16, 0:T].rearrange("p (n j) -> p n j", j=128)
        seq = [(v_lg, lgamb), (v_c1, cblkb), (v_c2, cbl2b)]
        cur = 0
        for st_i, d_ in enumerate([1, 2, 4, 8, 16, 32, 64]):
            svv, svb = seq[cur]
            dst_i = 1 if cur != 1 else 2
            if st_i == 6:
                dst_i = 1
            dvv, dvb = seq[dst_i]
            hs_step(svv, svb, dvv, dvb, d_)
            cur = dst_i
        p.bop("dve", lambda e, o=bsr[0:16, 0:NB_],
              s=cblk[0:16, 0:T].rearrange("p (n j) -> p n j", j=128)[:, :, 127:128]:
              e.tensor_copy(o, s.rearrange("p n j -> p (n j)")),
              reads=[cblkb], writes=[bsrb])
        cur_o, oth_o = 0, 16
        d_ = 1
        while d_ < NB_:
            p.bop("dve", lambda e, d_=d_, a=cur_o, b_=oth_o: e.tensor_add(
                bsr[0:16, b_ + d_:b_ + NB_], bsr[0:16, a + d_:a + NB_],
                bsr[0:16, a:a + NB_ - d_]), reads=[bsrb], writes=[bsrb])
            p.bop("dve", lambda e, d_=d_, a=cur_o, b_=oth_o: e.tensor_copy(
                bsr[0:16, b_:b_ + d_], bsr[0:16, a:a + d_]),
                reads=[bsrb], writes=[bsrb])
            cur_o, oth_o = oth_o, cur_o
            d_ *= 2
        p.bop("dve", lambda e, a=cur_o, o=bsr[0:16, 32:32 + NB_],
              s=cblk[0:16, 0:T].rearrange("p (n j) -> p n j", j=128)[:, :, 127:128]:
              e.tensor_sub(o, bsr[0:16, a:a + NB_],
                           s.rearrange("p n j -> p (n j)")),
              reads=[bsrb, cblkb], writes=[bsrb])
        p.bop("dve", lambda e,
              o=cbl2[0:16, 0:T].rearrange("p (n j) -> p n j", j=128),
              s=cblk[0:16, 0:T].rearrange("p (n j) -> p n j", j=128),
              of=bsr[0:16, 32:32 + NB_].rearrange(
                  "p (n j) -> p n j", j=1).broadcast_to((16, NB_, 128)):
              e.tensor_tensor(o, s, of, OP.add),
              reads=[bsrb, cblkb], writes=[cbl2b])
        af_ = p.bop("act", lambda e: e.activation(
            afull[0:16, 0:T], cbl2[0:16, 0:T], AF.Exp),
            reads=[cbl2b], writes=[afullb])
        tg = p.dma("sync", afd[0:16, :], afull[0:16, 0:T], "afd",
                   deps=[af_] + afdb.need_for_write())
        afullb.read(tg)
        afdb.wrote(tg)
        for nb in range(NB_):
            xbk = fb[5]
            p.bop("pe", lambda e, nb=nb, o=xbk.ap: e.transpose(
                o[:, 0:16], cblk[0:16, nb * 128:(nb + 1) * 128],
                csb["c_idf"][0:16, 0:16]), reads=[cblkb], writes=[xbk])
            p.bop("dve", lambda e, nb=nb, i=xbk.ap: e.tensor_scalar_mul(
                ctn[:, nb * 16:(nb + 1) * 16], i[:, 0:16], -1.0),
                reads=[xbk], writes=[ctnb])
        for hg in range(NH // 2):
            heads = [hg * 2, hg * 2 + 1]
            wt8 = {}
            for hi, h in enumerate(heads):
                for wi2, (nm, wsrc) in enumerate(
                        [("k", WB["wkb"]), ("v", WB["wvb"]),
                         ("ig", WB["wig"]), ("q", WB["wqb"])]):
                    slot = hi * 4 + wi2
                    tg = p.dma("pool", wsl[slot][:, :], wsrc[h], f"wsl{slot}",
                               deps=wslb[slot].need_for_write())
                    wslb[slot].wrote(tg)
                    wt8[(h, nm)] = tg
            for tc in range(TC):
                deps = list(hbdb.need_for_read())
                for k in range(KT):
                    deps += hbtb[k].need_for_write()
                for k in range(KT):
                    hl = p.dma("sync", hbt[k][:, :],
                               hbd[k, :, 3 + tc * 512:3 + (tc + 1) * 512],
                               "hbt", deps=deps if k == 0 else ())
                for k in range(KT):
                    hbtb[k].wrote(hl)
                hbdb.read(hl)
                deps = list(xcdb.need_for_read())
                for k in range(KT):
                    deps += xctb[k].need_for_write()
                for k in range(KT):
                    xl = p.dma("sync", xct[k][:, :],
                               xcd[k, :, tc * 512:(tc + 1) * 512],
                               "xct", deps=deps if k == 0 else ())
                for k in range(KT):
                    xctb[k].wrote(xl)
                xcdb.read(xl)
                for hi, h in enumerate(heads):
                    bk_k, bk_v, bk_ig, bk_q = fb[0], fb[1], fb[2], fb[3]
                    for (bk, nm, rhs_t, rhs_b) in (
                            (bk_k, "k", hbt, hbtb), (bk_v, "v", hbt, hbtb),
                            (bk_ig, "ig", xct, xctb), (bk_q, "q", hbt, hbtb)):
                        slot = hi * 4 + {"k": 0, "v": 1, "ig": 2, "q": 3}[nm]
                        for k in range(KT):
                            p.bop("pe", lambda e, k=k, w=wsl[slot], o=bk.ap,
                                  r_=rhs_t[k]: e.matmul(
                                      o[:, 0:512], w[:, k * 128:(k + 1) * 128],
                                      r_[:, :], start=(k == 0),
                                      stop=(k == KT - 1)),
                                  reads=[wslb[slot], rhs_b[k]], writes=[bk])
                    b_l2(bk_k, kbt, kbtb, True)
                    b_vln(bk_v, vft, vftb)
                    p.bop("act", lambda e, d=igt, i=bk_ig.ap, h=h:
                          e.activation(d[:, :], i[:, 0:512], AF.Sigmoid,
                                       bias=csb["c_bigb"][:, h:h + 1]),
                          reads=[bk_ig], writes=[igtb])
                    b_l2(bk_q, stbf[3], stbfb[3], True)
                    qt_ = stbfb[3].writer
                    tg = p.dma("sync", qbd[h, :, tc * 512:(tc + 1) * 512],
                               stbf[3][:, :], "qbst",
                               deps=[qt_] + qbdb.need_for_write())
                    stbfb[3].read(tg)
                    qbdb.wrote(tg)
                    p.bop("dve", lambda e, d=stg[2], a=igt, b_=kbt:
                          e.tensor_tensor(d[:, :], a[:, :], b_[:, :], OP.mult),
                          reads=[igtb, kbtb], writes=[stgb[2]])
                    ut_ = p.bop("dve", lambda e, d=stbf[2], a=stg[2], b_=vft:
                                e.tensor_tensor(d[:, :], a[:, :], b_[:, :],
                                                OP.mult),
                                reads=[stgb[2], vftb], writes=[stbfb[2]])
                    tg = p.dma("sync", umd[h, :, tc * 512:(tc + 1) * 512],
                               stbf[2][:, :], "umst",
                               deps=[ut_] + umdb.need_for_write())
                    stbfb[2].read(tg)
                    umdb.wrote(tg)
        p.bop("dve", lambda e: e.memset(carry[:, :], 0.0), writes=[carryb])
        it_list = [(nb, h) for nb in range(NB_) for h in range(NH)]
        for ii, (nb, h) in enumerate(it_list):
            par = ii % 2
            ufm, ufmb = ufm2[ii % 4], ufm2b[ii % 4]
            uts, utsb = uts2[par], uts2b[par]
            wt_, wtb = wt2[par], wt2b[par]
            mst, mstb = mst2[par], mst2b[par]
            gcolt, gcoltb = gcol2[par], gcol2b[par]
            if ii == 0:
                for jj in range(3):
                    nb2, h2 = it_list[jj]
                    ul = p.dma("sync", ufm2[jj][:, :],
                               umd[h2, :, nb2 * 128:(nb2 + 1) * 128],
                               f"ufm{jj}", deps=ufm2b[jj].need_for_write() +
                               umdb.need_for_read())
                    ufm2b[jj].wrote(ul)
                    umdb.read(ul)
            if ii + 3 < len(it_list):
                jj = (ii + 3) % 4
                nb2, h2 = it_list[ii + 3]
                ul = p.dma("sync", ufm2[jj][:, :],
                           umd[h2, :, nb2 * 128:(nb2 + 1) * 128],
                           f"ufm{jj}", deps=ufm2b[jj].need_for_write() +
                           umdb.need_for_read())
                ufm2b[jj].wrote(ul)
                umdb.read(ul)
            bkg = fb[2 + par]
            p.bop("pe", lambda e, nb=nb, h=h, o=bkg.ap: e.matmul(
                o[:, 0:1], csb["c_onesf"][0:1, :],
                ctn[0:1, nb * 16 + h:nb * 16 + h + 1],
                start=True, stop=True), reads=[ctnb], writes=[bkg])
            p.bop("act", lambda e, i=bkg.ap, g=gcolt: e.activation(
                g[:, 0:1], i[:, 0:1], AF.Exp, scale=-1.0),
                reads=[bkg], writes=[gcoltb])
            p.bop("dve", lambda e, h=h, u=ufm, g=gcolt: e.scalar_tensor_tensor(
                u[:, 0:1], carry[:, h:h + 1], g[:, 0:1],
                u[:, 0:1], OP.mult, OP.add),
                reads=[carryb, gcoltb, ufmb], writes=[ufmb])
            xbk = xb[par]
            p.bop("pe", lambda e, o=xbk.ap, u=ufm: e.transpose(
                o[:, 0:128], u[:, :], c_id[:, :]),
                reads=[ufmb], writes=[xbk])
            p.bop("dve", lambda e, i=xbk.ap, u=uts: e.tensor_copy(
                u[:, :], i[:, 0:128]), reads=[xbk], writes=[utsb])
            bkc = fb[4 + par]
            p.bop("pe", lambda e, nb=nb, h=h, o=bkc.ap: e.matmul(
                o[:, 0:128], csb["c_ohf"][0:16, h * 128:(h + 1) * 128],
                cblk[0:16, nb * 128:(nb + 1) * 128],
                start=True, stop=True), reads=[cblkb], writes=[bkc])
            zt_, ztb_ = stg[par], stgb[par]
            p.bop("dve", lambda e, i=bkc.ap, d=zt_: e.tensor_tensor(
                d[:, 0:128], i[:, 0:128], csb["c_trin"][:, :], OP.add),
                reads=[bkc], writes=[ztb_])
            p.bop("act", lambda e, nb=nb, h=h, s=zt_, w=wt_: e.activation(
                w[:, :], s[:, 0:128], AF.Exp,
                bias=ctn[:, nb * 16 + h:nb * 16 + h + 1]),
                reads=[ztb_, ctnb], writes=[wtb])
            bkm = fb[par]
            p.bop("pe", lambda e, o=bkm.ap, u=uts, w=wt_: e.matmul(
                o[:, 0:128], u[:, :], w[:, :], start=True, stop=True),
                reads=[utsb, wtb], writes=[bkm])
            mt_ = p.bop("dve", lambda e, i=bkm.ap, m_=mst: e.tensor_copy(
                m_[:, :], i[:, 0:128]), reads=[bkm], writes=[mstb])
            p.bop("dve", lambda e, h=h, m_=mst: e.tensor_copy(
                carry[:, h:h + 1], m_[:, 127:128]),
                reads=[mstb], writes=[carryb])
            tg = p.dma("sync", memd[h, :, nb * 128:(nb + 1) * 128],
                       mst[:, :], f"memst{par}",
                       deps=[mt_] + memdb.need_for_write())
            mstb.read(tg)
            memdb.wrote(tg)
        tg = p.dma("sync", cc2i[:, :], carry[:, :], "cc2s",
                   deps=carryb.need_for_read() + cc2ib.need_for_write())
        carryb.read(tg)
        cc2ib.wrote(tg)
    p.barrier()

    cct2 = collective(cc2i, cc2o)
    p.barrier()

    with ExitStack() as b2_es:
        sb2 = lambda n, s, d=F32: b2_es.enter_context(nc.sbuf_tensor(n, s, d))
        alloc_generic(b2_es, nws=2, n_xt=0, with_hbt=False,
                      rows_len=64)
        statei = sb2("statei", [128, 16])
        stateib = Buf(statei)
        afl2 = sb2("afl2", [16, T])
        afl2b = Buf(afl2)
        ogs = [sb2(f"ogs{k}", [128, 512], BF16) for k in range(KT)]
        ogsb = [Buf(t) for t in ogs]
        xct2 = [sb2(f"xct2{k}", [128, 512], BF16) for k in range(KT)]
        xct2b = [Buf(t) for t in xct2]
        mg = sb2("mg", [128, 512])
        mgb = Buf(mg)
        m2 = sb2("m2", [128, 512])
        m2b = Buf(m2)
        qbt2 = sb2("qbt2", [128, 512], BF16)
        qbt2b = Buf(qbt2)
        tg = p.dma("sync", statei[:, :], cc2o[0:128, :], "cc2l", deps=[cct2])
        stateib.wrote(tg)
        p.bop("dve", lambda e: e.tensor_tensor(
            statei[:, :], statei[:, :], csb["c_mask"][:, 0:16], OP.mult),
            reads=[stateib], writes=[stateib])
        tg = p.dma("sync", afl2[0:16, :], afd[0:16, :], "afl2",
                   deps=afdb.need_for_read())
        afl2b.wrote(tg)
        afdb.read(tg)

        def dd_norm(src_t, src_b, wcol_ap, bcol_ap, out_t, out_b, out_bf16):
            sq_, sqb_ = stbf[0], stbfb[0]
            p.bop("act", lambda e, d=sq_, s=src_t: e.activation(
                d[:, :], s[:, :], AF.Square), reads=[src_b], writes=[sqb_])
            bs = fb[4]
            p.bop("pe", lambda e, s=src_t, o=bs.ap: e.matmul(
                o[0:1, 0:512], csb["c_onesc"][:, 0:1], s[:, :],
                start=True, stop=True), reads=[src_b], writes=[bs])
            p.bop("dve", lambda e, o=rw(0), i=bs.ap: e.tensor_copy(
                o, i[0:1, 0:512]), reads=[bs], writes=[rwsb])
            p.bop("pe", lambda e, d=sq_, o=bs.ap: e.matmul(
                o[0:1, 0:512], c_ones[:, 0:1], d[:, :], start=True, stop=True),
                reads=[sqb_], writes=[bs])
            p.bop("dve", lambda e, o=rw(1), i=bs.ap: e.tensor_copy(
                o, i[0:1, 0:512]), reads=[bs], writes=[rwsb])
            p.bop("dve", lambda e, o=rw(2), i=rw(0): e.tensor_scalar_mul(
                o, i, 1.0 / 128), reads=[rwsb], writes=[rwsb])
            p.bop("dve", lambda e, o=rw(3), i=rw(2): e.tensor_mul(o, i, i),
                  reads=[rwsb], writes=[rwsb])
            p.bop("dve", lambda e, o=rw(4), i=rw(1): e.tensor_scalar(
                o, i, 1.0 / 128, None, OP.mult), reads=[rwsb], writes=[rwsb])
            p.bop("dve", lambda e, o=rw(4), i=rw(3): e.tensor_sub(o, o, i),
                  reads=[rwsb], writes=[rwsb])
            p.bop("act", lambda e, o=rw(4): e.activation(
                o, o, AF.Ln, bias=csb["c_cvb"][0:1, 0:1]),
                reads=[rwsb], writes=[rwsb])
            p.bop("act", lambda e, o=rw(4): e.activation(
                o, o, AF.Exp, scale=-0.5), reads=[rwsb], writes=[rwsb])
            p.bop("dve", lambda e, o=rw(5), a=rw(2), b_=rw(4): e.tensor_mul(
                o, a, b_), reads=[rwsb], writes=[rwsb])
            p.bop("dve", lambda e, o=rw(5), i=rw(5): e.tensor_scalar_mul(
                o, i, -1.0), reads=[rwsb], writes=[rwsb])
            b1_ = fb[2]
            b2_ = fb[3]
            p.bop("pe", lambda e, o=b1_.ap, i=rw(4): e.matmul(
                o[:, 0:512], csb["c_onesf"][0:1, :], i, start=True, stop=True),
                reads=[rwsb], writes=[b1_])
            p.bop("pe", lambda e, o=b2_.ap, i=rw(5): e.matmul(
                o[:, 0:512], csb["c_onesf"][0:1, :], i, start=True, stop=True),
                reads=[rwsb], writes=[b2_])
            s1_, s1b_ = stg[2], stgb[2]
            p.bop("dve", lambda e, d=s1_, i=b1_.ap: e.tensor_copy(
                d[:, :], i[:, 0:512]), reads=[b1_], writes=[s1b_])
            s3_, s3b_ = stg[3], stgb[3]
            p.bop("dve", lambda e, d=s3_, s=src_t, r_=s1_: e.tensor_tensor(
                d[:, :], s[:, :], r_[:, :], OP.mult),
                reads=[src_b, s1b_], writes=[s3b_])
            p.bop("dve", lambda e, d=s3_, i=b2_.ap: e.tensor_tensor(
                d[:, :], d[:, :], i[:, 0:512], OP.add),
                reads=[b2_, s3b_], writes=[s3b_])
            p.bop("dve", lambda e, d=out_t, s=s3_, w=wcol_ap, b_=bcol_ap:
                  e.tensor_scalar(d[:, :], s[:, :], w, b_, OP.mult, OP.add),
                  reads=[s3b_], writes=[out_b])

        for tc in range(TC):
            deps = list(xcdb.need_for_read())
            for k in range(KT):
                deps += xct2b[k].need_for_write()
            for k in range(KT):
                xl = p.dma("sync", xct2[k][:, :],
                           xcd[k, :, tc * 512:(tc + 1) * 512], "xct2",
                           deps=deps if k == 0 else ())
            for k in range(KT):
                xct2b[k].wrote(xl)
            xcdb.read(xl)
            for h in range(NH):
                ml = p.dma("sync", mg[:, :], memd[h, :, tc * 512:(tc + 1) * 512],
                           "mgl", deps=mgb.need_for_write() +
                           memdb.need_for_read())
                mgb.wrote(ml)
                memdb.read(ml)
                ba = fb[5]
                p.bop("pe", lambda e, h=h, tc=tc, o=ba.ap: e.matmul(
                    o[:, 0:512], csb["c_ohf"][0:16, h * 128:(h + 1) * 128],
                    afl2[0:16, tc * 512:(tc + 1) * 512],
                    start=True, stop=True), reads=[afl2b], writes=[ba])
                p.bop("dve", lambda e, h=h, i=ba.ap: e.scalar_tensor_tensor(
                    mg[:, :], i[:, 0:512], statei[:, h:h + 1], mg[:, :],
                    OP.mult, OP.add), reads=[ba, stateib, mgb], writes=[mgb])
                dd_norm(mg, mgb, csb["c_mnw"][:, 0:1], csb["c_mnb"][:, 0:1],
                        m2, m2b, False)
                ql = p.dma("sync", qbt2[:, :], qbd[h, :, tc * 512:(tc + 1) * 512],
                           "qbl", deps=qbt2b.need_for_write() +
                           qbdb.need_for_read())
                qbt2b.wrote(ql)
                qbdb.read(ql)
                p.bop("dve", lambda e: e.tensor_tensor(
                    m2[:, :], m2[:, :], qbt2[:, :], OP.mult),
                    reads=[qbt2b, m2b], writes=[m2b])
                dd_norm(m2, m2b, csb["c_gnw"][:, h:h + 1],
                        csb["c_gnb"][:, h:h + 1], ogs[h], ogsb[h], True)
            for m_ in range(KT):
                slot = m_ % 2
                wt = p.dma("pool", wsl[slot][:, :], WB["wog"][m_], f"wsl{slot}",
                           deps=wslb[slot].need_for_write())
                wslb[slot].wrote(wt)
                bk = fb[m_ % 2]
                for k in range(KT):
                    p.bop("pe", lambda e, k=k, w=wsl[slot], o=bk.ap,
                          r_=xct2[k]: e.matmul(
                              o[:, 0:512], w[:, k * 128:(k + 1) * 128],
                              r_[:, :], start=(k == 0), stop=(k == KT - 1)),
                          reads=[wslb[slot], xct2b[k]], writes=[bk])
                g_, gb_ = stg[0], stgb[0]
                p.bop("act", lambda e, d=g_, i=bk.ap, m_=m_: e.activation(
                    d[:, :], i[:, 0:512], AF.Sigmoid,
                    bias=csb["c_bogb"][:, m_:m_ + 1]),
                    reads=[bk], writes=[gb_])
                p.bop("dve", lambda e, d=ogs[m_], g=g_: e.tensor_tensor(
                    d[:, :], d[:, :], g[:, :], OP.mult),
                    reads=[gb_, ogsb[m_]], writes=[ogsb[m_]])
            for mo in range(KT):
                slot = mo % 2
                wt = p.dma("pool", wsl[slot][:, :], WB["wob"][mo], f"wsl{slot}",
                           deps=wslb[slot].need_for_write())
                wslb[slot].wrote(wt)
                bk = fb[2 + mo % 2]
                for k in range(KT):
                    p.bop("pe", lambda e, k=k, w=wsl[slot], o=bk.ap,
                          r_=ogs[k]: e.matmul(
                              o[:, 0:512], w[:, k * 128:(k + 1) * 128],
                              r_[:, :], start=(k == 0), stop=(k == KT - 1)),
                          reads=[wslb[slot], ogsb[k]], writes=[bk])
                s1_, s1b_ = stg[1], stgb[1]
                lt = p.dma("sync", s1_[:, :], xbd[mo, :, tc * 512:(tc + 1) * 512],
                           "xbl", deps=s1b_.need_for_write() +
                           xbdb.need_for_read())
                s1b_.wrote(lt)
                xbdb.read(lt)
                s2_, s2b_ = stg[2], stgb[2]
                et = p.bop("dve", lambda e, d=s2_, a=s1_, i=bk.ap:
                           e.tensor_tensor(d[:, :], a[:, :], i[:, 0:512],
                                           OP.add), reads=[bk, s1b_],
                           writes=[s2b_])
                tg = p.dma("sync", hmid[mo, :, tc * 512:(tc + 1) * 512],
                           s2_[:, :], "hmst", deps=[et] +
                           hmidb.need_for_write())
                s2b_.read(tg)
                hmidb.wrote(tg)
    p.barrier()

    ffn_phase("b4", hmid, r3, r3b, FFM, wgate, wval, wout, True, False,
              0, 0, [r2], wcol=csb["c_l2w"], bcolbf=c_l2bbf, wcolbf=c_l2wbf)
    p.barrier()
    yb_ = Buf(y)
    ffn_phase("f2", r3, y, yb_, FFN, wq1d, wq2d, wq3d, False, True,
              0, FFN // 128, [])
    p.barrier()

    for s in list(p.cnt):
        if s in ("PE", "DV", "AC", "CC"):
            continue
        p.wait("sync", (s, p.cnt[s]))

    p.finish()
    p.es.close()
    return nc, dict(T=T, TC=TC, QC=QC, SK=SK, PK=PK, NB=NB, ST=ST)


def _tile_w(w):
    din, dout = w.shape
    kk, mm = din // 128, dout // 128
    return np.ascontiguousarray(
        w.reshape(kk, 128, mm, 128).transpose(2, 1, 0, 3).reshape(mm, 128, din))


def _col16(v):
    return np.ascontiguousarray(np.asarray(v).reshape(-1, 128).T)


def make_core_inputs(inputs, core, T=2048):
    b, half = core // 2, core % 2
    QC = T // 128
    SK = 2 * T // 512
    PK = T // 512
    x = np.asarray(inputs["L0_attn_a_x"])
    xo = np.ascontiguousarray(x[b, half * T:(half + 1) * T].T)
    xp = np.ascontiguousarray(x[b, (half - 1) * T:half * T].T) if half == 1 else xo
    m = {
        "x_own": xo.reshape(KT, 128, T),
        "x_prev": xp.reshape(KT, 128, T),
        "wqa": _tile_w(inputs["L0_attn_a_Wq"]),
        "wka": _tile_w(inputs["L0_attn_a_Wk"]),
        "wva": _tile_w(inputs["L0_attn_a_Wv"]),
        "wpa": _tile_w(inputs["L0_attn_a_Wproj"]),
        "wqb": _tile_w(inputs["L0_attn_b_Wq"]),
        "wkb": _tile_w(inputs["L0_attn_b_Wk"]),
        "wvb": _tile_w(inputs["L0_attn_b_Wv"]),
        "wob": _tile_w(inputs["L0_attn_b_Wo"]),
        "wig": _tile_w(inputs["L0_attn_b_Wig"]),
        "wog": _tile_w(inputs["L0_attn_b_Wog"]),
        "wgate": _tile_w(inputs["L0_attn_b_Wgate"]),
        "wval": _tile_w(inputs["L0_attn_b_Wval"]),
        "wout": _tile_w(inputs["L0_attn_b_Wout"]),
        "fw1": _tile_w(np.ascontiguousarray(inputs["L0_ffn_w1"].T)),
        "fw2": _tile_w(np.ascontiguousarray(inputs["L0_ffn_w2"].T)),
        "fw3": _tile_w(np.ascontiguousarray(inputs["L0_ffn_w3"].T)),
        "wgg": np.ascontiguousarray(
            inputs["L0_attn_b_Wg"].reshape(KT, 128, NH)),
        "qrw": inputs["L0_attn_a_q_rms_w"].reshape(128, 1),
        "krw": inputs["L0_attn_a_k_rms_w"].reshape(128, 1),
        "slw": inputs["L0_attn_a_subln_w"].reshape(128, 1),
        "vnw": inputs["L0_attn_b_vnw"].reshape(128, 1),
        "vnb": inputs["L0_attn_b_vnb"].reshape(128, 1),
        "mnw": inputs["L0_attn_b_mnw"].reshape(128, 1),
        "mnb": inputs["L0_attn_b_mnb"].reshape(128, 1),
        "cw": np.ascontiguousarray(
            inputs["L0_attn_b_conv_w"][:, 0, :].reshape(KT, 128, 4)
            .transpose(1, 0, 2).reshape(128, KT * 4)),
        "cb": _col16(inputs["L0_attn_b_conv_b"]),
        "bigb": _col16(inputs["L0_attn_b_big"]),
        "bogb": _col16(inputs["L0_attn_b_bog"]),
        "bgb": np.ascontiguousarray(
            np.broadcast_to(inputs["L0_attn_b_bg"][None, :], (128, NH))),
        "gnw": _col16(inputs["L0_attn_b_gnw"]),
        "gnb": _col16(inputs["L0_attn_b_gnb"]),
        "l1w": _col16(inputs["L0_attn_b_ln1w"]),
        "l1b": _col16(inputs["L0_attn_b_ln1b"]),
        "l2w": _col16(inputs["L0_attn_b_ln2w"]),
        "l2b": _col16(inputs["L0_attn_b_ln2b"]),
    }
    m = {k: np.ascontiguousarray(v, dtype=np.float32) for k, v in m.items()}

    sl = alibi_slopes(NH)
    PK_ = T // 512
    QG = (T // 128) // 4
    CL = [4 * (PK_ + g + 1) * 512 for g in range(QG)]
    s_glob = np.concatenate([np.arange(T) + (half - 1) * T,
                             np.arange(T) + half * T]).astype(np.float64)
    if half == 0:
        s_glob[:T] = -1.0e12
    cstr = np.zeros((128, sum(CL)), np.float32)
    pp = np.arange(128)
    off = 0
    for g in range(QG):
        nk = PK_ + g + 1
        for qi in range(4):
            qc = g * 4 + qi
            tg_ = half * T + qc * 128 + pp
            for kc in range(nk):
                sg_ = s_glob[kc * 512:(kc + 1) * 512]
                dlt = tg_[:, None] - sg_[None, :]
                cc = np.where(dlt >= 0, dlt, 1.0e9)
                cstr[:, off + (qi * nk + kc) * 512:
                     off + (qi * nk + kc + 1) * 512] = cc
        off += CL[g]
    m["cstrd"] = cstr
    oh = np.zeros((16, NH * 128), np.float32)
    for h in range(NH):
        oh[h, h * 128:(h + 1) * 128] = 1.0
    m["onesbf"] = np.ones((128, 128), BF)
    m["onesf"] = np.ones((1, 128), np.float32)
    m["onesc"] = np.ones((128, 1), np.float32)
    m["onehotf"] = oh
    m["identf"] = np.eye(128, dtype=np.float32)
    s_ = np.arange(128)
    m["trineg"] = np.where(s_[:, None] <= s_[None, :], 0.0, NEG).astype(np.float32)
    cv = np.zeros(8, np.float32)
    cv[0], cv[1], cv[2] = 1e-5, 1e-12, 1e-8
    m["cvb"] = np.ascontiguousarray(np.broadcast_to(cv[None, :], (128, 8)))
    m["ident"] = np.eye(128).astype(BF)
    m["maskprev"] = np.full((128, 64), float(half), np.float32)
    return m


_CACHE = {}


def kernel(**inputs):
    T = 2048
    n_cores = 8
    if "nc" not in _CACHE:
        _CACHE["nc"], _CACHE["meta"] = build(T, n_cores)
    nc = _CACHE["nc"]
    in_maps = [make_core_inputs(inputs, c, T) for c in range(n_cores)]
    res = run_bass_kernel_spmd(nc, in_maps, core_ids=list(range(n_cores)))
    out = np.empty((4, 4096, 2048), np.float32)
    for c in range(n_cores):
        b, half = c // 2, c % 2
        yc = res.results[c]["y"].reshape(D, T)
        out[b, half * T:(half + 1) * T] = yc.T
    return out
